# version 11
# baseline (speedup 1.0000x reference)
import numpy as np
import ml_dtypes

import concourse.bass as bass
import concourse.mybir as mybir
from concourse.bass_utils import run_bass_kernel_spmd

F32 = mybir.dt.float32
BF16 = mybir.dt.bfloat16
I32 = mybir.dt.int32
AF = mybir.ActivationFunctionType
ALU = mybir.AluOpType

D = 2048
S = 4096
TOWN = 2048
NH = 16
FI = 8192
EXP_SCALE = 1.0 / float(np.sqrt(128.0))
MASK = -30000.0
EPS = 1e-5
NPAN1 = 16
PW1 = 256

ENGS = ("sync", "gpsimd", "tensor", "vector", "scalar")
SEM_OF = {"sync": "sy", "gpsimd": "sgb", "tensor": "st", "vector": "sv",
          "scalar": "sc"}
SYNC_SEMS = ["scst", "ssp", "svs0", "svs1", "shl0", "shl1", "sxo", "sx2w",
             "sl3w", "sl3l", "sx2l", "sod0", "sod1"]
GP_SEMS = ["gx1a", "gx1b", "gw0", "gw1", "gwv", "gwo0", "gwo1", "ggu0",
           "ggu1", "gwd0", "gwd1"]


class Prog:
    def __init__(self):
        self.ms = {}
        self.sems = {}
        self.emit = False
        self.cnt = {}
        self.mm_mode = None

    def start(self, emit):
        import collections
        self.emit = emit
        self.cnt = collections.defaultdict(int)
        self.mm_mode = None

    def op(self, E, name, *args, inc=None, key=None, **kw):
        inst = getattr(E, name)(*args, **kw) if self.emit else None
        if inc is not None:
            amt = 1 if inc in ("st", "sv", "sc", "sy", "sgb") else 16
            self.cnt[inc] += amt
            if key is not None and not self.emit:
                assert key not in self.ms, f"dup {key}"
                self.ms[key] = self.cnt[inc]
            if self.emit and inst is not None:
                inst.then_inc(self.sems[inc], amt)
        return inst

    def wait(self, E, sem, key):
        if key is None:
            return
        if self.emit:
            assert key in self.ms, f"missing milestone {key}"
            E.wait_ge(self.sems[sem], self.ms[key])


def build_nc():
    import os as _os
    K1V = _os.environ.get("K1V", "1") == "1"
    K1SP = _os.environ.get("K1SP", "1") == "1"
    K1FC = int(_os.environ.get("K1FC", "32"))
    global NPAN1
    NPAN1 = int(_os.environ.get("KNP", "16"))
    nc = bass.Bass("TRN2", target_bir_lowering=False,
                   detect_race_conditions=False)

    xt = nc.dram_tensor("xt", [D, S], F32, kind="ExternalInput").ap()
    wqkv = nc.dram_tensor("wqkv", [D, 3 * D], F32, kind="ExternalInput").ap()
    bqk = nc.dram_tensor("bqk", [128, 32], F32, kind="ExternalInput").ap()
    bvd = nc.dram_tensor("bv", [128, 16], F32, kind="ExternalInput").ap()
    wod = nc.dram_tensor("wo", [16, 128, 16, 128], F32,
                         kind="ExternalInput").ap()
    bod = nc.dram_tensor("bo", [128, 16], F32, kind="ExternalInput").ap()
    wgt = nc.dram_tensor("wgt", [D, FI], I32, kind="ExternalInput").ap()
    wut = nc.dram_tensor("wut", [D, FI], I32, kind="ExternalInput").ap()
    wdt = nc.dram_tensor("wdt", [FI, D], I32, kind="ExternalInput").ap()
    gsd = nc.dram_tensor("gs", [128, 64], F32, kind="ExternalInput").ap()
    usd = nc.dram_tensor("us", [128, 64], F32, kind="ExternalInput").ap()
    dsd = nc.dram_tensor("ds", [128, 16], F32, kind="ExternalInput").ap()
    mtri = nc.dram_tensor("mtri", [128, 128], F32, kind="ExternalInput").ap()
    mrow = nc.dram_tensor("mrow", [1, 4 * 512], BF16,
                          kind="ExternalInput").ap()
    out = nc.dram_tensor("o", [D, TOWN], F32, kind="ExternalOutput").ap()
    qs = nc.dram_tensor("qs", [D, TOWN], BF16, kind="Internal").ap()
    ks = nc.dram_tensor("ks", [D, S], BF16, kind="Internal").ap()
    vs = nc.dram_tensor("vs", [S, D], BF16, kind="Internal").ap()
    x2s = nc.dram_tensor("x2s", [D, TOWN], BF16, kind="Internal").ap()
    KDBG = _os.environ.get("KDBG", "0") == "1"
    dbg = (nc.dram_tensor("dbg", [D, TOWN], BF16, kind="ExternalOutput").ap()
           if KDBG else None)
    dbga = (nc.dram_tensor("dbga", [128, NH * TOWN], BF16,
                           kind="ExternalOutput").ap() if KDBG else None)
    l3s = nc.dram_tensor("l3s", [D, TOWN], BF16, kind="Internal").ap()

    P = Prog()
    P.sems = {}
    guards = []

    def mk(name, shape, dt, kind="sbuf"):
        g = (nc.sbuf_tensor if kind == "sbuf" else nc.psum_tensor)(
            name, shape, dt)
        h = g.__enter__()
        guards.append(g)
        return h

    scoped = {}

    def salloc(name, shape, dt):
        g = nc.sbuf_tensor(name, shape, dt)
        h = g.__enter__()
        scoped[name] = g
        return h

    def sfree(*names):
        for n in names:
            scoped.pop(n).__exit__(None, None, None)

    c_bqk = mk("c_bqk", [128, 32], F32)
    c_bv = mk("c_bv", [128, 16], F32)
    c_bo = mk("c_bo", [128, 16], F32)
    c_gs = mk("c_gs", [128, 64], F32)
    c_us = mk("c_us", [128, 64], F32)
    c_ds = mk("c_ds", [128, 16], F32)
    ones_k = mk("ones_k", [128, 1], BF16)
    ones_r1b = mk("ones_r1b", [1, 128], BF16)
    ones_r1f = mk("ones_r1f", [1, 128], F32)
    neg128_r1f = mk("neg128_r1f", [1, 128], F32)
    eps_t = mk("eps_t", [128, 1], F32)
    ones128 = mk("ones128", [128, 128], BF16)
    sm_bc = mk("sm_bc", [128, 512], F32)
    sq_bc = mk("sq_bc", [128, 512], F32)
    st_bc = mk("st_bc", [128, 512], F32)
    sv_bc = mk("sv_bc", [128, 512], F32)
    rb_bf = mk("rb_bf", [128, 512], BF16)
    mb_bf = mk("mb_bf", [128, 512], BF16)
    t_ao = mk("t_ao", [128, 512], F32)
    tnrm = mk("tnrm", [128, 512], F32)
    sqtmp = [mk(f"sqtmp{i}", [128, 512], BF16) for i in range(2)]

    pA = mk("pA", [128, 512], F32, "psum")
    pB = mk("pB", [128, 512], F32, "psum")
    pav = mk("pav", [128, 512], F32, "psum")
    pbc1 = mk("pbc1", [128, 512], F32, "psum")
    pbc2 = mk("pbc2", [128, 512], F32, "psum")
    pz = mk("pz", [128, 512], F32, "psum")
    ps1 = mk("ps1", [128, 512], F32, "psum")
    ps2 = mk("ps2", [128, 512], F32, "psum")
    pQ = [pA, pB]
    pavq = [pav, pbc2]
    pzq = [pz, ps1]
    pS3 = [pA, pB, pbc1]

    lnx = salloc("lnx", [128, NH, S], BF16)
    xtf = [salloc(f"xtf{i}", [128, 16, PW1], F32) for i in range(2)]
    xtb = salloc("xtb", [128, 16, PW1], BF16)
    xsq = salloc("xsq", [128, 16, PW1], BF16)
    sfree("xsq", "xtb", "xtf1", "xtf0")
    wqk_sb = [salloc(f"wqk{i}", [128, 16, 128], BF16) for i in range(2)]
    wv_sb = salloc("wv_sb", [128, 16, 512], BF16)
    qstage = salloc("qstage", [128, 4, 512], BF16)
    kstage = salloc("kstage", [128, 4, 512], BF16)
    vstage = [salloc(f"vstage{i}", [128, 512], BF16) for i in range(2)]
    sfree("vstage1", "vstage0", "kstage", "qstage", "wv_sb",
          "wqk1", "wqk0", "lnx")
    ao = salloc("ao", [128, NH, TOWN], BF16)
    kh = [salloc(f"kh{i}", [128, S], BF16) for i in range(2)]
    zl128 = salloc("zl128", [128, 512], F32)
    mask_bc = salloc("mask_bc", [128, 4, 512], BF16)
    qh = [salloc(f"qh{i}", [128, TOWN], BF16) for i in range(2)]
    vh = [salloc(f"vh{i}", [128, 32, 128], BF16) for i in range(2)]
    p_sb = [salloc(f"p_sb{i}", [128, 512], BF16) for i in range(3)]
    rzb_sb = [salloc(f"rzb_sb{i}", [128, 512], BF16) for i in range(2)]
    c_tri = salloc("c_tri", [128, 128], F32)
    c_mrow = salloc("c_mrow", [1, 4 * 512], BF16)
    sfree("c_mrow", "c_tri", "rzb_sb1", "rzb_sb0",
          "p_sb2", "p_sb1", "p_sb0", "vh1", "vh0", "qh1", "qh0",
          "mask_bc", "zl128", "kh1", "kh0")
    yb = salloc("yb", [128, NH, TOWN], BF16)
    wo_sb = [salloc(f"wo{i}", [128, 16, 128], BF16) for i in range(2)]
    xo_h = salloc("xo_h", [128, 4, 512], F32)
    x2b = salloc("x2b", [128, 16, 512], BF16)
    l3t = salloc("l3t", [128, 16, 512], BF16)
    sfree("l3t", "x2b", "xo_h", "wo1", "wo0", "yb", "ao")
    hact = salloc("hact", [128, 64, 1024], BF16)
    l3_sb = salloc("l3_sb", [128, 16, 1024], BF16)
    wg_sb = [salloc(f"wg{i}", [128, 16, 128], BF16) for i in range(2)]
    wu_sb = [salloc(f"wu{i}", [128, 16, 128], BF16) for i in range(2)]
    silu_sb = salloc("silu_sb", [128, 512], F32)
    sfree("silu_sb", "wu1", "wu0", "wg1", "wg0", "l3_sb")
    wd_sb = [salloc(f"wd{i}", [128, 32, 128], BF16) for i in range(2)]
    ffnb = salloc("ffnb", [128, 16, 512], BF16)
    x2h = salloc("x2h", [128, 8, 512], BF16)
    x3b = [salloc(f"x3b{i}", [128, 512], F32) for i in range(2)]
    rsb_sb = [salloc(f"rsb{i}", [128, 512], F32) for i in range(2)]

    BSEM_OF = {"sync": "sy", "gpsimd": "sgb", "tensor": "st",
               "vector": "sv", "scalar": "sc"}

    _BE = set(_os.environ.get("KBENGS", "sync,gpsimd,tensor,vector,scalar").split(","))

    def barrier(tag, eng, E):
        if eng not in _BE:
            return
        if eng == "gpsimd":
            if P.emit:
                for s in GP_SEMS:
                    if P.cnt[s]:
                        E.wait_ge(P.sems[s], P.cnt[s])
                E.sem_inc(P.sems["sgb"], 1)
            P.cnt["sgb"] += 1
            if not P.emit:
                P.ms[f"bar.{tag}.gpsimd"] = P.cnt["sgb"]
        elif eng == "sync":
            if P.emit:
                for s in SYNC_SEMS:
                    if P.cnt[s]:
                        E.wait_ge(P.sems[s], P.cnt[s])
            P.op(E, "nop", inc="sy", key=f"bar.{tag}.{eng}")
        else:
            P.op(E, "drain", inc=SEM_OF[eng], key=f"bar.{tag}.{eng}")
        for o in ENGS:
            if o != eng and o in _BE:
                P.wait(E, BSEM_OF[o], f"bar.{tag}.{o}")

    def ln_stats_T(E, tag, prev, rhs1, rhs2, width=512):
        if prev is not None:
            P.wait(E, "sc", f"mq.{prev}")
        for j in range(16):
            P.op(E, "matmul", ps1[:, 0:width], ones128[:], rhs1(j),
                 start=(j == 0), stop=(j == 15))
        for j in range(16):
            P.op(E, "matmul", ps2[:, 0:width], ones128[:], rhs2(j),
                 start=(j == 0), stop=(j == 15),
                 **({"inc": "st", "key": f"st.{tag}"} if j == 15 else {}))

    def ln_stats_S(E, tag, prev, width=512, prev_done=None):
        P.wait(E, "st", f"st.{tag}")
        if prev is not None:
            P.wait(E, "sv", f"mr.{prev}")
        if prev_done is not None:
            P.wait(E, "sv", prev_done)
        P.op(E, "activation", sm_bc[:, 0:width], ps1[:, 0:width],
             AF.Identity, scale=1.0 / D)
        P.op(E, "activation", sq_bc[:, 0:width], ps2[:, 0:width],
             AF.Identity, scale=1.0 / D, inc="sc", key=f"mq.{tag}")
        P.wait(E, "sv", f"var.{tag}")
        P.op(E, "activation", st_bc[:, 0:width], sv_bc[:, 0:width],
             AF.Ln, bias=eps_t[:])
        P.op(E, "activation", rb_bf[:, 0:width], st_bc[:, 0:width],
             AF.Exp, scale=-0.5, inc="sc", key=f"r.{tag}")

    def ln_stats_V(E, tag, width=512):
        P.wait(E, "sc", f"mq.{tag}")
        P.op(E, "tensor_tensor", st_bc[:, 0:width], sm_bc[:, 0:width],
             sm_bc[:, 0:width], ALU.mult)
        P.op(E, "tensor_tensor", sv_bc[:, 0:width], sq_bc[:, 0:width],
             st_bc[:, 0:width], ALU.subtract, inc="sv", key=f"var.{tag}")
        P.wait(E, "sc", f"r.{tag}")
        P.op(E, "tensor_tensor", mb_bf[:, 0:width], sm_bc[:, 0:width],
             rb_bf[:, 0:width], ALU.mult, inc="sv", key=f"mr.{tag}")

    def p1ac_sync(E):
        for dst, srcap in ((c_bqk, bqk), (c_bv, bvd), (c_bo, bod),
                           (c_gs, gsd), (c_us, usd), (c_ds, dsd)):
            P.op(E, "dma_start", dst[:], srcap, inc="scst")
        if not P.emit:
            P.ms["p0.loads"] = P.cnt["scst"]

    def p1ac_gpsimd(E):
        xr = xt.rearrange("(j p) t -> p j t", p=128)
        for p in range(NPAN1):
            if p >= 2:
                P.wait(E, "sv", f"nrm1.{p-2}")
            P.op(E, "dma_start", xtf[p % 2][:],
                 xr[:, :, p * PW1:(p + 1) * PW1],
                 inc="gx1a" if p % 2 == 0 else "gx1b", key=f"x1.{p}")

    def p1ac_vector(E):
        P.wait(E, "scst", "p0.loads")
        P.op(E, "memset", ones128[:], 1.0)
        P.op(E, "memset", ones_r1b[:], 1.0)
        P.op(E, "memset", eps_t[:], EPS, inc="sv", key="p0.ready")
        for p in range(NPAN1):
            tag = f"p1.{p}"
            ln_stats_V(E, tag, width=PW1)
            for j in range(16):
                P.op(E, "tensor_tensor", tnrm[:, 0:PW1], xtf[p % 2][:, j, :],
                     rb_bf[:, 0:PW1], ALU.mult)
                P.op(E, "tensor_tensor", lnx[:, j, p * PW1:(p + 1) * PW1],
                     tnrm[:, 0:PW1], mb_bf[:, 0:PW1], ALU.subtract,
                     **({"inc": "sv", "key": f"nrm1.{p}"} if j == 15 else {}))

    def p1ac_tensor(E):
        for p in range(NPAN1):
            tag = f"p1.{p}"
            prev = f"p1.{p-1}" if p else None
            P.wait(E, "sc", f"sq1.{p}")
            ln_stats_T(E, tag, prev,
                       lambda j: xtb[:, j, :],
                       lambda j: xsq[:, j, :], width=PW1)

    def p1ac_scalar(E):
        P.wait(E, "sv", "p0.ready")
        for p in range(NPAN1):
            P.wait(E, "gx1a" if p % 2 == 0 else "gx1b", f"x1.{p}")
            if p >= 1:
                P.wait(E, "st", f"st.p1.{p-1}")
            P.op(E, "activation", xtb[:], xtf[p % 2][:], AF.Identity)
            P.op(E, "activation", xsq[:], xtf[p % 2][:], AF.Square,
                 inc="sc", key=f"sq1.{p}")
            ln_stats_S(E, f"p1.{p}", f"p1.{p-1}" if p else None, width=PW1,
                       prev_done=f"nrm1.{p-1}" if p else None)

    def p1d_gpsimd(E):
        barrier("b1", "gpsimd", E)
        wr = wqkv.rearrange("(j p) f -> p j f", p=128)
        def gidx(fc):
            return (fc + 1) * 4 - 1 if fc < 16 else 64 + (fc - 15) * 8 - 1
        for fc in range(K1FC):
            if fc >= 2:
                P.wait(E, "st", f"q.{gidx(fc-2)}")
            P.op(E, "dma_start", wqk_sb[fc % 2][:],
                 wr[:, :, fc * 128:(fc + 1) * 128],
                 inc=f"gw{fc % 2}", key=f"w.{fc}")
        for vf in range(4 if K1V else 0):
            if vf >= 1:
                P.wait(E, "st", f"v.{vf*32-1}")
            P.op(E, "dma_start", wv_sb[:],
                 wr[:, :, 2 * D + vf * 512:2 * D + (vf + 1) * 512],
                 inc="gwv", key=f"wv.{vf}")

    def p1d_tensor(E):
        barrier("b1", "tensor", E)
        gi = 0
        for fc in range(K1FC):
            P.wait(E, f"gw{fc % 2}", f"w.{fc}")
            nts = 4 if fc < 16 else 8
            for ts in range(nts):
                bank = pQ[gi % 2]
                if gi >= 2:
                    P.wait(E, "sv", f"qe.{gi-2}")
                for j in range(16):
                    P.op(E, "matmul", bank[:], wqk_sb[fc % 2][:, j, :],
                         lnx[:, j, ts * 512:(ts + 1) * 512],
                         start=(j == 0), stop=(j == 15),
                         **({"inc": "st", "key": f"q.{gi}"} if j == 15 else {}))
                gi += 1
        vi = 0
        for vf in range(4 if K1V else 0):
            P.wait(E, "gwv", f"wv.{vf}")
            for tc in range(32):
                bank = pQ[vi % 2]
                if vi >= 2:
                    P.wait(E, "sc", f"ve.{vi-2}")
                elif K1FC == 32:
                    P.wait(E, "sv", f"qe.{190+vi}")
                for j in range(16):
                    P.op(E, "matmul", bank[:],
                         lnx[:, j, tc * 128:(tc + 1) * 128], wv_sb[:, j, :],
                         start=(j == 0), stop=(j == 15),
                         **({"inc": "st", "key": f"v.{vi}"} if j == 15 else {}))
                vi += 1

    def p1d_vector(E):
        barrier("b1", "vector", E)
        gi = 0
        for fc in range(K1FC):
            nts = 4 if fc < 16 else 8
            for ts in range(nts):
                bank = pQ[gi % 2]
                P.wait(E, "st", f"q.{gi}")
                if fc < 16:
                    if fc >= 1 and ts == 0 and K1SP:
                        P.wait(E, "ssp", f"qsp.{fc-1}")
                    dst = qstage[:, ts, :]
                else:
                    if ts % 4 == 0 and not (fc == 16 and ts == 0) and K1SP:
                        P.wait(E, "ssp", f"ksp.{(fc-16)*2 + ts//4 - 1}")
                    dst = kstage[:, ts % 4, :]
                P.op(E, "tensor_scalar", dst, bank[:], c_bqk[:, fc:fc + 1],
                     None, ALU.add, inc="sv", key=f"qe.{gi}")
                gi += 1

    def p1d_scalar(E):
        barrier("b1", "scalar", E)
        vi = 0
        for vf in range(4 if K1V else 0):
            for tc in range(32):
                bank = pQ[vi % 2]
                P.wait(E, "st", f"v.{vi}")
                if vi >= 2:
                    P.wait(E, f"svs{vi % 2}", f"vsp.{vi-2}")
                P.op(E, "copy", vstage[vi % 2][:], bank[:], inc="sc",
                     key=f"ve.{vi}")
                vi += 1

    def p1d_sync(E):
        barrier("b1", "sync", E)
        gi = 0
        for fc in range(min(32, K1FC) if K1SP else 0):
            if fc < 16:
                gi += 4
                P.wait(E, "sv", f"qe.{gi-1}")
                P.op(E, "dma_start", qs[fc * 128:(fc + 1) * 128, :],
                     qstage.ap().rearrange("p a t -> p (a t)"),
                     inc="ssp", key=f"qsp.{fc}")
            else:
                for g2 in range(2):
                    gi += 4
                    P.wait(E, "sv", f"qe.{gi-1}")
                    P.op(E, "dma_start",
                         ks[(fc - 16) * 128:(fc - 15) * 128,
                            g2 * 2048:(g2 + 1) * 2048],
                         kstage.ap().rearrange("p a t -> p (a t)"),
                         inc="ssp", key=f"ksp.{(fc-16)*2+g2}")
        vi = 0
        for vf in range(4 if K1V else 0):
            for tc in range(32):
                P.wait(E, "sc", f"ve.{vi}")
                P.op(E, "dma_start",
                     vs[tc * 128:(tc + 1) * 128, vf * 512:(vf + 1) * 512],
                     vstage[vi % 2][:], inc=f"svs{vi % 2}", key=f"vsp.{vi}")
                vi += 1


    def k_iter_list(qt):
        its = []
        for j in range(4 * qt + 4):
            r = j - 4 * qt
            its.append((j * 128, j, r if r >= 0 else None, False))
        for j in range(4 * qt + 4):
            r = j - 4 * qt
            its.append((TOWN + j * 128, 16 + j, r if r >= 0 else None, True))
        return its

    def p2_sync(E):
        barrier("b2", "sync", E)
        P.op(E, "dma_start", c_tri[:], mtri, inc="scst")
        P.op(E, "dma_start", c_mrow[:], mrow, inc="scst", key="p2.consts")
        P.wait(E, "ssp", "ksp.31")
        P.wait(E, "svs0", "vsp.126")
        P.wait(E, "svs1", "vsp.127")
        vsr = vs.rearrange("(a p) f -> p a f", p=128)
        for h in range(NH):
            if h >= 2:
                P.wait(E, "st", f"az.{80*(h-2)+79}")
            P.op(E, "dma_start", kh[h % 2][:], ks[h * 128:(h + 1) * 128, :],
                 inc=f"shl{h % 2}")
            P.op(E, "dma_start", qh[h % 2][:], qs[h * 128:(h + 1) * 128, :],
                 inc=f"shl{h % 2}")
            P.op(E, "dma_start", vh[h % 2][:],
                 vsr[:, :, h * 128:(h + 1) * 128], inc=f"shl{h % 2}",
                 key=f"hl.{h}")

    def p2_tensor(E):
        barrier("b2", "tensor", E)
        P.wait(E, "scst", "p2.consts")
        for r in range(4):
            if r:
                P.wait(E, "sv", f"mbe.{r-1}")
            P.op(E, "matmul", pbc1[:], ones_r1b[:],
                 c_mrow[:, r * 512:(r + 1) * 512], start=True, stop=True,
                 inc="st", key=f"mbld.{r}")
        P.wait(E, "sv", "mbe.3")
        gi = 0
        for h in range(NH):
            P.wait(E, f"shl{h % 2}", f"hl.{h}")
            for qt in range(4):
                its = k_iter_list(qt)
                n = len(its)
                base = gi

                def s_mm(li):
                    g = base + li
                    col, vchunk, r, part = its[li]
                    c0 = 128 * (r or 0)
                    bank = pS3[g % 3]
                    if g >= 3:
                        P.wait(E, "sc", f"e.{g-3}")
                    P.op(E, "matmul", bank[:, c0:512],
                         kh[h % 2][:, col:col + 128],
                         qh[h % 2][:, qt * 512 + c0:(qt + 1) * 512],
                         start=True, stop=True, inc="st", key=f"s.{g}")

                def av_z(li):
                    g = base + li
                    col, vchunk, r, part = its[li]
                    c0 = 128 * (r or 0)
                    hq0 = h * 4 + qt
                    P.wait(E, "sc", f"e.{g}")
                    if li == 0 and hq0 >= 2:
                        P.wait(E, "sv", f"aoe.{hq0-2}")
                        P.wait(E, "sc", f"z.{hq0-2}")
                    P.op(E, "matmul", pavq[hq0 % 2][:, c0:512],
                         vh[h % 2][:, vchunk, :], p_sb[g % 3][:, c0:512],
                         start=(li == 0), stop=(li == n - 1))
                    P.op(E, "matmul", pzq[hq0 % 2][:, c0:512], ones128[:],
                         p_sb[g % 3][:, c0:512],
                         start=(li == 0), stop=(li == n - 1),
                         inc="st", key=f"az.{g}")

                s_mm(0)
                s_mm(1)
                for li in range(2, n):
                    s_mm(li)
                    av_z(li - 2)
                av_z(n - 2)
                av_z(n - 1)
                gi += n

    def p2_vector(E):
        barrier("b2", "vector", E)
        for r in range(4):
            P.wait(E, "st", f"mbld.{r}")
            P.op(E, "tensor_copy", mask_bc[:, r, :], pbc1[:],
                 inc="sv", key=f"mbe.{r}")
        gi = 0
        for h in range(NH):
            for qt in range(4):
                its = k_iter_list(qt)
                for li, (col, vchunk, r, part) in enumerate(its):
                    g = gi + li
                    if r is None:
                        continue
                    c0 = 128 * r
                    P.wait(E, "st", f"s.{g}")
                    bank = pS3[g % 3]
                    if not part:
                        P.op(E, "tensor_tensor", bank[:, c0:c0 + 128],
                             bank[:, c0:c0 + 128], c_tri[:], ALU.add,
                             inc="sv", key=f"m.{g}")
                    else:
                        P.op(E, "tensor_tensor", bank[:, c0:512],
                             bank[:, c0:512], mask_bc[:, r, c0:512],
                             ALU.add, inc="sv", key=f"m.{g}")
                gi += len(its)
                hq = h * 4 + qt
                P.wait(E, "sc", f"z.{hq}")
                P.op(E, "scalar_tensor_tensor", t_ao[:], rzb_sb[hq % 2][:],
                     1.0, pavq[hq % 2][:], ALU.mult, ALU.mult)
                P.op(E, "tensor_scalar", ao[:, h, qt * 512:(qt + 1) * 512],
                     t_ao[:], c_bv[:, h:h + 1], None, ALU.add,
                     inc="sv", key=f"aoe.{hq}")

    def p2_scalar(E):
        barrier("b2", "scalar", E)
        gi = 0
        for h in range(NH):
            for qt in range(4):
                its = k_iter_list(qt)
                n = len(its)
                for li, (col, vchunk, r, part) in enumerate(its):
                    g = gi + li
                    c0 = 128 * (r or 0)
                    if r is not None:
                        P.wait(E, "sv", f"m.{g}")
                    else:
                        P.wait(E, "st", f"s.{g}")
                    if g >= 3:
                        P.wait(E, "st", f"az.{g-3}")
                    P.op(E, "activation", p_sb[g % 3][:, c0:512],
                         pS3[g % 3][:, c0:512], AF.Exp, scale=EXP_SCALE,
                         inc="sc", key=f"e.{g}")
                gi += n
                hq = h * 4 + qt
                P.wait(E, "st", f"az.{gi-1}")
                if hq >= 2:
                    P.wait(E, "sv", f"aoe.{hq-2}")
                P.op(E, "activation", zl128[:], pzq[hq % 2][:], AF.Ln)
                P.op(E, "activation", rzb_sb[hq % 2][:], zl128[:], AF.Exp,
                     scale=-1.0, inc="sc", key=f"z.{hq}")

    pQ4 = [pA, pB, pav, pbc2]

    def p3_gpsimd(E):
        barrier("b3", "gpsimd", E)
        for ts in range(4):
            for oc in range(16):
                gg = ts * 16 + oc
                if gg >= 2:
                    P.wait(E, "st", f"y.{gg-2}")
                P.op(E, "dma_start", wo_sb[gg % 2][:],
                     wod[oc], inc=f"gwo{gg % 2}",
                     key=f"wol.{gg}")

    def p3_tensor(E):
        barrier("b3", "tensor", E)
        prev = None
        for ts in range(4):
            sl = slice(ts * 512, (ts + 1) * 512)
            for oc in range(16):
                gg = ts * 16 + oc
                bank = pQ4[gg % 4]
                P.wait(E, f"gwo{gg % 2}", f"wol.{gg}")
                if gg >= 4:
                    P.wait(E, "sv", f"ye.{gg-4}")
                for j in range(16):
                    P.op(E, "matmul", bank[:], wo_sb[gg % 2][:, j, :],
                         ao[:, j, sl],
                         start=(j == 0), stop=(j == 15),
                         **({"inc": "st", "key": f"y.{gg}"} if j == 15 else {}))
            P.wait(E, "sv", f"sql2.{ts}")
            ln_stats_T(E, f"l2.{ts}", prev,
                       lambda j: yb[:, j, sl], lambda j: l3t[:, j, :])
            prev = f"l2.{ts}"
            P.wait(E, "sv", f"sql3.{ts}")
            ln_stats_T(E, f"l3.{ts}", prev,
                       lambda j: x2b[:, j, :], lambda j: l3t[:, j, :])
            prev = f"l3.{ts}"

    def p3_scalar(E):
        barrier("b3", "scalar", E)
        prev = None
        pdone = None
        for ts in range(4):
            ln_stats_S(E, f"l2.{ts}", prev, prev_done=pdone)
            prev = f"l2.{ts}"
            pdone = f"x2.{ts}.3"
            ln_stats_S(E, f"l3.{ts}", prev, prev_done=pdone)
            prev = f"l3.{ts}"
            pdone = f"l3n.{ts}"

    def p3_vector(E):
        barrier("b3", "vector", E)
        for ts in range(4):
            sl = slice(ts * 512, (ts + 1) * 512)
            for oc in range(16):
                gg = ts * 16 + oc
                bank = pQ4[gg % 4]
                P.wait(E, "st", f"y.{gg}")
                P.op(E, "tensor_scalar", yb[:, oc, sl],
                     bank[:], c_bo[:, oc:oc + 1], None, ALU.add,
                     inc="sv", key=f"ye.{gg}")
            if ts >= 1:
                P.wait(E, "sl3w", f"l3sp.{ts-1}")
            for j in range(16):
                P.op(E, "tensor_tensor", l3t[:, j, :], yb[:, j, sl],
                     yb[:, j, sl], ALU.mult,
                     **({"inc": "sv", "key": f"sql2.{ts}"} if j == 15 else {}))
            ln_stats_V(E, f"l2.{ts}")
            P.wait(E, "st", f"st.l2.{ts}")
            if ts >= 1:
                P.wait(E, "sx2w", f"x2w.{ts-1}")
            for qtr in range(4):
                P.wait(E, "sxo", f"xo.{ts}.{qtr}")
                for j4 in range(4):
                    j = qtr * 4 + j4
                    P.op(E, "tensor_tensor", tnrm[:], yb[:, j, sl],
                         rb_bf[:], ALU.mult)
                    P.op(E, "tensor_tensor", tnrm[:], tnrm[:], mb_bf[:],
                         ALU.subtract)
                    P.op(E, "tensor_tensor", x2b[:, j, :], xo_h[:, j4, :],
                         tnrm[:], ALU.add,
                         **({"inc": "sv", "key": f"x2.{ts}.{qtr}"}
                            if j4 == 3 else {}))
            for j in range(16):
                P.op(E, "tensor_tensor", l3t[:, j, :], x2b[:, j, :],
                     x2b[:, j, :], ALU.mult,
                     **({"inc": "sv", "key": f"sql3.{ts}"} if j == 15 else {}))
            ln_stats_V(E, f"l3.{ts}")
            P.wait(E, "st", f"st.l3.{ts}")
            P.wait(E, "sx2w", f"x2w.{ts}")
            for j in range(16):
                P.op(E, "tensor_tensor", tnrm[:], x2b[:, j, :], rb_bf[:],
                     ALU.mult)
                P.op(E, "tensor_tensor", l3t[:, j, :], tnrm[:], mb_bf[:],
                     ALU.subtract,
                     **({"inc": "sv", "key": f"l3n.{ts}"} if j == 15 else {}))

    def p3_sync(E):
        barrier("b3", "sync", E)
        xr = xt.rearrange("(j p) t -> p j t", p=128)
        x2r = x2s.rearrange("(j p) t -> p j t", p=128)
        l3r = l3s.rearrange("(j p) t -> p j t", p=128)
        for ts in range(4):
            sl = slice(ts * 512, (ts + 1) * 512)
            for qtr in range(4):
                if ts >= 1 and qtr == 0:
                    P.wait(E, "sv", f"x2.{ts-1}.3")
                if qtr >= 1:
                    P.wait(E, "sv", f"x2.{ts}.{qtr-1}")
                P.op(E, "dma_start", xo_h[:],
                     xr[:, qtr * 4:(qtr + 1) * 4, sl],
                     inc="sxo", key=f"xo.{ts}.{qtr}")
            P.wait(E, "sv", f"x2.{ts}.3")
            P.op(E, "dma_start", x2r[:, :, sl], x2b[:], inc="sx2w",
                 key=f"x2w.{ts}")
            P.wait(E, "sv", f"l3n.{ts}")
            P.op(E, "dma_start", l3r[:, :, sl], l3t[:], inc="sl3w",
                 key=f"l3sp.{ts}")

    def wait_gate_done(E, grp):
        P.wait(E, "st", f"u.{grp*128+127}")
        P.wait(E, "sv", f"hh.{grp*128+127}")

    def wait_down_done(E, grp):
        P.wait(E, "sv", f"x3j.{grp}.1.1.7")
        P.wait(E, "st", f"d.{grp*32+31}")
        P.wait(E, "sod0", f"od.{grp}.1.1.6")
        P.wait(E, "sod1", f"od.{grp}.1.1.7")

    def odkey(seq):
        g, r = divmod(seq, 32)
        t, r2 = divmod(r, 16)
        hf, j8 = divmod(r2, 8)
        return f"od.{g}.{t}.{hf}.{j8}"

    def p4_gpsimd(E):
        barrier("b4", "gpsimd", E)
        wgr = wgt.rearrange("(j p) i -> p j i", p=128)
        wur = wut.rearrange("(j p) i -> p j i", p=128)
        wdr = wdt.rearrange("(m p) o -> p m o", p=128)
        for grp in range(2):
            for ic in range(64):
                gg = grp * 64 + ic
                if grp >= 1 and ic < 2:
                    wait_down_done(E, grp - 1)
                if gg >= 2:
                    P.wait(E, "st", f"u.{(gg-2)*2+1}")
                P.op(E, "dma_start", wg_sb[ic % 2][:],
                     wgr[:, :, ic * 128:(ic + 1) * 128], inc=f"ggu{gg % 2}")
                P.op(E, "dma_start", wu_sb[ic % 2][:],
                     wur[:, :, ic * 128:(ic + 1) * 128],
                     inc=f"ggu{gg % 2}", key=f"wgu.{gg}")
            hi = 0
            for ts in range(2):
                for oc in range(16):
                    for half in range(2):
                        gg2 = grp * 64 + hi
                        if hi < 2:
                            wait_gate_done(E, grp)
                        if gg2 >= 2:
                            pv = gg2 - 2
                            phi = pv % 64
                            pgrp = pv // 64
                            pts, prr = divmod(phi, 32)
                            poc, phalf = divmod(prr, 2)
                            pkey = (f"d.{pgrp*32+pts*16+poc}" if phalf == 1
                                    else f"dmmA.{pv}")
                            P.wait(E, "st", pkey)
                        P.op(E, "dma_start", wd_sb[hi % 2][:],
                             wdr[:, half * 32:(half + 1) * 32,
                                 oc * 128:(oc + 1) * 128],
                             inc=f"gwd{gg2 % 2}", key=f"wd.{gg2}")
                        hi += 1

    def p4_sync(E):
        barrier("b4", "sync", E)
        P.wait(E, "sx2w", "x2w.3")
        P.wait(E, "sl3w", "l3sp.3")
        l3r = l3s.rearrange("(j p) t -> p j t", p=128)
        x2r = x2s.rearrange("(j p) t -> p j t", p=128)
        orr = out.rearrange("(j p) t -> p j t", p=128)
        for grp in range(2):
            gsl = slice(grp * 1024, (grp + 1) * 1024)
            if grp >= 1:
                P.wait(E, "st", f"u.{(grp-1)*128+127}")
                wait_down_done(E, grp - 1)
            P.op(E, "dma_start", l3_sb[:], l3r[:, :, gsl], inc="sl3l",
                 key=f"l3l.{grp}")
            for ts in range(2):
                t0 = grp * 1024 + ts * 512
                for half in range(2):
                    seq = (grp * 2 + ts) * 2 + half
                    if ts == 0 and half == 0:
                        wait_gate_done(E, grp)
                    if seq >= 1:
                        g_, r_ = divmod(seq - 1, 4)
                        t_, hf_ = divmod(r_, 2)
                        P.wait(E, "sv", f"x3j.{g_}.{t_}.{hf_}.7")
                    P.op(E, "dma_start", x2h[:],
                         x2r[:, half * 8:(half + 1) * 8, t0:t0 + 512],
                         inc="sx2l", key=f"x2l.{grp}.{ts}.{half}")
                    for j8 in range(8):
                        P.wait(E, "sv", f"x3j.{grp}.{ts}.{half}.{j8}")
                        P.op(E, "dma_start",
                             orr[:, half * 8 + j8, t0:t0 + 512],
                             x3b[j8 % 2][:], inc=f"sod{j8 % 2}",
                             key=f"od.{grp}.{ts}.{half}.{j8}")
        if KDBG:
            P.op(E, "dma_start", dbg, x2s, inc="sx2w")
        if not P.emit:
            P.ms["end.sod0"] = P.cnt["sod0"]
            P.ms["end.sod1"] = P.cnt["sod1"]
        if P.emit:
            E.wait_ge(P.sems["sod0"], P.ms["end.sod0"])
            E.wait_ge(P.sems["sod1"], P.ms["end.sod1"])

    def p4_tensor(E):
        barrier("b4", "tensor", E)
        for grp in range(2):
            P.wait(E, "sl3l", f"l3l.{grp}")
            gi = 0
            for ic in range(64):
                gg = grp * 64 + ic
                P.wait(E, f"ggu{gg % 2}", f"wgu.{gg}")
                for ts in range(2):
                    g2 = grp * 128 + gi
                    if g2 >= 1:
                        P.wait(E, "sc", f"sl.{g2-1}")
                    for j in range(16):
                        P.op(E, "matmul", pA[:], wg_sb[ic % 2][:, j, :],
                             l3_sb[:, j, ts * 512:(ts + 1) * 512],
                             start=(j == 0), stop=(j == 15),
                             **({"inc": "st", "key": f"g.{g2}"}
                                if j == 15 else {}))
                    if g2 >= 1:
                        P.wait(E, "sv", f"he1.{g2-1}")
                    for j in range(16):
                        P.op(E, "matmul", pB[:], wu_sb[ic % 2][:, j, :],
                             l3_sb[:, j, ts * 512:(ts + 1) * 512],
                             start=(j == 0), stop=(j == 15),
                             **({"inc": "st", "key": f"u.{g2}"}
                                if j == 15 else {}))
                    gi += 1
            for ts in range(2):
                P.wait(E, "sv", f"hh.{grp*128+126+ts}")
                if grp * 2 + ts > 0:
                    pg, pt = divmod(grp * 2 + ts - 1, 2)
                    P.wait(E, "sc", f"rsbe.{pg}.{pt}")
                for m in range(64):
                    P.op(E, "matmul", pz[:], ones128[:],
                         hact[:, m, ts * 512:(ts + 1) * 512],
                         start=(m == 0), stop=(m == 63),
                         **({"inc": "st", "key": f"rs.{grp}.{ts}"}
                            if m == 63 else {}))
            hi = 0
            for ts in range(2):
                tag = f"ln4.{grp}.{ts}"
                prev = (f"ln4.{grp}.{ts-1}" if ts else
                        (f"ln4.{grp-1}.1" if grp else None))

                def lnmm(j):
                    if j == 0 and prev is not None:
                        P.wait(E, "sc", f"mq.{prev}")
                    P.wait(E, "sv", f"fsq.{grp}.{ts}.{j}")
                    P.op(E, "matmul", ps1[:], ones128[:], ffnb[:, j, :],
                         start=(j == 0), stop=(j == 15))
                    P.op(E, "matmul", ps2[:], ones128[:], sqtmp[j % 2][:],
                         start=(j == 0), stop=(j == 15),
                         **({"inc": "st", "key": f"st.{tag}"} if j == 15
                            else {"inc": "st", "key": f"lnmm.{grp}.{ts}.{j}"}))

                for oc in range(16):
                    dd = grp * 32 + ts * 16 + oc
                    for half in range(2):
                        gg2 = grp * 64 + hi
                        P.wait(E, f"gwd{gg2 % 2}", f"wd.{gg2}")
                        if half == 0 and dd >= 2:
                            P.wait(E, "sv", f"de.{dd-2}")
                        mk_ = ({"inc": "st", "key": f"d.{dd}"} if half == 1
                               else {"inc": "st", "key": f"dmmA.{gg2}"})
                        for m in range(32):
                            P.op(E, "matmul", pQ[dd % 2][:],
                                 wd_sb[hi % 2][:, m, :],
                                 hact[:, half * 32 + m,
                                      ts * 512:(ts + 1) * 512],
                                 start=(half == 0 and m == 0),
                                 stop=(half == 1 and m == 31),
                                 **(mk_ if m == 31 else {}))
                        hi += 1
                    if oc >= 2:
                        lnmm(oc - 2)
                lnmm(14)
                lnmm(15)


    def p4_scalar(E):
        barrier("b4", "scalar", E)
        for grp in range(2):
            gi = 0
            for ic in range(64):
                for ts in range(2):
                    g2 = grp * 128 + gi
                    P.wait(E, "st", f"g.{g2}")
                    if g2 >= 1:
                        P.wait(E, "sv", f"hh.{g2-1}")
                    if g2 >= 1:
                        P.wait(E, "sv", f"he1.{g2-1}")
                    P.op(E, "activation", silu_sb[:], pA[:], AF.Silu,
                         scale=c_gs[:, ic:ic + 1], inc="sc", key=f"sl.{g2}")
                    gi += 1
            for ts in range(2):
                P.wait(E, "st", f"rs.{grp}.{ts}")
                if grp * 2 + ts >= 2:
                    pg, pt = divmod(grp * 2 + ts - 2, 2)
                    P.wait(E, "sv", f"de.{pg*32+pt*16+15}")
                P.op(E, "activation", rsb_sb[ts][:], pz[:],
                     AF.Identity, scale=-128.0, inc="sc",
                     key=f"rsbe.{grp}.{ts}")
            for ts in range(2):
                tag = f"ln4.{grp}.{ts}"
                prev = (f"ln4.{grp}.{ts-1}" if ts else
                        (f"ln4.{grp-1}.1" if grp else None))
                pdone = (f"x3j.{grp}.0.1.7" if ts else
                         (f"x3j.{grp-1}.1.1.7" if grp else None))
                ln_stats_S(E, tag, prev, prev_done=pdone)

    def p4_vector(E):
        barrier("b4", "vector", E)
        for grp in range(2):
            gi = 0
            for ic in range(64):
                for ts in range(2):
                    g2 = grp * 128 + gi
                    P.wait(E, "sc", f"sl.{g2}")
                    P.wait(E, "st", f"u.{g2}")
                    P.op(E, "tensor_tensor", tnrm[:], silu_sb[:], pB[:],
                         ALU.mult, inc="sv", key=f"he1.{g2}")
                    P.op(E, "tensor_scalar",
                         hact[:, ic, ts * 512:(ts + 1) * 512],
                         tnrm[:], c_us[:, ic:ic + 1], None, ALU.mult,
                         inc="sv", key=f"hh.{g2}")
                    gi += 1
            for ts in range(2):
                tag = f"ln4.{grp}.{ts}"
                P.wait(E, "sc", f"rsbe.{grp}.{ts}")
                for oc in range(16):
                    dd = grp * 32 + ts * 16 + oc
                    P.wait(E, "st", f"d.{dd}")
                    P.op(E, "scalar_tensor_tensor", t_ao[:], rsb_sb[ts][:],
                         1.0, pQ[dd % 2][:], ALU.mult, ALU.add)
                    P.op(E, "tensor_scalar", ffnb[:, oc, :], t_ao[:],
                         c_ds[:, oc:oc + 1], None, ALU.mult,
                         inc="sv", key=f"de.{dd}")
                    if oc >= 2:
                        P.wait(E, "st", f"lnmm.{grp}.{ts}.{oc-2}")
                    P.op(E, "tensor_tensor", sqtmp[oc % 2][:], ffnb[:, oc, :],
                         ffnb[:, oc, :], ALU.mult, inc="sv",
                         key=f"fsq.{grp}.{ts}.{oc}")
                ln_stats_V(E, tag)
                for half in range(2):
                    P.wait(E, "sx2l", f"x2l.{grp}.{ts}.{half}")
                    for j8 in range(8):
                        j = half * 8 + j8
                        seq = ((grp * 2 + ts) * 2 + half) * 8 + j8
                        if seq >= 2:
                            P.wait(E, f"sod{j8 % 2}", odkey(seq - 2))
                        P.op(E, "tensor_tensor", tnrm[:], ffnb[:, j, :],
                             rb_bf[:], ALU.mult)
                        P.op(E, "tensor_tensor", tnrm[:], tnrm[:], mb_bf[:],
                             ALU.subtract)
                        P.op(E, "tensor_tensor", x3b[j8 % 2][:],
                             x2h[:, j8, :], tnrm[:], ALU.add,
                             inc="sv", key=f"x3j.{grp}.{ts}.{half}.{j8}")

    def p1d_tensor_wrap(E):
        p1ac_tensor(E)
        p1d_tensor(E)

    phase_tbl = {
        "sync": [p1ac_sync, p1d_sync, p2_sync, p3_sync, p4_sync],
        "gpsimd": [p1ac_gpsimd, p1d_gpsimd,
                   lambda E: barrier("b2", "gpsimd", E),
                   p3_gpsimd, p4_gpsimd],
        "tensor": [p1d_tensor_wrap, p2_tensor, p3_tensor, p4_tensor],
        "vector": [p1ac_vector, p1d_vector, p2_vector, p3_vector, p4_vector],
        "scalar": [p1ac_scalar, p1d_scalar, p2_scalar, p3_scalar, p4_scalar],
    }
    import os
    _L = int(os.environ.get("KPH", "4"))
    if _L == 0:
        phase_tbl = {"sync": [p1ac_sync], "gpsimd": [p1ac_gpsimd],
                     "tensor": [p1ac_tensor], "vector": [p1ac_vector],
                     "scalar": [p1ac_scalar]}
        if _os.environ.get("KB1", "0") == "1":
            for e in ENGS:
                phase_tbl[e] = phase_tbl[e] + [
                    (lambda ee: lambda E: barrier("b1", ee, E))(e)]
    elif _L < 4:
        cut = {"sync": 1 + _L, "gpsimd": 1 + _L, "tensor": _L,
               "vector": 1 + _L, "scalar": 1 + _L}
        phase_tbl = {k: v[:cut[k]] for k, v in phase_tbl.items()}

    sem_names = (["st", "sv", "sc", "sy", "sgb"] + SYNC_SEMS + GP_SEMS)
    sem_guards = []
    for sn in sem_names:
        g = nc.semaphore("sem_" + sn)
        P.sems[sn] = g.__enter__()
        sem_guards.append(g)
    with nc.Block() as block:
        P.start(False)
        for eng in ENGS:
            for fn in phase_tbl[eng]:
                fn(None)
        totals = dict(P.cnt)
        P.start(True)

        @block.sync
        def _(sync):
            for fn in phase_tbl["sync"]:
                fn(sync)

        @block.gpsimd
        def _(g):
            for fn in phase_tbl["gpsimd"]:
                fn(g)

        @block.tensor
        def _(t):
            for fn in phase_tbl["tensor"]:
                fn(t)

        @block.vector
        def _(v):
            for fn in phase_tbl["vector"]:
                fn(v)

        @block.scalar
        def _(s):
            for fn in phase_tbl["scalar"]:
                fn(s)

        for k in totals:
            assert P.cnt[k] == totals[k], (k, P.cnt[k], totals[k])
    return nc


_NC_CACHE = []


def kernel(**inputs):
    x = np.ascontiguousarray(np.asarray(inputs["L0_attn_x"], np.float32))
    Wqkv = np.ascontiguousarray(np.asarray(inputs["L0_attn_Wqkv"], np.float32))
    bqkv = np.asarray(inputs["L0_attn_bqkv"], np.float32)
    Wo = np.ascontiguousarray(
        np.asarray(inputs["L0_attn_Wo"], np.float32)
        .reshape(16, 128, 16, 128).transpose(2, 1, 0, 3))
    bo = np.asarray(inputs["L0_attn_bo"], np.float32)
    gw = np.asarray(inputs["L0_ffn_gate_w"], np.int32)
    gsc = np.asarray(inputs["L0_ffn_gate_scale"], np.float32)
    uw = np.asarray(inputs["L0_ffn_up_w"], np.int32)
    usc = np.asarray(inputs["L0_ffn_up_scale"], np.float32)
    dw = np.asarray(inputs["L0_ffn_down_w"], np.int32)
    dsc = np.asarray(inputs["L0_ffn_down_scale"], np.float32)

    wgt = np.ascontiguousarray(gw.T)
    wut = np.ascontiguousarray(uw.T)
    wdt = np.ascontiguousarray(dw.T)
    bqk = np.ascontiguousarray(bqkv[:2 * D].reshape(32, 128).T)
    bv_ = np.ascontiguousarray(bqkv[2 * D:].reshape(16, 128).T)
    bo_ = np.ascontiguousarray(bo.reshape(16, 128).T)
    gs_ = np.ascontiguousarray(gsc.reshape(64, 128).T)
    us_ = np.ascontiguousarray(usc.reshape(64, 128).T)
    ds_ = np.ascontiguousarray(dsc.reshape(16, 128).T)
    ii = np.arange(128)
    mtri = np.ascontiguousarray(
        np.where(ii[:, None] > ii[None, :], np.float32(MASK),
                 np.float32(0.0)).astype(np.float32))

    if not _NC_CACHE:
        _NC_CACHE.append(build_nc())
    nc = _NC_CACHE[0]
    in_maps = []
    perms = []
    for c in range(8):
        b, hg = divmod(c, 2)
        own = np.concatenate([np.arange(g * 128, (g + 1) * 128)
                              for g in range(hg, 32, 2)])
        par = np.concatenate([np.arange(g * 128, (g + 1) * 128)
                              for g in range(1 - hg, 32, 2)])
        perm = np.concatenate([own, par])
        perms.append(own)
        xt_c = np.ascontiguousarray(x[b].T[:, perm])
        mr = np.zeros((4, 512), np.float32)
        for r in range(4):
            mr[r, : (r + 1 - hg) * 128] = MASK
        mrow_c = np.ascontiguousarray(
            mr.reshape(1, 4 * 512)).astype(ml_dtypes.bfloat16)
        in_maps.append({
            "xt": xt_c, "wqkv": Wqkv, "bqk": bqk, "bv": bv_, "wo": Wo,
            "bo": bo_, "wgt": wgt, "wut": wut, "wdt": wdt, "gs": gs_,
            "us": us_, "ds": ds_, "mtri": mtri, "mrow": mrow_c,
        })

    res = run_bass_kernel_spmd(nc, in_maps, core_ids=list(range(8)))
    global _LAST_RES
    _LAST_RES = res.results
    outp = np.empty_like(x)
    for c in range(8):
        b = c // 2
        outp[b, perms[c], :] = res.results[c]["o"].T
    return outp


# version 13
# speedup vs baseline: 1.0235x; 1.0235x over previous
import numpy as np
import ml_dtypes

import concourse.bass as bass
import concourse.mybir as mybir
from concourse.bass_utils import run_bass_kernel_spmd

F32 = mybir.dt.float32
BF16 = mybir.dt.bfloat16
I32 = mybir.dt.int32
AF = mybir.ActivationFunctionType
ALU = mybir.AluOpType

D = 2048
S = 4096
TOWN = 2048
NH = 16
FI = 8192
EXP_SCALE = 1.0 / float(np.sqrt(128.0))
MASK = -30000.0
EPS = 1e-5
NPAN1 = 16
PW1 = 256

ENGS = ("sync", "gpsimd", "tensor", "vector", "scalar")
SEM_OF = {"sync": "sy", "gpsimd": "sgb", "tensor": "st", "vector": "sv",
          "scalar": "sc"}
SYNC_SEMS = ["scst", "ssp", "svs0", "svs1", "shl0", "shl1", "sxo", "sx2w",
             "sl3w", "sl3l", "sx2l", "sod0", "sod1"]
GP_SEMS = ["gx1a", "gx1b", "gw0", "gw1", "gwv", "gwo0", "gwo1", "ggu0",
           "ggu1", "gwd0", "gwd1"]


class Prog:
    def __init__(self):
        self.ms = {}
        self.sems = {}
        self.emit = False
        self.cnt = {}
        self.mm_mode = None

    def start(self, emit):
        import collections
        self.emit = emit
        self.cnt = collections.defaultdict(int)
        self.mm_mode = None

    def op(self, E, name, *args, inc=None, key=None, **kw):
        inst = getattr(E, name)(*args, **kw) if self.emit else None
        if inc is not None:
            amt = 1 if inc in ("st", "sv", "sc", "sy", "sgb") else 16
            self.cnt[inc] += amt
            if key is not None and not self.emit:
                assert key not in self.ms, f"dup {key}"
                self.ms[key] = self.cnt[inc]
            if self.emit and inst is not None:
                inst.then_inc(self.sems[inc], amt)
        return inst

    def wait(self, E, sem, key):
        if key is None:
            return
        if self.emit:
            assert key in self.ms, f"missing milestone {key}"
            E.wait_ge(self.sems[sem], self.ms[key])


def build_nc():
    import os as _os
    K1V = _os.environ.get("K1V", "1") == "1"
    K1SP = _os.environ.get("K1SP", "1") == "1"
    K1FC = int(_os.environ.get("K1FC", "32"))
    global NPAN1
    NPAN1 = int(_os.environ.get("KNP", "16"))
    nc = bass.Bass("TRN2", target_bir_lowering=False,
                   detect_race_conditions=False)

    xt = nc.dram_tensor("xt", [D, S], F32, kind="ExternalInput").ap()
    wqkv = nc.dram_tensor("wqkv", [D, 3 * D], F32, kind="ExternalInput").ap()
    bqk = nc.dram_tensor("bqk", [128, 32], F32, kind="ExternalInput").ap()
    bvd = nc.dram_tensor("bv", [128, 16], F32, kind="ExternalInput").ap()
    wod = nc.dram_tensor("wo", [16, 128, 16, 128], F32,
                         kind="ExternalInput").ap()
    bod = nc.dram_tensor("bo", [128, 16], F32, kind="ExternalInput").ap()
    wgt = nc.dram_tensor("wgt", [D, FI], I32, kind="ExternalInput").ap()
    wut = nc.dram_tensor("wut", [D, FI], I32, kind="ExternalInput").ap()
    wdt = nc.dram_tensor("wdt", [FI, D], I32, kind="ExternalInput").ap()
    gsd = nc.dram_tensor("gs", [128, 64], F32, kind="ExternalInput").ap()
    usd = nc.dram_tensor("us", [128, 64], F32, kind="ExternalInput").ap()
    dsd = nc.dram_tensor("ds", [128, 16], F32, kind="ExternalInput").ap()
    mtri = nc.dram_tensor("mtri", [128, 128], F32, kind="ExternalInput").ap()
    mrow = nc.dram_tensor("mrow", [1, 4 * 512], BF16,
                          kind="ExternalInput").ap()
    out = nc.dram_tensor("o", [D, TOWN], F32, kind="ExternalOutput").ap()
    qs = nc.dram_tensor("qs", [D, TOWN], BF16, kind="Internal").ap()
    ks = nc.dram_tensor("ks", [D, S], BF16, kind="Internal").ap()
    vs = nc.dram_tensor("vs", [S, D], BF16, kind="Internal").ap()
    x2s = nc.dram_tensor("x2s", [D, TOWN], BF16, kind="Internal").ap()
    KDBG = _os.environ.get("KDBG", "0") == "1"
    dbg = (nc.dram_tensor("dbg", [D, TOWN], BF16, kind="ExternalOutput").ap()
           if KDBG else None)
    dbga = (nc.dram_tensor("dbga", [128, NH * TOWN], BF16,
                           kind="ExternalOutput").ap() if KDBG else None)
    l3s = nc.dram_tensor("l3s", [D, TOWN], BF16, kind="Internal").ap()

    P = Prog()
    P.sems = {}
    guards = []

    def mk(name, shape, dt, kind="sbuf"):
        g = (nc.sbuf_tensor if kind == "sbuf" else nc.psum_tensor)(
            name, shape, dt)
        h = g.__enter__()
        guards.append(g)
        return h

    scoped = {}

    def salloc(name, shape, dt):
        g = nc.sbuf_tensor(name, shape, dt)
        h = g.__enter__()
        scoped[name] = g
        return h

    def sfree(*names):
        for n in names:
            scoped.pop(n).__exit__(None, None, None)

    c_bqk = mk("c_bqk", [128, 32], F32)
    c_bv = mk("c_bv", [128, 16], F32)
    c_bo = mk("c_bo", [128, 16], F32)
    c_gs = mk("c_gs", [128, 64], F32)
    c_us = mk("c_us", [128, 64], F32)
    c_ds = mk("c_ds", [128, 16], F32)
    ones_k = mk("ones_k", [128, 1], BF16)
    ones_r1b = mk("ones_r1b", [1, 128], BF16)
    ones_r1f = mk("ones_r1f", [1, 128], F32)
    neg128_r1f = mk("neg128_r1f", [1, 128], F32)
    eps_t = mk("eps_t", [128, 1], F32)
    ones128 = mk("ones128", [128, 128], BF16)
    sm_bc = mk("sm_bc", [128, 512], F32)
    sq_bc = mk("sq_bc", [128, 512], F32)
    st_bc = mk("st_bc", [128, 512], F32)
    sv_bc = mk("sv_bc", [128, 512], F32)
    rb_bf = mk("rb_bf", [128, 512], BF16)
    mb_bf = mk("mb_bf", [128, 512], BF16)
    t_ao = mk("t_ao", [128, 512], F32)
    tnrm = mk("tnrm", [128, 512], F32)
    sqtmp = [mk(f"sqtmp{i}", [128, 512], BF16) for i in range(2)]

    pA = mk("pA", [128, 512], F32, "psum")
    pB = mk("pB", [128, 512], F32, "psum")
    pav = mk("pav", [128, 512], F32, "psum")
    pbc1 = mk("pbc1", [128, 512], F32, "psum")
    pbc2 = mk("pbc2", [128, 512], F32, "psum")
    pz = mk("pz", [128, 512], F32, "psum")
    ps1 = mk("ps1", [128, 512], F32, "psum")
    ps2 = mk("ps2", [128, 512], F32, "psum")
    pQ = [pA, pB]
    pavq = [pav, pbc2]
    pzq = [pz, ps1]
    pS3 = [pA, pB, pbc1]

    lnx = salloc("lnx", [128, NH, S], BF16)
    xtf = [salloc(f"xtf{i}", [128, 16, PW1], F32) for i in range(2)]
    xtb = salloc("xtb", [128, 16, PW1], BF16)
    xsq = salloc("xsq", [128, 16, PW1], BF16)
    sfree("xsq", "xtb", "xtf1", "xtf0")
    wqk_sb = [salloc(f"wqk{i}", [128, 16, 128], BF16) for i in range(2)]
    wv_sb = salloc("wv_sb", [128, 16, 512], BF16)
    qstage = salloc("qstage", [128, 4, 512], BF16)
    kstage = salloc("kstage", [128, 4, 512], BF16)
    vstage = [salloc(f"vstage{i}", [128, 512], BF16) for i in range(2)]
    sfree("vstage1", "vstage0", "kstage", "qstage", "wv_sb",
          "wqk1", "wqk0", "lnx")
    ao = salloc("ao", [128, NH, TOWN], BF16)
    kh = [salloc(f"kh{i}", [128, S], BF16) for i in range(2)]
    zl128 = salloc("zl128", [128, 512], F32)
    mask_bc = salloc("mask_bc", [128, 4, 512], BF16)
    qh = [salloc(f"qh{i}", [128, TOWN], BF16) for i in range(2)]
    vh = [salloc(f"vh{i}", [128, 32, 128], BF16) for i in range(2)]
    p_sb = [salloc(f"p_sb{i}", [128, 512], BF16) for i in range(3)]
    rzb_sb = [salloc(f"rzb_sb{i}", [128, 512], BF16) for i in range(2)]
    c_tri = salloc("c_tri", [128, 128], F32)
    c_mrow = salloc("c_mrow", [1, 4 * 512], BF16)
    sfree("c_mrow", "c_tri", "rzb_sb1", "rzb_sb0",
          "p_sb2", "p_sb1", "p_sb0", "vh1", "vh0", "qh1", "qh0",
          "mask_bc", "zl128", "kh1", "kh0")
    yb = salloc("yb", [128, NH, TOWN], BF16)
    wo_sb = [salloc(f"wo{i}", [128, 16, 128], BF16) for i in range(2)]
    xo_h = salloc("xo_h", [128, 4, 512], F32)
    x2b = salloc("x2b", [128, 16, 512], BF16)
    l3t = salloc("l3t", [128, 16, 512], BF16)
    sfree("l3t", "x2b", "xo_h", "wo1", "wo0", "yb", "ao")
    hact = salloc("hact", [128, 64, 1024], BF16)
    l3_sb = salloc("l3_sb", [128, 16, 1024], BF16)
    wg_sb = [salloc(f"wg{i}", [128, 16, 128], BF16) for i in range(2)]
    wu_sb = [salloc(f"wu{i}", [128, 16, 128], BF16) for i in range(2)]
    silu_sb = salloc("silu_sb", [128, 512], F32)
    sfree("silu_sb", "wu1", "wu0", "wg1", "wg0", "l3_sb")
    wd_sb = [salloc(f"wd{i}", [128, 32, 128], BF16) for i in range(2)]
    ffnb = salloc("ffnb", [128, 16, 512], BF16)
    x2h = salloc("x2h", [128, 8, 512], BF16)
    x3b = [salloc(f"x3b{i}", [128, 512], F32) for i in range(2)]
    rsb_sb = [salloc(f"rsb{i}", [128, 512], F32) for i in range(2)]

    BSEM_OF = {"sync": "sy", "gpsimd": "sgb", "tensor": "st",
               "vector": "sv", "scalar": "sc"}

    _BE = set(_os.environ.get("KBENGS", "sync,gpsimd,tensor,vector,scalar").split(","))

    def barrier(tag, eng, E):
        if eng not in _BE:
            return
        if eng == "gpsimd":
            if P.emit:
                for s in GP_SEMS:
                    if P.cnt[s]:
                        E.wait_ge(P.sems[s], P.cnt[s])
                E.sem_inc(P.sems["sgb"], 1)
            P.cnt["sgb"] += 1
            if not P.emit:
                P.ms[f"bar.{tag}.gpsimd"] = P.cnt["sgb"]
        elif eng == "sync":
            if P.emit:
                for s in SYNC_SEMS:
                    if P.cnt[s]:
                        E.wait_ge(P.sems[s], P.cnt[s])
            P.op(E, "nop", inc="sy", key=f"bar.{tag}.{eng}")
        else:
            P.op(E, "drain", inc=SEM_OF[eng], key=f"bar.{tag}.{eng}")
        for o in ENGS:
            if o != eng and o in _BE:
                P.wait(E, BSEM_OF[o], f"bar.{tag}.{o}")

    def ln_stats_T(E, tag, prev, rhs1, rhs2, width=512):
        if prev is not None:
            P.wait(E, "sc", f"mq.{prev}")
        for j in range(16):
            P.op(E, "matmul", ps1[:, 0:width], ones128[:], rhs1(j),
                 start=(j == 0), stop=(j == 15))
        for j in range(16):
            P.op(E, "matmul", ps2[:, 0:width], ones128[:], rhs2(j),
                 start=(j == 0), stop=(j == 15),
                 **({"inc": "st", "key": f"st.{tag}"} if j == 15 else {}))

    def ln_stats_S(E, tag, prev, width=512, prev_done=None):
        P.wait(E, "st", f"st.{tag}")
        if prev is not None:
            P.wait(E, "sv", f"mr.{prev}")
        if prev_done is not None:
            P.wait(E, "sv", prev_done)
        P.op(E, "activation", sm_bc[:, 0:width], ps1[:, 0:width],
             AF.Identity, scale=1.0 / D)
        P.op(E, "activation", sq_bc[:, 0:width], ps2[:, 0:width],
             AF.Identity, scale=1.0 / D, inc="sc", key=f"mq.{tag}")
        P.wait(E, "sv", f"var.{tag}")
        P.op(E, "activation", st_bc[:, 0:width], sv_bc[:, 0:width],
             AF.Ln, bias=eps_t[:])
        P.op(E, "activation", rb_bf[:, 0:width], st_bc[:, 0:width],
             AF.Exp, scale=-0.5, inc="sc", key=f"r.{tag}")

    def ln_stats_V(E, tag, width=512):
        P.wait(E, "sc", f"mq.{tag}")
        P.op(E, "tensor_tensor", st_bc[:, 0:width], sm_bc[:, 0:width],
             sm_bc[:, 0:width], ALU.mult)
        P.op(E, "tensor_tensor", sv_bc[:, 0:width], sq_bc[:, 0:width],
             st_bc[:, 0:width], ALU.subtract, inc="sv", key=f"var.{tag}")
        P.wait(E, "sc", f"r.{tag}")
        P.op(E, "tensor_tensor", mb_bf[:, 0:width], sm_bc[:, 0:width],
             rb_bf[:, 0:width], ALU.mult, inc="sv", key=f"mr.{tag}")

    def p1ac_sync(E):
        for dst, srcap in ((c_bqk, bqk), (c_bv, bvd), (c_bo, bod),
                           (c_gs, gsd), (c_us, usd), (c_ds, dsd)):
            P.op(E, "dma_start", dst[:], srcap, inc="scst")
        if not P.emit:
            P.ms["p0.loads"] = P.cnt["scst"]

    def p1ac_gpsimd(E):
        xr = xt.rearrange("(j p) t -> p j t", p=128)
        for p in range(NPAN1):
            if p >= 2:
                P.wait(E, "sv", f"nrm1.{p-2}")
            P.op(E, "dma_start", xtf[p % 2][:],
                 xr[:, :, p * PW1:(p + 1) * PW1],
                 inc="gx1a" if p % 2 == 0 else "gx1b", key=f"x1.{p}")

    def p1ac_vector(E):
        P.wait(E, "scst", "p0.loads")
        P.op(E, "memset", ones128[:], 1.0)
        P.op(E, "memset", ones_r1b[:], 1.0)
        P.op(E, "memset", eps_t[:], EPS, inc="sv", key="p0.ready")
        for p in range(NPAN1):
            tag = f"p1.{p}"
            ln_stats_V(E, tag, width=PW1)
            for j in range(16):
                P.op(E, "tensor_tensor", tnrm[:, 0:PW1], xtf[p % 2][:, j, :],
                     rb_bf[:, 0:PW1], ALU.mult)
                P.op(E, "tensor_tensor", lnx[:, j, p * PW1:(p + 1) * PW1],
                     tnrm[:, 0:PW1], mb_bf[:, 0:PW1], ALU.subtract,
                     **({"inc": "sv", "key": f"nrm1.{p}"} if j == 15 else {}))

    def p1ac_tensor(E):
        for p in range(NPAN1):
            tag = f"p1.{p}"
            prev = f"p1.{p-1}" if p else None
            P.wait(E, "sc", f"sq1.{p}")
            ln_stats_T(E, tag, prev,
                       lambda j: xtb[:, j, :],
                       lambda j: xsq[:, j, :], width=PW1)

    def p1ac_scalar(E):
        P.wait(E, "sv", "p0.ready")
        for p in range(NPAN1):
            P.wait(E, "gx1a" if p % 2 == 0 else "gx1b", f"x1.{p}")
            if p >= 1:
                P.wait(E, "st", f"st.p1.{p-1}")
            P.op(E, "activation", xtb[:], xtf[p % 2][:], AF.Identity)
            P.op(E, "activation", xsq[:], xtf[p % 2][:], AF.Square,
                 inc="sc", key=f"sq1.{p}")
            ln_stats_S(E, f"p1.{p}", f"p1.{p-1}" if p else None, width=PW1,
                       prev_done=f"nrm1.{p-1}" if p else None)

    def p1d_gpsimd(E):
        barrier("b1", "gpsimd", E)
        wr = wqkv.rearrange("(j p) f -> p j f", p=128)
        def gidx(fc):
            return (fc + 1) * 4 - 1 if fc < 16 else 64 + (fc - 15) * 8 - 1
        for fc in range(K1FC):
            if fc >= 2:
                P.wait(E, "st", f"q.{gidx(fc-2)}")
            P.op(E, "dma_start", wqk_sb[fc % 2][:],
                 wr[:, :, fc * 128:(fc + 1) * 128],
                 inc=f"gw{fc % 2}", key=f"w.{fc}")
        for vf in range(4 if K1V else 0):
            if vf >= 1:
                P.wait(E, "st", f"v.{vf*32-1}")
            P.op(E, "dma_start", wv_sb[:],
                 wr[:, :, 2 * D + vf * 512:2 * D + (vf + 1) * 512],
                 inc="gwv", key=f"wv.{vf}")

    def p1d_tensor(E):
        barrier("b1", "tensor", E)
        gi = 0
        for fc in range(K1FC):
            P.wait(E, f"gw{fc % 2}", f"w.{fc}")
            nts = 4 if fc < 16 else 8
            for ts in range(nts):
                bank = pQ[gi % 2]
                if gi >= 2:
                    P.wait(E, "sv", f"qe.{gi-2}")
                for j in range(16):
                    P.op(E, "matmul", bank[:], wqk_sb[fc % 2][:, j, :],
                         lnx[:, j, ts * 512:(ts + 1) * 512],
                         start=(j == 0), stop=(j == 15),
                         **({"inc": "st", "key": f"q.{gi}"} if j == 15 else {}))
                gi += 1
        vi = 0
        for vf in range(4 if K1V else 0):
            P.wait(E, "gwv", f"wv.{vf}")
            for tc in range(32):
                bank = pQ[vi % 2]
                if vi >= 2:
                    P.wait(E, "sc", f"ve.{vi-2}")
                elif K1FC == 32:
                    P.wait(E, "sv", f"qe.{190+vi}")
                for j in range(16):
                    P.op(E, "matmul", bank[:],
                         lnx[:, j, tc * 128:(tc + 1) * 128], wv_sb[:, j, :],
                         start=(j == 0), stop=(j == 15),
                         **({"inc": "st", "key": f"v.{vi}"} if j == 15 else {}))
                vi += 1

    def p1d_vector(E):
        barrier("b1", "vector", E)
        gi = 0
        for fc in range(K1FC):
            nts = 4 if fc < 16 else 8
            for ts in range(nts):
                bank = pQ[gi % 2]
                P.wait(E, "st", f"q.{gi}")
                if fc < 16:
                    if fc >= 1 and ts == 0 and K1SP:
                        P.wait(E, "ssp", f"qsp.{fc-1}")
                    dst = qstage[:, ts, :]
                else:
                    if ts % 4 == 0 and not (fc == 16 and ts == 0) and K1SP:
                        P.wait(E, "ssp", f"ksp.{(fc-16)*2 + ts//4 - 1}")
                    dst = kstage[:, ts % 4, :]
                P.op(E, "tensor_scalar", dst, bank[:], c_bqk[:, fc:fc + 1],
                     None, ALU.add, inc="sv", key=f"qe.{gi}")
                gi += 1

    def p1d_scalar(E):
        barrier("b1", "scalar", E)
        vi = 0
        for vf in range(4 if K1V else 0):
            for tc in range(32):
                bank = pQ[vi % 2]
                P.wait(E, "st", f"v.{vi}")
                if vi >= 2:
                    P.wait(E, f"svs{vi % 2}", f"vsp.{vi-2}")
                P.op(E, "copy", vstage[vi % 2][:], bank[:], inc="sc",
                     key=f"ve.{vi}")
                vi += 1

    def p1d_sync(E):
        barrier("b1", "sync", E)
        gi = 0
        for fc in range(min(32, K1FC) if K1SP else 0):
            if fc < 16:
                gi += 4
                P.wait(E, "sv", f"qe.{gi-1}")
                P.op(E, "dma_start", qs[fc * 128:(fc + 1) * 128, :],
                     qstage.ap().rearrange("p a t -> p (a t)"),
                     inc="ssp", key=f"qsp.{fc}")
            else:
                for g2 in range(2):
                    gi += 4
                    P.wait(E, "sv", f"qe.{gi-1}")
                    P.op(E, "dma_start",
                         ks[(fc - 16) * 128:(fc - 15) * 128,
                            g2 * 2048:(g2 + 1) * 2048],
                         kstage.ap().rearrange("p a t -> p (a t)"),
                         inc="ssp", key=f"ksp.{(fc-16)*2+g2}")
        vi = 0
        for vf in range(4 if K1V else 0):
            for tc in range(32):
                P.wait(E, "sc", f"ve.{vi}")
                P.op(E, "dma_start",
                     vs[tc * 128:(tc + 1) * 128, vf * 512:(vf + 1) * 512],
                     vstage[vi % 2][:], inc=f"svs{vi % 2}", key=f"vsp.{vi}")
                vi += 1


    def k_iter_list(qt):
        its = []
        for j in range(4 * qt + 4):
            r = j - 4 * qt
            its.append((j * 128, j, r if r >= 0 else None, False))
        for j in range(4 * qt + 4):
            r = j - 4 * qt
            its.append((TOWN + j * 128, 16 + j, r if r >= 0 else None, True))
        return its

    def p2_sync(E):
        barrier("b2", "sync", E)
        P.op(E, "dma_start", c_tri[:], mtri, inc="scst")
        P.op(E, "dma_start", c_mrow[:], mrow, inc="scst", key="p2.consts")
        P.wait(E, "ssp", "ksp.31")
        P.wait(E, "svs0", "vsp.126")
        P.wait(E, "svs1", "vsp.127")
        vsr = vs.rearrange("(a p) f -> p a f", p=128)
        for h in range(NH):
            if h >= 2:
                P.wait(E, "st", f"az.{80*(h-2)+79}")
            P.op(E, "dma_start", kh[h % 2][:], ks[h * 128:(h + 1) * 128, :],
                 inc=f"shl{h % 2}")
            P.op(E, "dma_start", qh[h % 2][:], qs[h * 128:(h + 1) * 128, :],
                 inc=f"shl{h % 2}")
            P.op(E, "dma_start", vh[h % 2][:],
                 vsr[:, :, h * 128:(h + 1) * 128], inc=f"shl{h % 2}",
                 key=f"hl.{h}")

    def p2_tensor(E):
        barrier("b2", "tensor", E)
        P.wait(E, "scst", "p2.consts")
        for r in range(4):
            if r:
                P.wait(E, "sv", f"mbe.{r-1}")
            P.op(E, "matmul", pbc1[:], ones_r1b[:],
                 c_mrow[:, r * 512:(r + 1) * 512], start=True, stop=True,
                 inc="st", key=f"mbld.{r}")
        P.wait(E, "sv", "mbe.3")
        gi = 0
        for h in range(NH):
            P.wait(E, f"shl{h % 2}", f"hl.{h}")
            for qt in range(4):
                its = k_iter_list(qt)
                n = len(its)
                base = gi

                def s_mm(li):
                    g = base + li
                    col, vchunk, r, part = its[li]
                    c0 = 128 * (r or 0)
                    bank = pS3[g % 3]
                    if g >= 3:
                        P.wait(E, "sc", f"e.{g-3}")
                    P.op(E, "matmul", bank[:, c0:512],
                         kh[h % 2][:, col:col + 128],
                         qh[h % 2][:, qt * 512 + c0:(qt + 1) * 512],
                         start=True, stop=True, inc="st", key=f"s.{g}")

                def av_z(li):
                    g = base + li
                    col, vchunk, r, part = its[li]
                    c0 = 128 * (r or 0)
                    hq0 = h * 4 + qt
                    P.wait(E, "sc", f"e.{g}")
                    if li == 0 and hq0 >= 2:
                        P.wait(E, "sv", f"aoe.{hq0-2}")
                        P.wait(E, "sc", f"z.{hq0-2}")
                    P.op(E, "matmul", pavq[hq0 % 2][:, c0:512],
                         vh[h % 2][:, vchunk, :], p_sb[g % 3][:, c0:512],
                         start=(li == 0), stop=(li == n - 1))
                    P.op(E, "matmul", pzq[hq0 % 2][:, c0:512], ones128[:],
                         p_sb[g % 3][:, c0:512],
                         start=(li == 0), stop=(li == n - 1),
                         inc="st", key=f"az.{g}")

                s_mm(0)
                s_mm(1)
                for li in range(2, n):
                    s_mm(li)
                    av_z(li - 2)
                av_z(n - 2)
                av_z(n - 1)
                gi += n

    def p2_vector(E):
        barrier("b2", "vector", E)
        for r in range(4):
            P.wait(E, "st", f"mbld.{r}")
            P.op(E, "tensor_copy", mask_bc[:, r, :], pbc1[:],
                 inc="sv", key=f"mbe.{r}")
        gi = 0
        for h in range(NH):
            for qt in range(4):
                its = k_iter_list(qt)
                for li, (col, vchunk, r, part) in enumerate(its):
                    g = gi + li
                    if r is None:
                        continue
                    c0 = 128 * r
                    P.wait(E, "st", f"s.{g}")
                    bank = pS3[g % 3]
                    if not part:
                        P.op(E, "tensor_tensor", bank[:, c0:c0 + 128],
                             bank[:, c0:c0 + 128], c_tri[:], ALU.add,
                             inc="sv", key=f"m.{g}")
                    else:
                        P.op(E, "tensor_tensor", bank[:, c0:512],
                             bank[:, c0:512], mask_bc[:, r, c0:512],
                             ALU.add, inc="sv", key=f"m.{g}")
                gi += len(its)
                hq = h * 4 + qt
                P.wait(E, "sc", f"z.{hq}")
                P.op(E, "scalar_tensor_tensor", t_ao[:], rzb_sb[hq % 2][:],
                     1.0, pavq[hq % 2][:], ALU.mult, ALU.mult)
                P.op(E, "tensor_scalar", ao[:, h, qt * 512:(qt + 1) * 512],
                     t_ao[:], c_bv[:, h:h + 1], None, ALU.add,
                     inc="sv", key=f"aoe.{hq}")

    def p2_scalar(E):
        barrier("b2", "scalar", E)
        gi = 0
        for h in range(NH):
            for qt in range(4):
                its = k_iter_list(qt)
                n = len(its)
                for li, (col, vchunk, r, part) in enumerate(its):
                    g = gi + li
                    c0 = 128 * (r or 0)
                    if r is not None:
                        P.wait(E, "sv", f"m.{g}")
                    else:
                        P.wait(E, "st", f"s.{g}")
                    if g >= 3:
                        P.wait(E, "st", f"az.{g-3}")
                    P.op(E, "activation", p_sb[g % 3][:, c0:512],
                         pS3[g % 3][:, c0:512], AF.Exp, scale=EXP_SCALE,
                         inc="sc", key=f"e.{g}")
                gi += n
                hq = h * 4 + qt
                P.wait(E, "st", f"az.{gi-1}")
                if hq >= 2:
                    P.wait(E, "sv", f"aoe.{hq-2}")
                P.op(E, "activation", zl128[:], pzq[hq % 2][:], AF.Ln)
                P.op(E, "activation", rzb_sb[hq % 2][:], zl128[:], AF.Exp,
                     scale=-1.0, inc="sc", key=f"z.{hq}")

    pQ4 = [pA, pB, pav, pbc2]

    def p3_gpsimd(E):
        barrier("b3", "gpsimd", E)
        for ts in range(4):
            for oc in range(16):
                gg = ts * 16 + oc
                if gg >= 2:
                    P.wait(E, "st", f"y.{gg-2}")
                P.op(E, "dma_start", wo_sb[gg % 2][:],
                     wod[oc], inc=f"gwo{gg % 2}",
                     key=f"wol.{gg}")

    def p3_tensor(E):
        barrier("b3", "tensor", E)
        prev = [None]

        def p3_stats_l2(ts):
            sl = slice(ts * 512, (ts + 1) * 512)
            P.wait(E, "sv", f"sql2.{ts}")
            ln_stats_T(E, f"l2.{ts}", prev[0],
                       lambda j: yb[:, j, sl], lambda j: l3t[:, j, :])
            prev[0] = f"l2.{ts}"

        def p3_stats_l3(ts):
            P.wait(E, "sv", f"sql3.{ts}")
            ln_stats_T(E, f"l3.{ts}", prev[0],
                       lambda j: x2b[:, j, :], lambda j: l3t[:, j, :])
            prev[0] = f"l3.{ts}"

        for ts in range(4):
            sl = slice(ts * 512, (ts + 1) * 512)
            for oc in range(16):
                gg = ts * 16 + oc
                bank = pQ4[gg % 4]
                P.wait(E, f"gwo{gg % 2}", f"wol.{gg}")
                if gg >= 4:
                    P.wait(E, "sv", f"ye.{gg-4}")
                for j in range(16):
                    P.op(E, "matmul", bank[:], wo_sb[gg % 2][:, j, :],
                         ao[:, j, sl],
                         start=(j == 0), stop=(j == 15),
                         **({"inc": "st", "key": f"y.{gg}"} if j == 15 else {}))
                if oc == 7 and ts >= 1:
                    p3_stats_l2(ts - 1)
            if ts >= 1:
                p3_stats_l3(ts - 1)
        p3_stats_l2(3)
        p3_stats_l3(3)

    def p3_scalar(E):
        barrier("b3", "scalar", E)
        prev = None
        pdone = None
        for ts in range(4):
            ln_stats_S(E, f"l2.{ts}", prev, prev_done=pdone)
            prev = f"l2.{ts}"
            pdone = f"x2.{ts}.3"
            ln_stats_S(E, f"l3.{ts}", prev, prev_done=pdone)
            prev = f"l3.{ts}"
            pdone = f"l3n.{ts}"

    def p3_vector(E):
        barrier("b3", "vector", E)

        def p3_sql2(ts):
            sl = slice(ts * 512, (ts + 1) * 512)
            if ts >= 1:
                P.wait(E, "sl3w", f"l3sp.{ts-1}")
            for j in range(16):
                P.op(E, "tensor_tensor", l3t[:, j, :], yb[:, j, sl],
                     yb[:, j, sl], ALU.mult,
                     **({"inc": "sv", "key": f"sql2.{ts}"} if j == 15 else {}))

        def p3_rest(ts):
            sl = slice(ts * 512, (ts + 1) * 512)
            ln_stats_V(E, f"l2.{ts}")
            P.wait(E, "st", f"st.l2.{ts}")
            if ts >= 1:
                P.wait(E, "sx2w", f"x2w.{ts-1}")
            for qtr in range(4):
                P.wait(E, "sxo", f"xo.{ts}.{qtr}")
                for j4 in range(4):
                    j = qtr * 4 + j4
                    P.op(E, "tensor_tensor", tnrm[:], yb[:, j, sl],
                         rb_bf[:], ALU.mult)
                    P.op(E, "tensor_tensor", tnrm[:], tnrm[:], mb_bf[:],
                         ALU.subtract)
                    P.op(E, "tensor_tensor", x2b[:, j, :], xo_h[:, j4, :],
                         tnrm[:], ALU.add,
                         **({"inc": "sv", "key": f"x2.{ts}.{qtr}"}
                            if j4 == 3 else {}))
            for j in range(16):
                P.op(E, "tensor_tensor", l3t[:, j, :], x2b[:, j, :],
                     x2b[:, j, :], ALU.mult,
                     **({"inc": "sv", "key": f"sql3.{ts}"} if j == 15 else {}))
            ln_stats_V(E, f"l3.{ts}")
            P.wait(E, "st", f"st.l3.{ts}")
            P.wait(E, "sx2w", f"x2w.{ts}")
            for j in range(16):
                P.op(E, "tensor_tensor", tnrm[:], x2b[:, j, :], rb_bf[:],
                     ALU.mult)
                P.op(E, "tensor_tensor", l3t[:, j, :], tnrm[:], mb_bf[:],
                     ALU.subtract,
                     **({"inc": "sv", "key": f"l3n.{ts}"} if j == 15 else {}))

        for ts in range(4):
            sl = slice(ts * 512, (ts + 1) * 512)
            if ts >= 1:
                p3_sql2(ts - 1)
            for oc in range(16):
                gg = ts * 16 + oc
                bank = pQ4[gg % 4]
                P.wait(E, "st", f"y.{gg}")
                P.op(E, "tensor_scalar", yb[:, oc, sl],
                     bank[:], c_bo[:, oc:oc + 1], None, ALU.add,
                     inc="sv", key=f"ye.{gg}")
            if ts >= 1:
                p3_rest(ts - 1)
        p3_sql2(3)
        p3_rest(3)

    def p3_sync(E):
        barrier("b3", "sync", E)
        xr = xt.rearrange("(j p) t -> p j t", p=128)
        x2r = x2s.rearrange("(j p) t -> p j t", p=128)
        l3r = l3s.rearrange("(j p) t -> p j t", p=128)
        for ts in range(4):
            sl = slice(ts * 512, (ts + 1) * 512)
            for qtr in range(4):
                if ts >= 1 and qtr == 0:
                    P.wait(E, "sv", f"x2.{ts-1}.3")
                if qtr >= 1:
                    P.wait(E, "sv", f"x2.{ts}.{qtr-1}")
                P.op(E, "dma_start", xo_h[:],
                     xr[:, qtr * 4:(qtr + 1) * 4, sl],
                     inc="sxo", key=f"xo.{ts}.{qtr}")
            P.wait(E, "sv", f"x2.{ts}.3")
            P.op(E, "dma_start", x2r[:, :, sl], x2b[:], inc="sx2w",
                 key=f"x2w.{ts}")
            P.wait(E, "sv", f"l3n.{ts}")
            P.op(E, "dma_start", l3r[:, :, sl], l3t[:], inc="sl3w",
                 key=f"l3sp.{ts}")

    def wait_gate_done(E, grp):
        P.wait(E, "st", f"u.{grp*128+127}")
        P.wait(E, "sv", f"hh.{grp*128+127}")

    def wait_down_done(E, grp):
        P.wait(E, "sv", f"x3j.{grp}.1.1.7")
        P.wait(E, "st", f"d.{grp*32+31}")
        P.wait(E, "sod0", f"od.{grp}.1.1.6")
        P.wait(E, "sod1", f"od.{grp}.1.1.7")

    def odkey(seq):
        g, r = divmod(seq, 32)
        t, r2 = divmod(r, 16)
        hf, j8 = divmod(r2, 8)
        return f"od.{g}.{t}.{hf}.{j8}"

    def p4_gpsimd(E):
        barrier("b4", "gpsimd", E)
        wgr = wgt.rearrange("(j p) i -> p j i", p=128)
        wur = wut.rearrange("(j p) i -> p j i", p=128)
        wdr = wdt.rearrange("(m p) o -> p m o", p=128)
        for grp in range(2):
            for ic in range(64):
                gg = grp * 64 + ic
                if grp >= 1 and ic < 2:
                    wait_down_done(E, grp - 1)
                if gg >= 2:
                    P.wait(E, "st", f"u.{(gg-2)*2+1}")
                P.op(E, "dma_start", wg_sb[ic % 2][:],
                     wgr[:, :, ic * 128:(ic + 1) * 128], inc=f"ggu{gg % 2}")
                P.op(E, "dma_start", wu_sb[ic % 2][:],
                     wur[:, :, ic * 128:(ic + 1) * 128],
                     inc=f"ggu{gg % 2}", key=f"wgu.{gg}")
            hi = 0
            for ts in range(2):
                for oc in range(16):
                    for half in range(2):
                        gg2 = grp * 64 + hi
                        if hi < 2:
                            wait_gate_done(E, grp)
                        if gg2 >= 2:
                            pv = gg2 - 2
                            phi = pv % 64
                            pgrp = pv // 64
                            pts, prr = divmod(phi, 32)
                            poc, phalf = divmod(prr, 2)
                            pkey = (f"d.{pgrp*32+pts*16+poc}" if phalf == 1
                                    else f"dmmA.{pv}")
                            P.wait(E, "st", pkey)
                        P.op(E, "dma_start", wd_sb[hi % 2][:],
                             wdr[:, half * 32:(half + 1) * 32,
                                 oc * 128:(oc + 1) * 128],
                             inc=f"gwd{gg2 % 2}", key=f"wd.{gg2}")
                        hi += 1

    def p4_sync(E):
        barrier("b4", "sync", E)
        P.wait(E, "sx2w", "x2w.3")
        P.wait(E, "sl3w", "l3sp.3")
        l3r = l3s.rearrange("(j p) t -> p j t", p=128)
        x2r = x2s.rearrange("(j p) t -> p j t", p=128)
        orr = out.rearrange("(j p) t -> p j t", p=128)
        for grp in range(2):
            gsl = slice(grp * 1024, (grp + 1) * 1024)
            if grp >= 1:
                P.wait(E, "st", f"u.{(grp-1)*128+127}")
                wait_down_done(E, grp - 1)
            P.op(E, "dma_start", l3_sb[:], l3r[:, :, gsl], inc="sl3l",
                 key=f"l3l.{grp}")
            for ts in range(2):
                t0 = grp * 1024 + ts * 512
                for half in range(2):
                    seq = (grp * 2 + ts) * 2 + half
                    if ts == 0 and half == 0:
                        wait_gate_done(E, grp)
                    if seq >= 1:
                        g_, r_ = divmod(seq - 1, 4)
                        t_, hf_ = divmod(r_, 2)
                        P.wait(E, "sv", f"x3j.{g_}.{t_}.{hf_}.7")
                    P.op(E, "dma_start", x2h[:],
                         x2r[:, half * 8:(half + 1) * 8, t0:t0 + 512],
                         inc="sx2l", key=f"x2l.{grp}.{ts}.{half}")
                    for j8 in range(8):
                        P.wait(E, "sv", f"x3j.{grp}.{ts}.{half}.{j8}")
                        P.op(E, "dma_start",
                             orr[:, half * 8 + j8, t0:t0 + 512],
                             x3b[j8 % 2][:], inc=f"sod{j8 % 2}",
                             key=f"od.{grp}.{ts}.{half}.{j8}")
        if KDBG:
            P.op(E, "dma_start", dbg, x2s, inc="sx2w")
        if not P.emit:
            P.ms["end.sod0"] = P.cnt["sod0"]
            P.ms["end.sod1"] = P.cnt["sod1"]
        if P.emit:
            E.wait_ge(P.sems["sod0"], P.ms["end.sod0"])
            E.wait_ge(P.sems["sod1"], P.ms["end.sod1"])

    def p4_tensor(E):
        barrier("b4", "tensor", E)
        for grp in range(2):
            P.wait(E, "sl3l", f"l3l.{grp}")
            gi = 0
            for ic in range(64):
                gg = grp * 64 + ic
                P.wait(E, f"ggu{gg % 2}", f"wgu.{gg}")
                for ts in range(2):
                    g2 = grp * 128 + gi
                    if g2 >= 1:
                        P.wait(E, "sc", f"sl.{g2-1}")
                    for j in range(16):
                        P.op(E, "matmul", pA[:], wg_sb[ic % 2][:, j, :],
                             l3_sb[:, j, ts * 512:(ts + 1) * 512],
                             start=(j == 0), stop=(j == 15),
                             **({"inc": "st", "key": f"g.{g2}"}
                                if j == 15 else {}))
                    if g2 >= 1:
                        P.wait(E, "sv", f"he1.{g2-1}")
                    for j in range(16):
                        P.op(E, "matmul", pB[:], wu_sb[ic % 2][:, j, :],
                             l3_sb[:, j, ts * 512:(ts + 1) * 512],
                             start=(j == 0), stop=(j == 15),
                             **({"inc": "st", "key": f"u.{g2}"}
                                if j == 15 else {}))
                    gi += 1
            for ts in range(2):
                P.wait(E, "sv", f"hh.{grp*128+126+ts}")
                if grp * 2 + ts > 0:
                    pg, pt = divmod(grp * 2 + ts - 1, 2)
                    P.wait(E, "sc", f"rsbe.{pg}.{pt}")
                for m in range(64):
                    P.op(E, "matmul", pz[:], ones128[:],
                         hact[:, m, ts * 512:(ts + 1) * 512],
                         start=(m == 0), stop=(m == 63),
                         **({"inc": "st", "key": f"rs.{grp}.{ts}"}
                            if m == 63 else {}))
            hi = 0
            for ts in range(2):
                tag = f"ln4.{grp}.{ts}"
                prev = (f"ln4.{grp}.{ts-1}" if ts else
                        (f"ln4.{grp-1}.1" if grp else None))

                def lnmm(j):
                    if j == 0 and prev is not None:
                        P.wait(E, "sc", f"mq.{prev}")
                    P.wait(E, "sv", f"fsq.{grp}.{ts}.{j}")
                    P.op(E, "matmul", ps1[:], ones128[:], ffnb[:, j, :],
                         start=(j == 0), stop=(j == 15))
                    P.op(E, "matmul", ps2[:], ones128[:], sqtmp[j % 2][:],
                         start=(j == 0), stop=(j == 15),
                         **({"inc": "st", "key": f"st.{tag}"} if j == 15
                            else {"inc": "st", "key": f"lnmm.{grp}.{ts}.{j}"}))

                for oc in range(16):
                    dd = grp * 32 + ts * 16 + oc
                    for half in range(2):
                        gg2 = grp * 64 + hi
                        P.wait(E, f"gwd{gg2 % 2}", f"wd.{gg2}")
                        if half == 0 and dd >= 2:
                            P.wait(E, "sv", f"de.{dd-2}")
                        mk_ = ({"inc": "st", "key": f"d.{dd}"} if half == 1
                               else {"inc": "st", "key": f"dmmA.{gg2}"})
                        for m in range(32):
                            P.op(E, "matmul", pQ[dd % 2][:],
                                 wd_sb[hi % 2][:, m, :],
                                 hact[:, half * 32 + m,
                                      ts * 512:(ts + 1) * 512],
                                 start=(half == 0 and m == 0),
                                 stop=(half == 1 and m == 31),
                                 **(mk_ if m == 31 else {}))
                        hi += 1
                    if oc >= 2:
                        lnmm(oc - 2)
                lnmm(14)
                lnmm(15)


    def p4_scalar(E):
        barrier("b4", "scalar", E)
        for grp in range(2):
            gi = 0
            for ic in range(64):
                for ts in range(2):
                    g2 = grp * 128 + gi
                    P.wait(E, "st", f"g.{g2}")
                    if g2 >= 1:
                        P.wait(E, "sv", f"hh.{g2-1}")
                    if g2 >= 1:
                        P.wait(E, "sv", f"he1.{g2-1}")
                    P.op(E, "activation", silu_sb[:], pA[:], AF.Silu,
                         scale=c_gs[:, ic:ic + 1], inc="sc", key=f"sl.{g2}")
                    gi += 1
            for ts in range(2):
                P.wait(E, "st", f"rs.{grp}.{ts}")
                if grp * 2 + ts >= 2:
                    pg, pt = divmod(grp * 2 + ts - 2, 2)
                    P.wait(E, "sv", f"de.{pg*32+pt*16+15}")
                P.op(E, "activation", rsb_sb[ts][:], pz[:],
                     AF.Identity, scale=-128.0, inc="sc",
                     key=f"rsbe.{grp}.{ts}")
            for ts in range(2):
                tag = f"ln4.{grp}.{ts}"
                prev = (f"ln4.{grp}.{ts-1}" if ts else
                        (f"ln4.{grp-1}.1" if grp else None))
                pdone = (f"x3j.{grp}.0.1.7" if ts else
                         (f"x3j.{grp-1}.1.1.7" if grp else None))
                ln_stats_S(E, tag, prev, prev_done=pdone)

    def p4_vector(E):
        barrier("b4", "vector", E)
        for grp in range(2):
            gi = 0
            for ic in range(64):
                for ts in range(2):
                    g2 = grp * 128 + gi
                    P.wait(E, "sc", f"sl.{g2}")
                    P.wait(E, "st", f"u.{g2}")
                    P.op(E, "tensor_tensor", tnrm[:], silu_sb[:], pB[:],
                         ALU.mult, inc="sv", key=f"he1.{g2}")
                    P.op(E, "tensor_scalar",
                         hact[:, ic, ts * 512:(ts + 1) * 512],
                         tnrm[:], c_us[:, ic:ic + 1], None, ALU.mult,
                         inc="sv", key=f"hh.{g2}")
                    gi += 1
            for ts in range(2):
                tag = f"ln4.{grp}.{ts}"
                P.wait(E, "sc", f"rsbe.{grp}.{ts}")
                for oc in range(16):
                    dd = grp * 32 + ts * 16 + oc
                    P.wait(E, "st", f"d.{dd}")
                    P.op(E, "scalar_tensor_tensor", t_ao[:], rsb_sb[ts][:],
                         1.0, pQ[dd % 2][:], ALU.mult, ALU.add)
                    P.op(E, "tensor_scalar", ffnb[:, oc, :], t_ao[:],
                         c_ds[:, oc:oc + 1], None, ALU.mult,
                         inc="sv", key=f"de.{dd}")
                    if oc >= 2:
                        P.wait(E, "st", f"lnmm.{grp}.{ts}.{oc-2}")
                    P.op(E, "tensor_tensor", sqtmp[oc % 2][:], ffnb[:, oc, :],
                         ffnb[:, oc, :], ALU.mult, inc="sv",
                         key=f"fsq.{grp}.{ts}.{oc}")
                ln_stats_V(E, tag)
                for half in range(2):
                    P.wait(E, "sx2l", f"x2l.{grp}.{ts}.{half}")
                    for j8 in range(8):
                        j = half * 8 + j8
                        seq = ((grp * 2 + ts) * 2 + half) * 8 + j8
                        if seq >= 2:
                            P.wait(E, f"sod{j8 % 2}", odkey(seq - 2))
                        P.op(E, "tensor_tensor", tnrm[:], ffnb[:, j, :],
                             rb_bf[:], ALU.mult)
                        P.op(E, "tensor_tensor", tnrm[:], tnrm[:], mb_bf[:],
                             ALU.subtract)
                        P.op(E, "tensor_tensor", x3b[j8 % 2][:],
                             x2h[:, j8, :], tnrm[:], ALU.add,
                             inc="sv", key=f"x3j.{grp}.{ts}.{half}.{j8}")

    def p1d_tensor_wrap(E):
        p1ac_tensor(E)
        p1d_tensor(E)

    phase_tbl = {
        "sync": [p1ac_sync, p1d_sync, p2_sync, p3_sync, p4_sync],
        "gpsimd": [p1ac_gpsimd, p1d_gpsimd,
                   lambda E: barrier("b2", "gpsimd", E),
                   p3_gpsimd, p4_gpsimd],
        "tensor": [p1d_tensor_wrap, p2_tensor, p3_tensor, p4_tensor],
        "vector": [p1ac_vector, p1d_vector, p2_vector, p3_vector, p4_vector],
        "scalar": [p1ac_scalar, p1d_scalar, p2_scalar, p3_scalar, p4_scalar],
    }
    import os
    _L = int(os.environ.get("KPH", "4"))
    if _L == 0:
        phase_tbl = {"sync": [p1ac_sync], "gpsimd": [p1ac_gpsimd],
                     "tensor": [p1ac_tensor], "vector": [p1ac_vector],
                     "scalar": [p1ac_scalar]}
        if _os.environ.get("KB1", "0") == "1":
            for e in ENGS:
                phase_tbl[e] = phase_tbl[e] + [
                    (lambda ee: lambda E: barrier("b1", ee, E))(e)]
    elif _L < 4:
        cut = {"sync": 1 + _L, "gpsimd": 1 + _L, "tensor": _L,
               "vector": 1 + _L, "scalar": 1 + _L}
        phase_tbl = {k: v[:cut[k]] for k, v in phase_tbl.items()}

    sem_names = (["st", "sv", "sc", "sy", "sgb"] + SYNC_SEMS + GP_SEMS)
    sem_guards = []
    for sn in sem_names:
        g = nc.semaphore("sem_" + sn)
        P.sems[sn] = g.__enter__()
        sem_guards.append(g)
    with nc.Block() as block:
        P.start(False)
        for eng in ENGS:
            for fn in phase_tbl[eng]:
                fn(None)
        totals = dict(P.cnt)
        P.start(True)

        @block.sync
        def _(sync):
            for fn in phase_tbl["sync"]:
                fn(sync)

        @block.gpsimd
        def _(g):
            for fn in phase_tbl["gpsimd"]:
                fn(g)

        @block.tensor
        def _(t):
            for fn in phase_tbl["tensor"]:
                fn(t)

        @block.vector
        def _(v):
            for fn in phase_tbl["vector"]:
                fn(v)

        @block.scalar
        def _(s):
            for fn in phase_tbl["scalar"]:
                fn(s)

        for k in totals:
            assert P.cnt[k] == totals[k], (k, P.cnt[k], totals[k])
    return nc


_NC_CACHE = []


def kernel(**inputs):
    x = np.ascontiguousarray(np.asarray(inputs["L0_attn_x"], np.float32))
    Wqkv = np.ascontiguousarray(np.asarray(inputs["L0_attn_Wqkv"], np.float32))
    bqkv = np.asarray(inputs["L0_attn_bqkv"], np.float32)
    Wo = np.ascontiguousarray(
        np.asarray(inputs["L0_attn_Wo"], np.float32)
        .reshape(16, 128, 16, 128).transpose(2, 1, 0, 3))
    bo = np.asarray(inputs["L0_attn_bo"], np.float32)
    gw = np.asarray(inputs["L0_ffn_gate_w"], np.int32)
    gsc = np.asarray(inputs["L0_ffn_gate_scale"], np.float32)
    uw = np.asarray(inputs["L0_ffn_up_w"], np.int32)
    usc = np.asarray(inputs["L0_ffn_up_scale"], np.float32)
    dw = np.asarray(inputs["L0_ffn_down_w"], np.int32)
    dsc = np.asarray(inputs["L0_ffn_down_scale"], np.float32)

    wgt = np.ascontiguousarray(gw.T)
    wut = np.ascontiguousarray(uw.T)
    wdt = np.ascontiguousarray(dw.T)
    bqk = np.ascontiguousarray(bqkv[:2 * D].reshape(32, 128).T)
    bv_ = np.ascontiguousarray(bqkv[2 * D:].reshape(16, 128).T)
    bo_ = np.ascontiguousarray(bo.reshape(16, 128).T)
    gs_ = np.ascontiguousarray(gsc.reshape(64, 128).T)
    us_ = np.ascontiguousarray(usc.reshape(64, 128).T)
    ds_ = np.ascontiguousarray(dsc.reshape(16, 128).T)
    ii = np.arange(128)
    mtri = np.ascontiguousarray(
        np.where(ii[:, None] > ii[None, :], np.float32(MASK),
                 np.float32(0.0)).astype(np.float32))

    if not _NC_CACHE:
        _NC_CACHE.append(build_nc())
    nc = _NC_CACHE[0]
    in_maps = []
    perms = []
    for c in range(8):
        b, hg = divmod(c, 2)
        own = np.concatenate([np.arange(g * 128, (g + 1) * 128)
                              for g in range(hg, 32, 2)])
        par = np.concatenate([np.arange(g * 128, (g + 1) * 128)
                              for g in range(1 - hg, 32, 2)])
        perm = np.concatenate([own, par])
        perms.append(own)
        xt_c = np.ascontiguousarray(x[b].T[:, perm])
        mr = np.zeros((4, 512), np.float32)
        for r in range(4):
            mr[r, : (r + 1 - hg) * 128] = MASK
        mrow_c = np.ascontiguousarray(
            mr.reshape(1, 4 * 512)).astype(ml_dtypes.bfloat16)
        in_maps.append({
            "xt": xt_c, "wqkv": Wqkv, "bqk": bqk, "bv": bv_, "wo": Wo,
            "bo": bo_, "wgt": wgt, "wut": wut, "wdt": wdt, "gs": gs_,
            "us": us_, "ds": ds_, "mtri": mtri, "mrow": mrow_c,
        })

    res = run_bass_kernel_spmd(nc, in_maps, core_ids=list(range(8)))
    global _LAST_RES
    _LAST_RES = res.results
    outp = np.empty_like(x)
    for c in range(8):
        b = c // 2
        outp[b, perms[c], :] = res.results[c]["o"].T
    return outp


# version 15 (final)
# speedup vs baseline: 1.0700x; 1.0454x over previous
import numpy as np
import ml_dtypes

import concourse.bass as bass
import concourse.mybir as mybir
from concourse.bass_utils import run_bass_kernel_spmd

F32 = mybir.dt.float32
BF16 = mybir.dt.bfloat16
I32 = mybir.dt.int32
AF = mybir.ActivationFunctionType
ALU = mybir.AluOpType

D = 2048
S = 4096
TOWN = 2048
NH = 16
FI = 8192
EXP_SCALE = 1.0 / float(np.sqrt(128.0))
MASK = -30000.0
EPS = 1e-5
NPAN1 = 16
PW1 = 256

ENGS = ("sync", "gpsimd", "tensor", "vector", "scalar")
SEM_OF = {"sync": "sy", "gpsimd": "sgb", "tensor": "st", "vector": "sv",
          "scalar": "sc"}
SYNC_SEMS = ["scst", "ssp", "svs0", "svs1", "shl0", "shl1", "sxo", "sx2w",
             "sl3w", "sl3l", "sx2l", "sod0", "sod1"]
GP_SEMS = ["gx1a", "gx1b", "gw0", "gw1", "gwv", "gwo0", "gwo1", "ggu0",
           "ggu1", "gwd0", "gwd1"]


class Prog:
    def __init__(self):
        self.ms = {}
        self.sems = {}
        self.emit = False
        self.cnt = {}
        self.mm_mode = None

    def start(self, emit):
        import collections
        self.emit = emit
        self.cnt = collections.defaultdict(int)
        self.mm_mode = None

    def op(self, E, name, *args, inc=None, key=None, **kw):
        inst = getattr(E, name)(*args, **kw) if self.emit else None
        if inc is not None:
            amt = 1 if inc in ("st", "sv", "sc", "sy", "sgb") else 16
            self.cnt[inc] += amt
            if key is not None and not self.emit:
                assert key not in self.ms, f"dup {key}"
                self.ms[key] = self.cnt[inc]
            if self.emit and inst is not None:
                inst.then_inc(self.sems[inc], amt)
        return inst

    def wait(self, E, sem, key):
        if key is None:
            return
        if self.emit:
            assert key in self.ms, f"missing milestone {key}"
            E.wait_ge(self.sems[sem], self.ms[key])


def build_nc():
    import os as _os
    K1V = _os.environ.get("K1V", "1") == "1"
    K1SP = _os.environ.get("K1SP", "1") == "1"
    K1FC = int(_os.environ.get("K1FC", "32"))
    global NPAN1
    NPAN1 = int(_os.environ.get("KNP", "16"))
    nc = bass.Bass("TRN2", target_bir_lowering=False,
                   detect_race_conditions=False)

    xt = nc.dram_tensor("xt", [D, S], F32, kind="ExternalInput").ap()
    wqkv = nc.dram_tensor("wqkv", [D, 3 * D], F32, kind="ExternalInput").ap()
    bqk = nc.dram_tensor("bqk", [128, 32], F32, kind="ExternalInput").ap()
    bvd = nc.dram_tensor("bv", [128, 16], F32, kind="ExternalInput").ap()
    wod = nc.dram_tensor("wo", [16, 128, 16, 128], F32,
                         kind="ExternalInput").ap()
    bod = nc.dram_tensor("bo", [128, 16], F32, kind="ExternalInput").ap()
    wgt = nc.dram_tensor("wgt", [64, 128, 16, 128], I32,
                         kind="ExternalInput").ap()
    wut = nc.dram_tensor("wut", [64, 128, 16, 128], I32,
                         kind="ExternalInput").ap()
    wdt = nc.dram_tensor("wdt", [16, 128, 64, 128], I32,
                         kind="ExternalInput").ap()
    gsd = nc.dram_tensor("gs", [128, 64], F32, kind="ExternalInput").ap()
    usd = nc.dram_tensor("us", [128, 64], F32, kind="ExternalInput").ap()
    dsd = nc.dram_tensor("ds", [128, 16], F32, kind="ExternalInput").ap()
    mtri = nc.dram_tensor("mtri", [128, 128], F32, kind="ExternalInput").ap()
    mrow = nc.dram_tensor("mrow", [1, 4 * 512], BF16,
                          kind="ExternalInput").ap()
    out = nc.dram_tensor("o", [D, TOWN], F32, kind="ExternalOutput").ap()
    qs = nc.dram_tensor("qs", [D, TOWN], BF16, kind="Internal").ap()
    ks = nc.dram_tensor("ks", [D, S], BF16, kind="Internal").ap()
    vs = nc.dram_tensor("vs", [S, D], BF16, kind="Internal").ap()
    x2s = nc.dram_tensor("x2s", [D, TOWN], BF16, kind="Internal").ap()
    KDBG = _os.environ.get("KDBG", "0") == "1"
    dbg = (nc.dram_tensor("dbg", [D, TOWN], BF16, kind="ExternalOutput").ap()
           if KDBG else None)
    dbga = (nc.dram_tensor("dbga", [128, NH * TOWN], BF16,
                           kind="ExternalOutput").ap() if KDBG else None)
    l3s = nc.dram_tensor("l3s", [D, TOWN], BF16, kind="Internal").ap()

    P = Prog()
    P.sems = {}
    guards = []

    def mk(name, shape, dt, kind="sbuf"):
        g = (nc.sbuf_tensor if kind == "sbuf" else nc.psum_tensor)(
            name, shape, dt)
        h = g.__enter__()
        guards.append(g)
        return h

    scoped = {}

    def salloc(name, shape, dt):
        g = nc.sbuf_tensor(name, shape, dt)
        h = g.__enter__()
        scoped[name] = g
        return h

    def sfree(*names):
        for n in names:
            scoped.pop(n).__exit__(None, None, None)

    c_bqk = mk("c_bqk", [128, 32], F32)
    c_bv = mk("c_bv", [128, 16], F32)
    c_bo = mk("c_bo", [128, 16], F32)
    c_gs = mk("c_gs", [128, 64], F32)
    c_us = mk("c_us", [128, 64], F32)
    c_ds = mk("c_ds", [128, 16], F32)
    ones_k = mk("ones_k", [128, 1], BF16)
    ones_r1b = mk("ones_r1b", [1, 128], BF16)
    ones_r1f = mk("ones_r1f", [1, 128], F32)
    neg128_r1f = mk("neg128_r1f", [1, 128], F32)
    eps_t = mk("eps_t", [128, 1], F32)
    ones128 = mk("ones128", [128, 128], BF16)
    sm_bc = mk("sm_bc", [128, 512], F32)
    sq_bc = mk("sq_bc", [128, 512], F32)
    st_bc = mk("st_bc", [128, 512], F32)
    sv_bc = mk("sv_bc", [128, 512], F32)
    rb_bf = mk("rb_bf", [128, 512], BF16)
    mb_bf = mk("mb_bf", [128, 512], BF16)
    t_ao = mk("t_ao", [128, 512], F32)
    tnrm = mk("tnrm", [128, 512], F32)
    sqtmp = [mk(f"sqtmp{i}", [128, 512], BF16) for i in range(2)]

    pA = mk("pA", [128, 512], F32, "psum")
    pB = mk("pB", [128, 512], F32, "psum")
    pav = mk("pav", [128, 512], F32, "psum")
    pbc1 = mk("pbc1", [128, 512], F32, "psum")
    pbc2 = mk("pbc2", [128, 512], F32, "psum")
    pz = mk("pz", [128, 512], F32, "psum")
    ps1 = mk("ps1", [128, 512], F32, "psum")
    ps2 = mk("ps2", [128, 512], F32, "psum")
    pQ = [pA, pB]
    pavq = [pav, pbc2]
    pzq = [pz, ps1]
    pS3 = [pA, pB, pbc1]

    lnx = salloc("lnx", [128, NH, S], BF16)
    xtf = [salloc(f"xtf{i}", [128, 16, PW1], F32) for i in range(2)]
    xtb = salloc("xtb", [128, 16, PW1], BF16)
    xsq = salloc("xsq", [128, 16, PW1], BF16)
    sfree("xsq", "xtb", "xtf1", "xtf0")
    wqk_sb = [salloc(f"wqk{i}", [128, 16, 128], BF16) for i in range(2)]
    wv_sb = salloc("wv_sb", [128, 16, 512], BF16)
    qstage = salloc("qstage", [128, 4, 512], BF16)
    kstage = salloc("kstage", [128, 4, 512], BF16)
    vstage = [salloc(f"vstage{i}", [128, 512], BF16) for i in range(2)]
    sfree("vstage1", "vstage0", "kstage", "qstage", "wv_sb",
          "wqk1", "wqk0", "lnx")
    ao = salloc("ao", [128, NH, TOWN], BF16)
    kh = [salloc(f"kh{i}", [128, S], BF16) for i in range(2)]
    zl128 = salloc("zl128", [128, 512], F32)
    mask_bc = salloc("mask_bc", [128, 4, 512], BF16)
    qh = [salloc(f"qh{i}", [128, TOWN], BF16) for i in range(2)]
    vh = [salloc(f"vh{i}", [128, 32, 128], BF16) for i in range(2)]
    p_sb = [salloc(f"p_sb{i}", [128, 512], BF16) for i in range(3)]
    rzb_sb = [salloc(f"rzb_sb{i}", [128, 512], BF16) for i in range(2)]
    c_tri = salloc("c_tri", [128, 128], F32)
    c_mrow = salloc("c_mrow", [1, 4 * 512], BF16)
    sfree("c_mrow", "c_tri", "rzb_sb1", "rzb_sb0",
          "p_sb2", "p_sb1", "p_sb0", "vh1", "vh0", "qh1", "qh0",
          "mask_bc", "zl128", "kh1", "kh0")
    yb = salloc("yb", [128, NH, TOWN], BF16)
    wo_sb = [salloc(f"wo{i}", [128, 16, 128], BF16) for i in range(2)]
    xo_h = salloc("xo_h", [128, 4, 512], F32)
    x2b = salloc("x2b", [128, 16, 512], BF16)
    l3t = salloc("l3t", [128, 16, 512], BF16)
    sfree("l3t", "x2b", "xo_h", "wo1", "wo0", "yb", "ao")
    hact = salloc("hact", [128, 64, 1024], BF16)
    l3_sb = salloc("l3_sb", [128, 16, 1024], BF16)
    wg_sb = [salloc(f"wg{i}", [128, 16, 128], BF16) for i in range(2)]
    wu_sb = [salloc(f"wu{i}", [128, 16, 128], BF16) for i in range(2)]
    silu_sb = salloc("silu_sb", [128, 512], F32)
    sfree("silu_sb", "wu1", "wu0", "wg1", "wg0", "l3_sb")
    wd_sb = [salloc(f"wd{i}", [128, 32, 128], BF16) for i in range(2)]
    ffnb = salloc("ffnb", [128, 16, 512], BF16)
    x2h = salloc("x2h", [128, 8, 512], BF16)
    x3b = [salloc(f"x3b{i}", [128, 512], F32) for i in range(2)]
    rsb_sb = [salloc(f"rsb{i}", [128, 512], F32) for i in range(2)]

    BSEM_OF = {"sync": "sy", "gpsimd": "sgb", "tensor": "st",
               "vector": "sv", "scalar": "sc"}

    _BE = set(_os.environ.get("KBENGS", "sync,gpsimd,tensor,vector,scalar").split(","))

    def barrier(tag, eng, E):
        if eng not in _BE:
            return
        if eng == "gpsimd":
            if P.emit:
                for s in GP_SEMS:
                    if P.cnt[s]:
                        E.wait_ge(P.sems[s], P.cnt[s])
                E.sem_inc(P.sems["sgb"], 1)
            P.cnt["sgb"] += 1
            if not P.emit:
                P.ms[f"bar.{tag}.gpsimd"] = P.cnt["sgb"]
        elif eng == "sync":
            if P.emit:
                for s in SYNC_SEMS:
                    if P.cnt[s]:
                        E.wait_ge(P.sems[s], P.cnt[s])
            P.op(E, "nop", inc="sy", key=f"bar.{tag}.{eng}")
        else:
            P.op(E, "drain", inc=SEM_OF[eng], key=f"bar.{tag}.{eng}")
        for o in ENGS:
            if o != eng and o in _BE:
                P.wait(E, BSEM_OF[o], f"bar.{tag}.{o}")

    def ln_stats_T(E, tag, prev, rhs1, rhs2, width=512):
        if prev is not None:
            P.wait(E, "sc", f"mq.{prev}")
        for j in range(16):
            P.op(E, "matmul", ps1[:, 0:width], ones128[:], rhs1(j),
                 start=(j == 0), stop=(j == 15))
        for j in range(16):
            P.op(E, "matmul", ps2[:, 0:width], ones128[:], rhs2(j),
                 start=(j == 0), stop=(j == 15),
                 **({"inc": "st", "key": f"st.{tag}"} if j == 15 else {}))

    def ln_stats_S(E, tag, prev, width=512, prev_done=None):
        P.wait(E, "st", f"st.{tag}")
        if prev is not None:
            P.wait(E, "sv", f"mr.{prev}")
        if prev_done is not None:
            P.wait(E, "sv", prev_done)
        P.op(E, "activation", sm_bc[:, 0:width], ps1[:, 0:width],
             AF.Identity, scale=1.0 / D)
        P.op(E, "activation", sq_bc[:, 0:width], ps2[:, 0:width],
             AF.Identity, scale=1.0 / D, inc="sc", key=f"mq.{tag}")
        P.wait(E, "sv", f"var.{tag}")
        P.op(E, "activation", st_bc[:, 0:width], sv_bc[:, 0:width],
             AF.Ln, bias=eps_t[:])
        P.op(E, "activation", rb_bf[:, 0:width], st_bc[:, 0:width],
             AF.Exp, scale=-0.5, inc="sc", key=f"r.{tag}")

    def ln_stats_V(E, tag, width=512):
        P.wait(E, "sc", f"mq.{tag}")
        P.op(E, "tensor_tensor", st_bc[:, 0:width], sm_bc[:, 0:width],
             sm_bc[:, 0:width], ALU.mult)
        P.op(E, "tensor_tensor", sv_bc[:, 0:width], sq_bc[:, 0:width],
             st_bc[:, 0:width], ALU.subtract, inc="sv", key=f"var.{tag}")
        P.wait(E, "sc", f"r.{tag}")
        P.op(E, "tensor_tensor", mb_bf[:, 0:width], sm_bc[:, 0:width],
             rb_bf[:, 0:width], ALU.mult, inc="sv", key=f"mr.{tag}")

    def p1ac_sync(E):
        for dst, srcap in ((c_bqk, bqk), (c_bv, bvd), (c_bo, bod),
                           (c_gs, gsd), (c_us, usd), (c_ds, dsd)):
            P.op(E, "dma_start", dst[:], srcap, inc="scst")
        if not P.emit:
            P.ms["p0.loads"] = P.cnt["scst"]

    def p1ac_gpsimd(E):
        xr = xt.rearrange("(j p) t -> p j t", p=128)
        for p in range(NPAN1):
            if p >= 2:
                P.wait(E, "sv", f"nrm1.{p-2}")
            P.op(E, "dma_start", xtf[p % 2][:],
                 xr[:, :, p * PW1:(p + 1) * PW1],
                 inc="gx1a" if p % 2 == 0 else "gx1b", key=f"x1.{p}")

    def p1ac_vector(E):
        P.wait(E, "scst", "p0.loads")
        P.op(E, "memset", ones128[:], 1.0)
        P.op(E, "memset", ones_r1b[:], 1.0)
        P.op(E, "memset", eps_t[:], EPS, inc="sv", key="p0.ready")
        for p in range(NPAN1):
            tag = f"p1.{p}"
            ln_stats_V(E, tag, width=PW1)
            for j in range(16):
                P.op(E, "tensor_tensor", tnrm[:, 0:PW1], xtf[p % 2][:, j, :],
                     rb_bf[:, 0:PW1], ALU.mult)
                P.op(E, "tensor_tensor", lnx[:, j, p * PW1:(p + 1) * PW1],
                     tnrm[:, 0:PW1], mb_bf[:, 0:PW1], ALU.subtract,
                     **({"inc": "sv", "key": f"nrm1.{p}"} if j == 15 else {}))

    def p1ac_tensor(E):
        for p in range(NPAN1):
            tag = f"p1.{p}"
            prev = f"p1.{p-1}" if p else None
            P.wait(E, "sc", f"sq1.{p}")
            ln_stats_T(E, tag, prev,
                       lambda j: xtb[:, j, :],
                       lambda j: xsq[:, j, :], width=PW1)

    def p1ac_scalar(E):
        P.wait(E, "sv", "p0.ready")
        for p in range(NPAN1):
            P.wait(E, "gx1a" if p % 2 == 0 else "gx1b", f"x1.{p}")
            if p >= 1:
                P.wait(E, "st", f"st.p1.{p-1}")
            P.op(E, "activation", xtb[:], xtf[p % 2][:], AF.Identity)
            P.op(E, "activation", xsq[:], xtf[p % 2][:], AF.Square,
                 inc="sc", key=f"sq1.{p}")
            ln_stats_S(E, f"p1.{p}", f"p1.{p-1}" if p else None, width=PW1,
                       prev_done=f"nrm1.{p-1}" if p else None)

    def p1d_gpsimd(E):
        barrier("b1", "gpsimd", E)
        wr = wqkv.rearrange("(j p) f -> p j f", p=128)
        def gidx(fc):
            return (fc + 1) * 4 - 1 if fc < 16 else 64 + (fc - 15) * 8 - 1
        for fc in range(K1FC):
            if fc >= 2:
                P.wait(E, "st", f"q.{gidx(fc-2)}")
            P.op(E, "dma_start", wqk_sb[fc % 2][:],
                 wr[:, :, fc * 128:(fc + 1) * 128],
                 inc=f"gw{fc % 2}", key=f"w.{fc}")
        for vf in range(4 if K1V else 0):
            if vf >= 1:
                P.wait(E, "st", f"v.{vf*32-1}")
            P.op(E, "dma_start", wv_sb[:],
                 wr[:, :, 2 * D + vf * 512:2 * D + (vf + 1) * 512],
                 inc="gwv", key=f"wv.{vf}")

    def p1d_tensor(E):
        barrier("b1", "tensor", E)
        gi = 0
        for fc in range(K1FC):
            P.wait(E, f"gw{fc % 2}", f"w.{fc}")
            nts = 4 if fc < 16 else 8
            for ts in range(nts):
                bank = pQ[gi % 2]
                if gi >= 2:
                    P.wait(E, "sv", f"qe.{gi-2}")
                for j in range(16):
                    P.op(E, "matmul", bank[:], wqk_sb[fc % 2][:, j, :],
                         lnx[:, j, ts * 512:(ts + 1) * 512],
                         start=(j == 0), stop=(j == 15),
                         **({"inc": "st", "key": f"q.{gi}"} if j == 15 else {}))
                gi += 1
        vi = 0
        for vf in range(4 if K1V else 0):
            P.wait(E, "gwv", f"wv.{vf}")
            for tc in range(32):
                bank = pQ[vi % 2]
                if vi >= 2:
                    P.wait(E, "sc", f"ve.{vi-2}")
                elif K1FC == 32:
                    P.wait(E, "sv", f"qe.{190+vi}")
                for j in range(16):
                    P.op(E, "matmul", bank[:],
                         lnx[:, j, tc * 128:(tc + 1) * 128], wv_sb[:, j, :],
                         start=(j == 0), stop=(j == 15),
                         **({"inc": "st", "key": f"v.{vi}"} if j == 15 else {}))
                vi += 1

    def p1d_vector(E):
        barrier("b1", "vector", E)
        gi = 0
        for fc in range(K1FC):
            nts = 4 if fc < 16 else 8
            for ts in range(nts):
                bank = pQ[gi % 2]
                P.wait(E, "st", f"q.{gi}")
                if fc < 16:
                    if fc >= 1 and ts == 0 and K1SP:
                        P.wait(E, "ssp", f"qsp.{fc-1}")
                    dst = qstage[:, ts, :]
                else:
                    if ts % 4 == 0 and not (fc == 16 and ts == 0) and K1SP:
                        P.wait(E, "ssp", f"ksp.{(fc-16)*2 + ts//4 - 1}")
                    dst = kstage[:, ts % 4, :]
                P.op(E, "tensor_scalar", dst, bank[:], c_bqk[:, fc:fc + 1],
                     None, ALU.add, inc="sv", key=f"qe.{gi}")
                gi += 1

    def p1d_scalar(E):
        barrier("b1", "scalar", E)
        vi = 0
        for vf in range(4 if K1V else 0):
            for tc in range(32):
                bank = pQ[vi % 2]
                P.wait(E, "st", f"v.{vi}")
                if vi >= 2:
                    P.wait(E, f"svs{vi % 2}", f"vsp.{vi-2}")
                P.op(E, "copy", vstage[vi % 2][:], bank[:], inc="sc",
                     key=f"ve.{vi}")
                vi += 1

    def p1d_sync(E):
        barrier("b1", "sync", E)
        gi = 0
        for fc in range(min(32, K1FC) if K1SP else 0):
            if fc < 16:
                gi += 4
                P.wait(E, "sv", f"qe.{gi-1}")
                P.op(E, "dma_start", qs[fc * 128:(fc + 1) * 128, :],
                     qstage.ap().rearrange("p a t -> p (a t)"),
                     inc="ssp", key=f"qsp.{fc}")
            else:
                for g2 in range(2):
                    gi += 4
                    P.wait(E, "sv", f"qe.{gi-1}")
                    P.op(E, "dma_start",
                         ks[(fc - 16) * 128:(fc - 15) * 128,
                            g2 * 2048:(g2 + 1) * 2048],
                         kstage.ap().rearrange("p a t -> p (a t)"),
                         inc="ssp", key=f"ksp.{(fc-16)*2+g2}")
        vi = 0
        for vf in range(4 if K1V else 0):
            for tc in range(32):
                P.wait(E, "sc", f"ve.{vi}")
                P.op(E, "dma_start",
                     vs[tc * 128:(tc + 1) * 128, vf * 512:(vf + 1) * 512],
                     vstage[vi % 2][:], inc=f"svs{vi % 2}", key=f"vsp.{vi}")
                vi += 1


    def k_iter_list(qt):
        its = []
        for j in range(4 * qt + 4):
            r = j - 4 * qt
            its.append((j * 128, j, r if r >= 0 else None, False))
        for j in range(4 * qt + 4):
            r = j - 4 * qt
            its.append((TOWN + j * 128, 16 + j, r if r >= 0 else None, True))
        return its

    def p2_sync(E):
        barrier("b2", "sync", E)
        P.op(E, "dma_start", c_tri[:], mtri, inc="scst")
        P.op(E, "dma_start", c_mrow[:], mrow, inc="scst", key="p2.consts")
        P.wait(E, "ssp", "ksp.31")
        P.wait(E, "svs0", "vsp.126")
        P.wait(E, "svs1", "vsp.127")
        vsr = vs.rearrange("(a p) f -> p a f", p=128)
        for h in range(NH):
            if h >= 2:
                P.wait(E, "st", f"az.{80*(h-2)+79}")
            P.op(E, "dma_start", kh[h % 2][:], ks[h * 128:(h + 1) * 128, :],
                 inc=f"shl{h % 2}")
            P.op(E, "dma_start", qh[h % 2][:], qs[h * 128:(h + 1) * 128, :],
                 inc=f"shl{h % 2}")
            P.op(E, "dma_start", vh[h % 2][:],
                 vsr[:, :, h * 128:(h + 1) * 128], inc=f"shl{h % 2}",
                 key=f"hl.{h}")

    def p2_tensor(E):
        barrier("b2", "tensor", E)
        P.wait(E, "scst", "p2.consts")
        for r in range(4):
            if r:
                P.wait(E, "sv", f"mbe.{r-1}")
            P.op(E, "matmul", pbc1[:], ones_r1b[:],
                 c_mrow[:, r * 512:(r + 1) * 512], start=True, stop=True,
                 inc="st", key=f"mbld.{r}")
        P.wait(E, "sv", "mbe.3")
        gi = 0
        for h in range(NH):
            P.wait(E, f"shl{h % 2}", f"hl.{h}")
            for qt in range(4):
                its = k_iter_list(qt)
                n = len(its)
                base = gi

                def s_mm(li):
                    g = base + li
                    col, vchunk, r, part = its[li]
                    c0 = 128 * (r or 0)
                    bank = pS3[g % 3]
                    if g >= 3:
                        P.wait(E, "sc", f"e.{g-3}")
                    P.op(E, "matmul", bank[:, c0:512],
                         kh[h % 2][:, col:col + 128],
                         qh[h % 2][:, qt * 512 + c0:(qt + 1) * 512],
                         start=True, stop=True, inc="st", key=f"s.{g}")

                def av_z(li):
                    g = base + li
                    col, vchunk, r, part = its[li]
                    c0 = 128 * (r or 0)
                    hq0 = h * 4 + qt
                    P.wait(E, "sc", f"e.{g}")
                    if li == 0 and hq0 >= 2:
                        P.wait(E, "sv", f"aoe.{hq0-2}")
                        P.wait(E, "sc", f"z.{hq0-2}")
                    P.op(E, "matmul", pavq[hq0 % 2][:, c0:512],
                         vh[h % 2][:, vchunk, :], p_sb[g % 3][:, c0:512],
                         start=(li == 0), stop=(li == n - 1))
                    P.op(E, "matmul", pzq[hq0 % 2][:, c0:512], ones128[:],
                         p_sb[g % 3][:, c0:512],
                         start=(li == 0), stop=(li == n - 1),
                         inc="st", key=f"az.{g}")

                s_mm(0)
                s_mm(1)
                for li in range(2, n):
                    s_mm(li)
                    av_z(li - 2)
                av_z(n - 2)
                av_z(n - 1)
                gi += n

    def p2_vector(E):
        barrier("b2", "vector", E)
        for r in range(4):
            P.wait(E, "st", f"mbld.{r}")
            P.op(E, "tensor_copy", mask_bc[:, r, :], pbc1[:],
                 inc="sv", key=f"mbe.{r}")
        gi = 0
        for h in range(NH):
            for qt in range(4):
                its = k_iter_list(qt)
                for li, (col, vchunk, r, part) in enumerate(its):
                    g = gi + li
                    if r is None:
                        continue
                    c0 = 128 * r
                    P.wait(E, "st", f"s.{g}")
                    bank = pS3[g % 3]
                    if not part:
                        P.op(E, "tensor_tensor", bank[:, c0:c0 + 128],
                             bank[:, c0:c0 + 128], c_tri[:], ALU.add,
                             inc="sv", key=f"m.{g}")
                    else:
                        P.op(E, "tensor_tensor", bank[:, c0:512],
                             bank[:, c0:512], mask_bc[:, r, c0:512],
                             ALU.add, inc="sv", key=f"m.{g}")
                gi += len(its)
                hq = h * 4 + qt
                P.wait(E, "sc", f"z.{hq}")
                P.op(E, "scalar_tensor_tensor", t_ao[:], rzb_sb[hq % 2][:],
                     1.0, pavq[hq % 2][:], ALU.mult, ALU.mult)
                P.op(E, "tensor_scalar", ao[:, h, qt * 512:(qt + 1) * 512],
                     t_ao[:], c_bv[:, h:h + 1], None, ALU.add,
                     inc="sv", key=f"aoe.{hq}")

    def p2_scalar(E):
        barrier("b2", "scalar", E)
        gi = 0
        for h in range(NH):
            for qt in range(4):
                its = k_iter_list(qt)
                n = len(its)
                for li, (col, vchunk, r, part) in enumerate(its):
                    g = gi + li
                    c0 = 128 * (r or 0)
                    if r is not None:
                        P.wait(E, "sv", f"m.{g}")
                    else:
                        P.wait(E, "st", f"s.{g}")
                    if g >= 3:
                        P.wait(E, "st", f"az.{g-3}")
                    P.op(E, "activation", p_sb[g % 3][:, c0:512],
                         pS3[g % 3][:, c0:512], AF.Exp, scale=EXP_SCALE,
                         inc="sc", key=f"e.{g}")
                gi += n
                hq = h * 4 + qt
                P.wait(E, "st", f"az.{gi-1}")
                if hq >= 2:
                    P.wait(E, "sv", f"aoe.{hq-2}")
                P.op(E, "activation", zl128[:], pzq[hq % 2][:], AF.Ln)
                P.op(E, "activation", rzb_sb[hq % 2][:], zl128[:], AF.Exp,
                     scale=-1.0, inc="sc", key=f"z.{hq}")

    pQ4 = [pA, pB, pav, pbc2]

    def p3_gpsimd(E):
        barrier("b3", "gpsimd", E)
        for ts in range(4):
            for oc in range(16):
                gg = ts * 16 + oc
                if gg >= 2:
                    P.wait(E, "st", f"y.{gg-2}")
                P.op(E, "dma_start", wo_sb[gg % 2][:],
                     wod[oc], inc=f"gwo{gg % 2}",
                     key=f"wol.{gg}")

    def p3_tensor(E):
        barrier("b3", "tensor", E)
        prev = [None]

        def p3_stats_l2(ts):
            sl = slice(ts * 512, (ts + 1) * 512)
            P.wait(E, "sv", f"sql2.{ts}")
            ln_stats_T(E, f"l2.{ts}", prev[0],
                       lambda j: yb[:, j, sl], lambda j: l3t[:, j, :])
            prev[0] = f"l2.{ts}"

        def p3_stats_l3(ts):
            P.wait(E, "sv", f"sql3.{ts}")
            ln_stats_T(E, f"l3.{ts}", prev[0],
                       lambda j: x2b[:, j, :], lambda j: l3t[:, j, :])
            prev[0] = f"l3.{ts}"

        for ts in range(4):
            sl = slice(ts * 512, (ts + 1) * 512)
            for oc in range(16):
                gg = ts * 16 + oc
                bank = pQ4[gg % 4]
                P.wait(E, f"gwo{gg % 2}", f"wol.{gg}")
                if gg >= 4:
                    P.wait(E, "sv", f"ye.{gg-4}")
                for j in range(16):
                    P.op(E, "matmul", bank[:], wo_sb[gg % 2][:, j, :],
                         ao[:, j, sl],
                         start=(j == 0), stop=(j == 15),
                         **({"inc": "st", "key": f"y.{gg}"} if j == 15 else {}))
                if oc == 7 and ts >= 1:
                    p3_stats_l2(ts - 1)
            if ts >= 1:
                p3_stats_l3(ts - 1)
        p3_stats_l2(3)
        p3_stats_l3(3)

    def p3_scalar(E):
        barrier("b3", "scalar", E)
        prev = None
        pdone = None
        for ts in range(4):
            ln_stats_S(E, f"l2.{ts}", prev, prev_done=pdone)
            prev = f"l2.{ts}"
            pdone = f"x2.{ts}.3"
            ln_stats_S(E, f"l3.{ts}", prev, prev_done=pdone)
            prev = f"l3.{ts}"
            pdone = f"l3n.{ts}"

    def p3_vector(E):
        barrier("b3", "vector", E)

        def ye1(ts, oc):
            sl = slice(ts * 512, (ts + 1) * 512)
            gg = ts * 16 + oc
            bank = pQ4[gg % 4]
            P.wait(E, "st", f"y.{gg}")
            P.op(E, "tensor_scalar", yb[:, oc, sl],
                 bank[:], c_bo[:, oc:oc + 1], None, ALU.add,
                 inc="sv", key=f"ye.{gg}")

        def tail_blocks(t):
            def b0():
                ln_stats_V(E, f"l3.{t}")
                P.wait(E, "st", f"st.l3.{t}")
                P.wait(E, "sx2w", f"x2w.{t}")
            blocks = [b0]
            for q in range(4):
                def bq(q=q):
                    for j in range(q * 4, q * 4 + 4):
                        P.op(E, "tensor_tensor", tnrm[:], x2b[:, j, :],
                             rb_bf[:], ALU.mult)
                        P.op(E, "tensor_tensor", l3t[:, j, :], tnrm[:],
                             mb_bf[:], ALU.subtract,
                             **({"inc": "sv", "key": f"l3n.{t}"}
                                if j == 15 else {}))
                blocks.append(bq)
            return blocks

        def mid_blocks(t):
            sl = slice(t * 512, (t + 1) * 512)

            def sq2(h):
                def f():
                    if h == 0 and t >= 1:
                        P.wait(E, "sl3w", f"l3sp.{t-1}")
                    for j in range(h * 8, h * 8 + 8):
                        P.op(E, "tensor_tensor", l3t[:, j, :], yb[:, j, sl],
                             yb[:, j, sl], ALU.mult,
                             **({"inc": "sv", "key": f"sql2.{t}"}
                                if j == 15 else {}))
                return f

            def vl2():
                ln_stats_V(E, f"l2.{t}")
                P.wait(E, "st", f"st.l2.{t}")
                if t >= 1:
                    P.wait(E, "sx2w", f"x2w.{t-1}")
            blocks = [sq2(0), sq2(1), vl2]
            for qtr in range(4):
                def xq(qtr=qtr):
                    P.wait(E, "sxo", f"xo.{t}.{qtr}")
                    for j4 in range(4):
                        j = qtr * 4 + j4
                        P.op(E, "tensor_tensor", tnrm[:], yb[:, j, sl],
                             rb_bf[:], ALU.mult)
                        P.op(E, "tensor_tensor", tnrm[:], tnrm[:], mb_bf[:],
                             ALU.subtract)
                        P.op(E, "tensor_tensor", x2b[:, j, :], xo_h[:, j4, :],
                             tnrm[:], ALU.add,
                             **({"inc": "sv", "key": f"x2.{t}.{qtr}"}
                                if j4 == 3 else {}))

                def s3q(qtr=qtr):
                    for j in range(qtr * 4, qtr * 4 + 4):
                        P.op(E, "tensor_tensor", l3t[:, j, :], x2b[:, j, :],
                             x2b[:, j, :], ALU.mult,
                             **({"inc": "sv", "key": f"sql3.{t}"}
                                if j == 15 else {}))
                blocks += [xq, s3q]
            return blocks

        for ts in range(4):
            blocks = []
            if ts >= 2:
                blocks += tail_blocks(ts - 2)
            if ts >= 1:
                blocks += mid_blocks(ts - 1)
            nb = len(blocks)
            bi = 0
            for oc in range(16):
                ye1(ts, oc)
                tgt = (oc + 1) * nb // 16
                while bi < tgt:
                    blocks[bi]()
                    bi += 1
            while bi < nb:
                blocks[bi]()
                bi += 1
        for blk in tail_blocks(2) + mid_blocks(3) + tail_blocks(3):
            blk()

    def p3_sync(E):
        barrier("b3", "sync", E)
        xr = xt.rearrange("(j p) t -> p j t", p=128)
        x2r = x2s.rearrange("(j p) t -> p j t", p=128)
        l3r = l3s.rearrange("(j p) t -> p j t", p=128)
        for ts in range(4):
            sl = slice(ts * 512, (ts + 1) * 512)
            for qtr in range(4):
                if ts >= 1 and qtr == 0:
                    P.wait(E, "sv", f"x2.{ts-1}.3")
                if qtr >= 1:
                    P.wait(E, "sv", f"x2.{ts}.{qtr-1}")
                P.op(E, "dma_start", xo_h[:],
                     xr[:, qtr * 4:(qtr + 1) * 4, sl],
                     inc="sxo", key=f"xo.{ts}.{qtr}")
            P.wait(E, "sv", f"x2.{ts}.3")
            P.op(E, "dma_start", x2r[:, :, sl], x2b[:], inc="sx2w",
                 key=f"x2w.{ts}")
            P.wait(E, "sv", f"l3n.{ts}")
            P.op(E, "dma_start", l3r[:, :, sl], l3t[:], inc="sl3w",
                 key=f"l3sp.{ts}")

    def wait_gate_done(E, grp):
        P.wait(E, "st", f"u.{grp*128+127}")
        P.wait(E, "sv", f"hh.{grp*128+127}")

    def wait_down_done(E, grp):
        P.wait(E, "sv", f"x3j.{grp}.1.1.7")
        P.wait(E, "st", f"d.{grp*32+31}")
        P.wait(E, "sod0", f"od.{grp}.1.1.6")
        P.wait(E, "sod1", f"od.{grp}.1.1.7")

    def odkey(seq):
        g, r = divmod(seq, 32)
        t, r2 = divmod(r, 16)
        hf, j8 = divmod(r2, 8)
        return f"od.{g}.{t}.{hf}.{j8}"

    def p4_gpsimd(E):
        barrier("b4", "gpsimd", E)
        pass
        for grp in range(2):
            for ic in range(64):
                gg = grp * 64 + ic
                if grp >= 1 and ic < 2:
                    wait_down_done(E, grp - 1)
                if gg >= 2:
                    P.wait(E, "st", f"u.{(gg-2)*2+1}")
                P.op(E, "dma_start", wg_sb[ic % 2][:],
                     wgt[ic], inc=f"ggu{gg % 2}")
                P.op(E, "dma_start", wu_sb[ic % 2][:],
                     wut[ic],
                     inc=f"ggu{gg % 2}", key=f"wgu.{gg}")
            hi = 0
            for ts in range(2):
                for oc in range(16):
                    for half in range(2):
                        gg2 = grp * 64 + hi
                        if hi < 2:
                            wait_gate_done(E, grp)
                        if gg2 >= 2:
                            pv = gg2 - 2
                            phi = pv % 64
                            pgrp = pv // 64
                            pts, prr = divmod(phi, 32)
                            poc, phalf = divmod(prr, 2)
                            pkey = (f"d.{pgrp*32+pts*16+poc}" if phalf == 1
                                    else f"dmmA.{pv}")
                            P.wait(E, "st", pkey)
                        P.op(E, "dma_start", wd_sb[hi % 2][:],
                             wdt[oc, :, half * 32:(half + 1) * 32, :],
                             inc=f"gwd{gg2 % 2}", key=f"wd.{gg2}")
                        hi += 1

    def p4_sync(E):
        barrier("b4", "sync", E)
        P.wait(E, "sx2w", "x2w.3")
        P.wait(E, "sl3w", "l3sp.3")
        l3r = l3s.rearrange("(j p) t -> p j t", p=128)
        x2r = x2s.rearrange("(j p) t -> p j t", p=128)
        orr = out.rearrange("(j p) t -> p j t", p=128)
        for grp in range(2):
            gsl = slice(grp * 1024, (grp + 1) * 1024)
            if grp >= 1:
                P.wait(E, "st", f"u.{(grp-1)*128+127}")
                wait_down_done(E, grp - 1)
            P.op(E, "dma_start", l3_sb[:], l3r[:, :, gsl], inc="sl3l",
                 key=f"l3l.{grp}")
            for ts in range(2):
                t0 = grp * 1024 + ts * 512
                for half in range(2):
                    seq = (grp * 2 + ts) * 2 + half
                    if ts == 0 and half == 0:
                        wait_gate_done(E, grp)
                    if seq >= 1:
                        g_, r_ = divmod(seq - 1, 4)
                        t_, hf_ = divmod(r_, 2)
                        P.wait(E, "sv", f"x3j.{g_}.{t_}.{hf_}.7")
                    P.op(E, "dma_start", x2h[:],
                         x2r[:, half * 8:(half + 1) * 8, t0:t0 + 512],
                         inc="sx2l", key=f"x2l.{grp}.{ts}.{half}")
                    for j8 in range(8):
                        P.wait(E, "sv", f"x3j.{grp}.{ts}.{half}.{j8}")
                        P.op(E, "dma_start",
                             orr[:, half * 8 + j8, t0:t0 + 512],
                             x3b[j8 % 2][:], inc=f"sod{j8 % 2}",
                             key=f"od.{grp}.{ts}.{half}.{j8}")
        if KDBG:
            P.op(E, "dma_start", dbg, x2s, inc="sx2w")
        if not P.emit:
            P.ms["end.sod0"] = P.cnt["sod0"]
            P.ms["end.sod1"] = P.cnt["sod1"]
        if P.emit:
            E.wait_ge(P.sems["sod0"], P.ms["end.sod0"])
            E.wait_ge(P.sems["sod1"], P.ms["end.sod1"])

    def p4_tensor(E):
        barrier("b4", "tensor", E)
        for grp in range(2):
            P.wait(E, "sl3l", f"l3l.{grp}")
            gi = 0
            for ic in range(64):
                gg = grp * 64 + ic
                P.wait(E, f"ggu{gg % 2}", f"wgu.{gg}")
                for ts in range(2):
                    g2 = grp * 128 + gi
                    if g2 >= 1:
                        P.wait(E, "sc", f"sl.{g2-1}")
                    for j in range(16):
                        P.op(E, "matmul", pA[:], wg_sb[ic % 2][:, j, :],
                             l3_sb[:, j, ts * 512:(ts + 1) * 512],
                             start=(j == 0), stop=(j == 15),
                             **({"inc": "st", "key": f"g.{g2}"}
                                if j == 15 else {}))
                    if g2 >= 1:
                        P.wait(E, "sv", f"he1.{g2-1}")
                    for j in range(16):
                        P.op(E, "matmul", pB[:], wu_sb[ic % 2][:, j, :],
                             l3_sb[:, j, ts * 512:(ts + 1) * 512],
                             start=(j == 0), stop=(j == 15),
                             **({"inc": "st", "key": f"u.{g2}"}
                                if j == 15 else {}))
                    gi += 1
            for ts in range(2):
                P.wait(E, "sv", f"hh.{grp*128+126+ts}")
                if grp * 2 + ts > 0:
                    pg, pt = divmod(grp * 2 + ts - 1, 2)
                    P.wait(E, "sc", f"rsbe.{pg}.{pt}")
                for m in range(64):
                    P.op(E, "matmul", pz[:], ones128[:],
                         hact[:, m, ts * 512:(ts + 1) * 512],
                         start=(m == 0), stop=(m == 63),
                         **({"inc": "st", "key": f"rs.{grp}.{ts}"}
                            if m == 63 else {}))
            hi = 0
            for ts in range(2):
                tag = f"ln4.{grp}.{ts}"
                prev = (f"ln4.{grp}.{ts-1}" if ts else
                        (f"ln4.{grp-1}.1" if grp else None))

                def lnmm(j):
                    if j == 0 and prev is not None:
                        P.wait(E, "sc", f"mq.{prev}")
                    P.wait(E, "sv", f"fsq.{grp}.{ts}.{j}")
                    P.op(E, "matmul", ps1[:], ones128[:], ffnb[:, j, :],
                         start=(j == 0), stop=(j == 15))
                    P.op(E, "matmul", ps2[:], ones128[:], sqtmp[j % 2][:],
                         start=(j == 0), stop=(j == 15),
                         **({"inc": "st", "key": f"st.{tag}"} if j == 15
                            else {"inc": "st", "key": f"lnmm.{grp}.{ts}.{j}"}))

                for oc in range(16):
                    dd = grp * 32 + ts * 16 + oc
                    for half in range(2):
                        gg2 = grp * 64 + hi
                        P.wait(E, f"gwd{gg2 % 2}", f"wd.{gg2}")
                        if half == 0 and dd >= 2:
                            P.wait(E, "sv", f"de.{dd-2}")
                        mk_ = ({"inc": "st", "key": f"d.{dd}"} if half == 1
                               else {"inc": "st", "key": f"dmmA.{gg2}"})
                        for m in range(32):
                            P.op(E, "matmul", pQ[dd % 2][:],
                                 wd_sb[hi % 2][:, m, :],
                                 hact[:, half * 32 + m,
                                      ts * 512:(ts + 1) * 512],
                                 start=(half == 0 and m == 0),
                                 stop=(half == 1 and m == 31),
                                 **(mk_ if m == 31 else {}))
                        hi += 1
                    if oc >= 2:
                        lnmm(oc - 2)
                lnmm(14)
                lnmm(15)


    def p4_scalar(E):
        barrier("b4", "scalar", E)
        for grp in range(2):
            gi = 0
            for ic in range(64):
                for ts in range(2):
                    g2 = grp * 128 + gi
                    P.wait(E, "st", f"g.{g2}")
                    if g2 >= 1:
                        P.wait(E, "sv", f"hh.{g2-1}")
                    if g2 >= 1:
                        P.wait(E, "sv", f"he1.{g2-1}")
                    P.op(E, "activation", silu_sb[:], pA[:], AF.Silu,
                         scale=c_gs[:, ic:ic + 1], inc="sc", key=f"sl.{g2}")
                    gi += 1
            for ts in range(2):
                P.wait(E, "st", f"rs.{grp}.{ts}")
                if grp * 2 + ts >= 2:
                    pg, pt = divmod(grp * 2 + ts - 2, 2)
                    P.wait(E, "sv", f"de.{pg*32+pt*16+15}")
                P.op(E, "activation", rsb_sb[ts][:], pz[:],
                     AF.Identity, scale=-128.0, inc="sc",
                     key=f"rsbe.{grp}.{ts}")
            for ts in range(2):
                tag = f"ln4.{grp}.{ts}"
                prev = (f"ln4.{grp}.{ts-1}" if ts else
                        (f"ln4.{grp-1}.1" if grp else None))
                pdone = (f"x3j.{grp}.0.1.7" if ts else
                         (f"x3j.{grp-1}.1.1.7" if grp else None))
                ln_stats_S(E, tag, prev, prev_done=pdone)

    def p4_vector(E):
        barrier("b4", "vector", E)
        for grp in range(2):
            gi = 0
            for ic in range(64):
                for ts in range(2):
                    g2 = grp * 128 + gi
                    P.wait(E, "sc", f"sl.{g2}")
                    P.wait(E, "st", f"u.{g2}")
                    P.op(E, "tensor_tensor", tnrm[:], silu_sb[:], pB[:],
                         ALU.mult, inc="sv", key=f"he1.{g2}")
                    P.op(E, "tensor_scalar",
                         hact[:, ic, ts * 512:(ts + 1) * 512],
                         tnrm[:], c_us[:, ic:ic + 1], None, ALU.mult,
                         inc="sv", key=f"hh.{g2}")
                    gi += 1
            for ts in range(2):
                tag = f"ln4.{grp}.{ts}"
                P.wait(E, "sc", f"rsbe.{grp}.{ts}")
                for oc in range(16):
                    dd = grp * 32 + ts * 16 + oc
                    P.wait(E, "st", f"d.{dd}")
                    P.op(E, "scalar_tensor_tensor", t_ao[:], rsb_sb[ts][:],
                         1.0, pQ[dd % 2][:], ALU.mult, ALU.add)
                    P.op(E, "tensor_scalar", ffnb[:, oc, :], t_ao[:],
                         c_ds[:, oc:oc + 1], None, ALU.mult,
                         inc="sv", key=f"de.{dd}")
                    if oc >= 2:
                        P.wait(E, "st", f"lnmm.{grp}.{ts}.{oc-2}")
                    P.op(E, "tensor_tensor", sqtmp[oc % 2][:], ffnb[:, oc, :],
                         ffnb[:, oc, :], ALU.mult, inc="sv",
                         key=f"fsq.{grp}.{ts}.{oc}")
                ln_stats_V(E, tag)
                for half in range(2):
                    P.wait(E, "sx2l", f"x2l.{grp}.{ts}.{half}")
                    for j8 in range(8):
                        j = half * 8 + j8
                        seq = ((grp * 2 + ts) * 2 + half) * 8 + j8
                        if seq >= 2:
                            P.wait(E, f"sod{j8 % 2}", odkey(seq - 2))
                        P.op(E, "tensor_tensor", tnrm[:], ffnb[:, j, :],
                             rb_bf[:], ALU.mult)
                        P.op(E, "tensor_tensor", tnrm[:], tnrm[:], mb_bf[:],
                             ALU.subtract)
                        P.op(E, "tensor_tensor", x3b[j8 % 2][:],
                             x2h[:, j8, :], tnrm[:], ALU.add,
                             inc="sv", key=f"x3j.{grp}.{ts}.{half}.{j8}")

    def p1d_tensor_wrap(E):
        p1ac_tensor(E)
        p1d_tensor(E)

    phase_tbl = {
        "sync": [p1ac_sync, p1d_sync, p2_sync, p3_sync, p4_sync],
        "gpsimd": [p1ac_gpsimd, p1d_gpsimd,
                   lambda E: barrier("b2", "gpsimd", E),
                   p3_gpsimd, p4_gpsimd],
        "tensor": [p1d_tensor_wrap, p2_tensor, p3_tensor, p4_tensor],
        "vector": [p1ac_vector, p1d_vector, p2_vector, p3_vector, p4_vector],
        "scalar": [p1ac_scalar, p1d_scalar, p2_scalar, p3_scalar, p4_scalar],
    }
    import os
    _L = int(os.environ.get("KPH", "4"))
    if _L == 0:
        phase_tbl = {"sync": [p1ac_sync], "gpsimd": [p1ac_gpsimd],
                     "tensor": [p1ac_tensor], "vector": [p1ac_vector],
                     "scalar": [p1ac_scalar]}
        if _os.environ.get("KB1", "0") == "1":
            for e in ENGS:
                phase_tbl[e] = phase_tbl[e] + [
                    (lambda ee: lambda E: barrier("b1", ee, E))(e)]
    elif _L < 4:
        cut = {"sync": 1 + _L, "gpsimd": 1 + _L, "tensor": _L,
               "vector": 1 + _L, "scalar": 1 + _L}
        phase_tbl = {k: v[:cut[k]] for k, v in phase_tbl.items()}

    sem_names = (["st", "sv", "sc", "sy", "sgb"] + SYNC_SEMS + GP_SEMS)
    sem_guards = []
    for sn in sem_names:
        g = nc.semaphore("sem_" + sn)
        P.sems[sn] = g.__enter__()
        sem_guards.append(g)
    with nc.Block() as block:
        P.start(False)
        for eng in ENGS:
            for fn in phase_tbl[eng]:
                fn(None)
        totals = dict(P.cnt)
        P.start(True)

        @block.sync
        def _(sync):
            for fn in phase_tbl["sync"]:
                fn(sync)

        @block.gpsimd
        def _(g):
            for fn in phase_tbl["gpsimd"]:
                fn(g)

        @block.tensor
        def _(t):
            for fn in phase_tbl["tensor"]:
                fn(t)

        @block.vector
        def _(v):
            for fn in phase_tbl["vector"]:
                fn(v)

        @block.scalar
        def _(s):
            for fn in phase_tbl["scalar"]:
                fn(s)

        for k in totals:
            assert P.cnt[k] == totals[k], (k, P.cnt[k], totals[k])
    return nc


_NC_CACHE = []


def kernel(**inputs):
    x = np.ascontiguousarray(np.asarray(inputs["L0_attn_x"], np.float32))
    Wqkv = np.ascontiguousarray(np.asarray(inputs["L0_attn_Wqkv"], np.float32))
    bqkv = np.asarray(inputs["L0_attn_bqkv"], np.float32)
    Wo = np.ascontiguousarray(
        np.asarray(inputs["L0_attn_Wo"], np.float32)
        .reshape(16, 128, 16, 128).transpose(2, 1, 0, 3))
    bo = np.asarray(inputs["L0_attn_bo"], np.float32)
    gw = np.asarray(inputs["L0_ffn_gate_w"], np.int32)
    gsc = np.asarray(inputs["L0_ffn_gate_scale"], np.float32)
    uw = np.asarray(inputs["L0_ffn_up_w"], np.int32)
    usc = np.asarray(inputs["L0_ffn_up_scale"], np.float32)
    dw = np.asarray(inputs["L0_ffn_down_w"], np.int32)
    dsc = np.asarray(inputs["L0_ffn_down_scale"], np.float32)

    wgt = np.ascontiguousarray(
        gw.T.reshape(16, 128, 64, 128).transpose(2, 1, 0, 3))
    wut = np.ascontiguousarray(
        uw.T.reshape(16, 128, 64, 128).transpose(2, 1, 0, 3))
    wdt = np.ascontiguousarray(
        dw.T.reshape(64, 128, 16, 128).transpose(2, 1, 0, 3))
    bqk = np.ascontiguousarray(bqkv[:2 * D].reshape(32, 128).T)
    bv_ = np.ascontiguousarray(bqkv[2 * D:].reshape(16, 128).T)
    bo_ = np.ascontiguousarray(bo.reshape(16, 128).T)
    gs_ = np.ascontiguousarray(gsc.reshape(64, 128).T)
    us_ = np.ascontiguousarray(usc.reshape(64, 128).T)
    ds_ = np.ascontiguousarray(dsc.reshape(16, 128).T)
    ii = np.arange(128)
    mtri = np.ascontiguousarray(
        np.where(ii[:, None] > ii[None, :], np.float32(MASK),
                 np.float32(0.0)).astype(np.float32))

    if not _NC_CACHE:
        _NC_CACHE.append(build_nc())
    nc = _NC_CACHE[0]
    in_maps = []
    perms = []
    for c in range(8):
        b, hg = divmod(c, 2)
        own = np.concatenate([np.arange(g * 128, (g + 1) * 128)
                              for g in range(hg, 32, 2)])
        par = np.concatenate([np.arange(g * 128, (g + 1) * 128)
                              for g in range(1 - hg, 32, 2)])
        perm = np.concatenate([own, par])
        perms.append(own)
        xt_c = np.ascontiguousarray(x[b].T[:, perm])
        mr = np.zeros((4, 512), np.float32)
        for r in range(4):
            mr[r, : (r + 1 - hg) * 128] = MASK
        mrow_c = np.ascontiguousarray(
            mr.reshape(1, 4 * 512)).astype(ml_dtypes.bfloat16)
        in_maps.append({
            "xt": xt_c, "wqkv": Wqkv, "bqk": bqk, "bv": bv_, "wo": Wo,
            "bo": bo_, "wgt": wgt, "wut": wut, "wdt": wdt, "gs": gs_,
            "us": us_, "ds": ds_, "mtri": mtri, "mrow": mrow_c,
        })

    res = run_bass_kernel_spmd(nc, in_maps, core_ids=list(range(8)))
    global _LAST_RES
    _LAST_RES = res.results
    outp = np.empty_like(x)
    for c in range(8):
        b = c // 2
        outp[b, perms[c], :] = res.results[c]["o"].T
    return outp


# version 16
# speedup vs baseline: 1.1247x; 1.0511x over previous
import numpy as np
import ml_dtypes

import concourse.bass as bass
import concourse.mybir as mybir
from concourse.bass_utils import run_bass_kernel_spmd

F32 = mybir.dt.float32
BF16 = mybir.dt.bfloat16
I32 = mybir.dt.int32
AF = mybir.ActivationFunctionType
ALU = mybir.AluOpType

D = 2048
S = 4096
TOWN = 2048
NH = 16
FI = 8192
EXP_SCALE = 1.0 / float(np.sqrt(128.0))
MASK = -30000.0
EPS = 1e-5
NPAN1 = 16
PW1 = 256

ENGS = ("sync", "gpsimd", "tensor", "vector", "scalar")
SEM_OF = {"sync": "sy", "gpsimd": "sgb", "tensor": "st", "vector": "sv",
          "scalar": "sc"}
SYNC_SEMS = ["scst", "ssp", "svs0", "svs1", "shl0", "shl1", "sxo", "sx2w",
             "sl3w", "sl3l", "sx2l", "sod0", "sod1"]
GP_SEMS = ["gx1a", "gx1b", "gw0", "gw1", "gwv", "gwo0", "gwo1", "ggu0",
           "ggu1", "gwd0", "gwd1"]


class Prog:
    def __init__(self):
        self.ms = {}
        self.sems = {}
        self.emit = False
        self.cnt = {}
        self.mm_mode = None

    def start(self, emit):
        import collections
        self.emit = emit
        self.cnt = collections.defaultdict(int)
        self.mm_mode = None

    def op(self, E, name, *args, inc=None, key=None, **kw):
        inst = getattr(E, name)(*args, **kw) if self.emit else None
        if inc is not None:
            amt = 1 if inc in ("st", "sv", "sc", "sy", "sgb") else 16
            self.cnt[inc] += amt
            if key is not None and not self.emit:
                assert key not in self.ms, f"dup {key}"
                self.ms[key] = self.cnt[inc]
            if self.emit and inst is not None:
                inst.then_inc(self.sems[inc], amt)
        return inst

    def wait(self, E, sem, key):
        if key is None:
            return
        if self.emit:
            assert key in self.ms, f"missing milestone {key}"
            E.wait_ge(self.sems[sem], self.ms[key])


def build_nc():
    import os as _os
    K1V = _os.environ.get("K1V", "1") == "1"
    K1SP = _os.environ.get("K1SP", "1") == "1"
    K1FC = int(_os.environ.get("K1FC", "32"))
    global NPAN1
    NPAN1 = int(_os.environ.get("KNP", "16"))
    nc = bass.Bass("TRN2", target_bir_lowering=False,
                   detect_race_conditions=False)

    xt = nc.dram_tensor("xt", [D, S], F32, kind="ExternalInput").ap()
    wqkv = nc.dram_tensor("wqkv", [D, 3 * D], F32, kind="ExternalInput").ap()
    bqk = nc.dram_tensor("bqk", [128, 32], F32, kind="ExternalInput").ap()
    bvd = nc.dram_tensor("bv", [128, 16], F32, kind="ExternalInput").ap()
    wod = nc.dram_tensor("wo", [16, 128, 16, 128], F32,
                         kind="ExternalInput").ap()
    bod = nc.dram_tensor("bo", [128, 16], F32, kind="ExternalInput").ap()
    wgt = nc.dram_tensor("wgt", [64, 128, 16, 128], I32,
                         kind="ExternalInput").ap()
    wut = nc.dram_tensor("wut", [64, 128, 16, 128], I32,
                         kind="ExternalInput").ap()
    wdt = nc.dram_tensor("wdt", [16, 128, 64, 128], I32,
                         kind="ExternalInput").ap()
    gsd = nc.dram_tensor("gs", [128, 64], F32, kind="ExternalInput").ap()
    usd = nc.dram_tensor("us", [128, 64], F32, kind="ExternalInput").ap()
    dsd = nc.dram_tensor("ds", [128, 16], F32, kind="ExternalInput").ap()
    mtri = nc.dram_tensor("mtri", [128, 128], F32, kind="ExternalInput").ap()
    mrow = nc.dram_tensor("mrow", [1, 4 * 512], BF16,
                          kind="ExternalInput").ap()
    out = nc.dram_tensor("o", [D, TOWN], F32, kind="ExternalOutput").ap()
    qs = nc.dram_tensor("qs", [D, TOWN], BF16, kind="Internal").ap()
    ks = nc.dram_tensor("ks", [D, S], BF16, kind="Internal").ap()
    vs = nc.dram_tensor("vs", [S, D], BF16, kind="Internal").ap()
    x2s = nc.dram_tensor("x2s", [D, TOWN], BF16, kind="Internal").ap()
    KDBG = _os.environ.get("KDBG", "0") == "1"
    dbg = (nc.dram_tensor("dbg", [D, TOWN], BF16, kind="ExternalOutput").ap()
           if KDBG else None)
    dbga = (nc.dram_tensor("dbga", [128, NH * TOWN], BF16,
                           kind="ExternalOutput").ap() if KDBG else None)
    l3s = nc.dram_tensor("l3s", [D, TOWN], BF16, kind="Internal").ap()

    P = Prog()
    P.sems = {}
    guards = []

    def mk(name, shape, dt, kind="sbuf"):
        g = (nc.sbuf_tensor if kind == "sbuf" else nc.psum_tensor)(
            name, shape, dt)
        h = g.__enter__()
        guards.append(g)
        return h

    scoped = {}

    def salloc(name, shape, dt):
        g = nc.sbuf_tensor(name, shape, dt)
        h = g.__enter__()
        scoped[name] = g
        return h

    def sfree(*names):
        for n in names:
            scoped.pop(n).__exit__(None, None, None)

    c_bqk = mk("c_bqk", [128, 32], F32)
    c_bv = mk("c_bv", [128, 16], F32)
    c_bo = mk("c_bo", [128, 16], F32)
    c_gs = mk("c_gs", [128, 64], F32)
    c_us = mk("c_us", [128, 64], F32)
    c_ds = mk("c_ds", [128, 16], F32)
    ones_k = mk("ones_k", [128, 1], BF16)
    ones_r1b = mk("ones_r1b", [1, 128], BF16)
    ones_r1f = mk("ones_r1f", [1, 128], F32)
    neg128_r1f = mk("neg128_r1f", [1, 128], F32)
    eps_t = mk("eps_t", [128, 1], F32)
    ones128 = mk("ones128", [128, 128], BF16)
    sm_bc = mk("sm_bc", [128, 512], F32)
    sq_bc = mk("sq_bc", [128, 512], F32)
    st_bc = mk("st_bc", [128, 512], F32)
    sv_bc = mk("sv_bc", [128, 512], F32)
    rb_bf = mk("rb_bf", [128, 512], BF16)
    mb_bf = mk("mb_bf", [128, 512], BF16)
    t_ao = mk("t_ao", [128, 512], F32)
    tnrm = mk("tnrm", [128, 512], F32)
    sqtmp = [mk(f"sqtmp{i}", [128, 512], BF16) for i in range(2)]

    pA = mk("pA", [128, 512], F32, "psum")
    pB = mk("pB", [128, 512], F32, "psum")
    pav = mk("pav", [128, 512], F32, "psum")
    pbc1 = mk("pbc1", [128, 512], F32, "psum")
    pbc2 = mk("pbc2", [128, 512], F32, "psum")
    pz = mk("pz", [128, 512], F32, "psum")
    ps1 = mk("ps1", [128, 512], F32, "psum")
    ps2 = mk("ps2", [128, 512], F32, "psum")
    pQ = [pA, pB]
    pavq = [pav, pbc2]
    pzq = [pz, ps1]
    pS3 = [pA, pB, pbc1, ps2]

    lnx = salloc("lnx", [128, NH, S], BF16)
    xtf = [salloc(f"xtf{i}", [128, 16, PW1], F32) for i in range(2)]
    xtb = salloc("xtb", [128, 16, PW1], BF16)
    xsq = salloc("xsq", [128, 16, PW1], BF16)
    sfree("xsq", "xtb", "xtf1", "xtf0")
    wqk_sb = [salloc(f"wqk{i}", [128, 16, 128], BF16) for i in range(2)]
    wv_sb = salloc("wv_sb", [128, 16, 512], BF16)
    qstage = salloc("qstage", [128, 4, 512], BF16)
    kstage = salloc("kstage", [128, 4, 512], BF16)
    vstage = [salloc(f"vstage{i}", [128, 512], BF16) for i in range(2)]
    sfree("vstage1", "vstage0", "kstage", "qstage", "wv_sb",
          "wqk1", "wqk0", "lnx")
    ao = salloc("ao", [128, NH, TOWN], BF16)
    kh = [salloc(f"kh{i}", [128, S], BF16) for i in range(2)]
    zl128 = salloc("zl128", [128, 512], F32)
    mask_bc = salloc("mask_bc", [128, 4, 512], BF16)
    qh = [salloc(f"qh{i}", [128, TOWN], BF16) for i in range(2)]
    vh = [salloc(f"vh{i}", [128, 32, 128], BF16) for i in range(2)]
    p_sb = [salloc(f"p_sb{i}", [128, 512], BF16) for i in range(4)]
    rzb_sb = [salloc(f"rzb_sb{i}", [128, 512], BF16) for i in range(2)]
    c_tri = salloc("c_tri", [128, 128], F32)
    c_mrow = salloc("c_mrow", [1, 4 * 512], BF16)
    sfree("c_mrow", "c_tri", "rzb_sb1", "rzb_sb0",
          "p_sb3", "p_sb2", "p_sb1", "p_sb0", "vh1", "vh0", "qh1", "qh0",
          "mask_bc", "zl128", "kh1", "kh0")
    yb = salloc("yb", [128, NH, TOWN], BF16)
    wo_sb = [salloc(f"wo{i}", [128, 16, 128], BF16) for i in range(2)]
    xo_h = salloc("xo_h", [128, 4, 512], F32)
    x2b = salloc("x2b", [128, 16, 512], BF16)
    l3t = salloc("l3t", [128, 16, 512], BF16)
    sfree("l3t", "x2b", "xo_h", "wo1", "wo0", "yb", "ao")
    hact = salloc("hact", [128, 64, 1024], BF16)
    l3_sb = salloc("l3_sb", [128, 16, 1024], BF16)
    wg_sb = [salloc(f"wg{i}", [128, 16, 128], BF16) for i in range(2)]
    wu_sb = [salloc(f"wu{i}", [128, 16, 128], BF16) for i in range(2)]
    silu_sb = salloc("silu_sb", [128, 512], F32)
    sfree("silu_sb", "wu1", "wu0", "wg1", "wg0", "l3_sb")
    wd_sb = [salloc(f"wd{i}", [128, 32, 128], BF16) for i in range(2)]
    ffnb = salloc("ffnb", [128, 16, 512], BF16)
    x2h = salloc("x2h", [128, 8, 512], BF16)
    x3b = [salloc(f"x3b{i}", [128, 512], F32) for i in range(2)]
    rsb_sb = [salloc(f"rsb{i}", [128, 512], F32) for i in range(2)]

    BSEM_OF = {"sync": "sy", "gpsimd": "sgb", "tensor": "st",
               "vector": "sv", "scalar": "sc"}

    _BE = set(_os.environ.get("KBENGS", "sync,gpsimd,tensor,vector,scalar").split(","))

    def barrier(tag, eng, E):
        if eng not in _BE:
            return
        if eng == "gpsimd":
            if P.emit:
                for s in GP_SEMS:
                    if P.cnt[s]:
                        E.wait_ge(P.sems[s], P.cnt[s])
                E.sem_inc(P.sems["sgb"], 1)
            P.cnt["sgb"] += 1
            if not P.emit:
                P.ms[f"bar.{tag}.gpsimd"] = P.cnt["sgb"]
        elif eng == "sync":
            if P.emit:
                for s in SYNC_SEMS:
                    if P.cnt[s]:
                        E.wait_ge(P.sems[s], P.cnt[s])
            P.op(E, "nop", inc="sy", key=f"bar.{tag}.{eng}")
        else:
            P.op(E, "drain", inc=SEM_OF[eng], key=f"bar.{tag}.{eng}")
        for o in ENGS:
            if o != eng and o in _BE:
                P.wait(E, BSEM_OF[o], f"bar.{tag}.{o}")

    def ln_stats_T(E, tag, prev, rhs1, rhs2, width=512):
        if prev is not None:
            P.wait(E, "sc", f"mq.{prev}")
        for j in range(16):
            P.op(E, "matmul", ps1[:, 0:width], ones128[:], rhs1(j),
                 start=(j == 0), stop=(j == 15))
        for j in range(16):
            P.op(E, "matmul", ps2[:, 0:width], ones128[:], rhs2(j),
                 start=(j == 0), stop=(j == 15),
                 **({"inc": "st", "key": f"st.{tag}"} if j == 15 else {}))

    def ln_stats_S(E, tag, prev, width=512, prev_done=None):
        P.wait(E, "st", f"st.{tag}")
        if prev is not None:
            P.wait(E, "sv", f"mr.{prev}")
        if prev_done is not None:
            P.wait(E, "sv", prev_done)
        P.op(E, "activation", sm_bc[:, 0:width], ps1[:, 0:width],
             AF.Identity, scale=1.0 / D)
        P.op(E, "activation", sq_bc[:, 0:width], ps2[:, 0:width],
             AF.Identity, scale=1.0 / D, inc="sc", key=f"mq.{tag}")
        P.wait(E, "sv", f"var.{tag}")
        P.op(E, "activation", st_bc[:, 0:width], sv_bc[:, 0:width],
             AF.Ln, bias=eps_t[:])
        P.op(E, "activation", rb_bf[:, 0:width], st_bc[:, 0:width],
             AF.Exp, scale=-0.5, inc="sc", key=f"r.{tag}")

    def ln_stats_V(E, tag, width=512):
        P.wait(E, "sc", f"mq.{tag}")
        P.op(E, "tensor_tensor", st_bc[:, 0:width], sm_bc[:, 0:width],
             sm_bc[:, 0:width], ALU.mult)
        P.op(E, "tensor_tensor", sv_bc[:, 0:width], sq_bc[:, 0:width],
             st_bc[:, 0:width], ALU.subtract, inc="sv", key=f"var.{tag}")
        P.wait(E, "sc", f"r.{tag}")
        P.op(E, "tensor_tensor", mb_bf[:, 0:width], sm_bc[:, 0:width],
             rb_bf[:, 0:width], ALU.mult, inc="sv", key=f"mr.{tag}")

    def p1ac_sync(E):
        for dst, srcap in ((c_bqk, bqk), (c_bv, bvd), (c_bo, bod),
                           (c_gs, gsd), (c_us, usd), (c_ds, dsd)):
            P.op(E, "dma_start", dst[:], srcap, inc="scst")
        if not P.emit:
            P.ms["p0.loads"] = P.cnt["scst"]

    def p1ac_gpsimd(E):
        xr = xt.rearrange("(j p) t -> p j t", p=128)
        for p in range(NPAN1):
            if p >= 2:
                P.wait(E, "sv", f"nrm1.{p-2}")
            P.op(E, "dma_start", xtf[p % 2][:],
                 xr[:, :, p * PW1:(p + 1) * PW1],
                 inc="gx1a" if p % 2 == 0 else "gx1b", key=f"x1.{p}")

    def p1ac_vector(E):
        P.wait(E, "scst", "p0.loads")
        P.op(E, "memset", ones128[:], 1.0)
        P.op(E, "memset", ones_r1b[:], 1.0)
        P.op(E, "memset", eps_t[:], EPS, inc="sv", key="p0.ready")
        for p in range(NPAN1):
            tag = f"p1.{p}"
            ln_stats_V(E, tag, width=PW1)
            for j in range(16):
                P.op(E, "tensor_tensor", tnrm[:, 0:PW1], xtf[p % 2][:, j, :],
                     rb_bf[:, 0:PW1], ALU.mult)
                P.op(E, "tensor_tensor", lnx[:, j, p * PW1:(p + 1) * PW1],
                     tnrm[:, 0:PW1], mb_bf[:, 0:PW1], ALU.subtract,
                     **({"inc": "sv", "key": f"nrm1.{p}"} if j == 15 else {}))

    def p1ac_tensor(E):
        for p in range(NPAN1):
            tag = f"p1.{p}"
            prev = f"p1.{p-1}" if p else None
            P.wait(E, "sc", f"sq1.{p}")
            ln_stats_T(E, tag, prev,
                       lambda j: xtb[:, j, :],
                       lambda j: xsq[:, j, :], width=PW1)

    def p1ac_scalar(E):
        P.wait(E, "sv", "p0.ready")
        for p in range(NPAN1):
            P.wait(E, "gx1a" if p % 2 == 0 else "gx1b", f"x1.{p}")
            if p >= 1:
                P.wait(E, "st", f"st.p1.{p-1}")
            P.op(E, "activation", xtb[:], xtf[p % 2][:], AF.Identity)
            P.op(E, "activation", xsq[:], xtf[p % 2][:], AF.Square,
                 inc="sc", key=f"sq1.{p}")
            ln_stats_S(E, f"p1.{p}", f"p1.{p-1}" if p else None, width=PW1,
                       prev_done=f"nrm1.{p-1}" if p else None)

    def p1d_gpsimd(E):
        barrier("b1", "gpsimd", E)
        wr = wqkv.rearrange("(j p) f -> p j f", p=128)
        def gidx(fc):
            return (fc + 1) * 4 - 1 if fc < 16 else 64 + (fc - 15) * 8 - 1
        for fc in range(K1FC):
            if fc >= 2:
                P.wait(E, "st", f"q.{gidx(fc-2)}")
            P.op(E, "dma_start", wqk_sb[fc % 2][:],
                 wr[:, :, fc * 128:(fc + 1) * 128],
                 inc=f"gw{fc % 2}", key=f"w.{fc}")
        for vf in range(4 if K1V else 0):
            if vf >= 1:
                P.wait(E, "st", f"v.{vf*32-1}")
            P.op(E, "dma_start", wv_sb[:],
                 wr[:, :, 2 * D + vf * 512:2 * D + (vf + 1) * 512],
                 inc="gwv", key=f"wv.{vf}")

    def p1d_tensor(E):
        barrier("b1", "tensor", E)
        gi = 0
        for fc in range(K1FC):
            P.wait(E, f"gw{fc % 2}", f"w.{fc}")
            nts = 4 if fc < 16 else 8
            for ts in range(nts):
                bank = pQ[gi % 2]
                if gi >= 2:
                    P.wait(E, "sv", f"qe.{gi-2}")
                for j in range(16):
                    P.op(E, "matmul", bank[:], wqk_sb[fc % 2][:, j, :],
                         lnx[:, j, ts * 512:(ts + 1) * 512],
                         start=(j == 0), stop=(j == 15),
                         **({"inc": "st", "key": f"q.{gi}"} if j == 15 else {}))
                gi += 1
        vi = 0
        for vf in range(4 if K1V else 0):
            P.wait(E, "gwv", f"wv.{vf}")
            for tc in range(32):
                bank = pQ[vi % 2]
                if vi >= 2:
                    P.wait(E, "sc", f"ve.{vi-2}")
                elif K1FC == 32:
                    P.wait(E, "sv", f"qe.{190+vi}")
                for j in range(16):
                    P.op(E, "matmul", bank[:],
                         lnx[:, j, tc * 128:(tc + 1) * 128], wv_sb[:, j, :],
                         start=(j == 0), stop=(j == 15),
                         **({"inc": "st", "key": f"v.{vi}"} if j == 15 else {}))
                vi += 1

    def p1d_vector(E):
        barrier("b1", "vector", E)
        gi = 0
        for fc in range(K1FC):
            nts = 4 if fc < 16 else 8
            for ts in range(nts):
                bank = pQ[gi % 2]
                P.wait(E, "st", f"q.{gi}")
                if fc < 16:
                    if fc >= 1 and ts == 0 and K1SP:
                        P.wait(E, "ssp", f"qsp.{fc-1}")
                    dst = qstage[:, ts, :]
                else:
                    if ts % 4 == 0 and not (fc == 16 and ts == 0) and K1SP:
                        P.wait(E, "ssp", f"ksp.{(fc-16)*2 + ts//4 - 1}")
                    dst = kstage[:, ts % 4, :]
                P.op(E, "tensor_scalar", dst, bank[:], c_bqk[:, fc:fc + 1],
                     None, ALU.add, inc="sv", key=f"qe.{gi}")
                gi += 1

    def p1d_scalar(E):
        barrier("b1", "scalar", E)
        vi = 0
        for vf in range(4 if K1V else 0):
            for tc in range(32):
                bank = pQ[vi % 2]
                P.wait(E, "st", f"v.{vi}")
                if vi >= 2:
                    P.wait(E, f"svs{vi % 2}", f"vsp.{vi-2}")
                P.op(E, "copy", vstage[vi % 2][:], bank[:], inc="sc",
                     key=f"ve.{vi}")
                vi += 1

    def p1d_sync(E):
        barrier("b1", "sync", E)
        gi = 0
        for fc in range(min(32, K1FC) if K1SP else 0):
            if fc < 16:
                gi += 4
                P.wait(E, "sv", f"qe.{gi-1}")
                P.op(E, "dma_start", qs[fc * 128:(fc + 1) * 128, :],
                     qstage.ap().rearrange("p a t -> p (a t)"),
                     inc="ssp", key=f"qsp.{fc}")
            else:
                for g2 in range(2):
                    gi += 4
                    P.wait(E, "sv", f"qe.{gi-1}")
                    P.op(E, "dma_start",
                         ks[(fc - 16) * 128:(fc - 15) * 128,
                            g2 * 2048:(g2 + 1) * 2048],
                         kstage.ap().rearrange("p a t -> p (a t)"),
                         inc="ssp", key=f"ksp.{(fc-16)*2+g2}")
        vi = 0
        for vf in range(4 if K1V else 0):
            for tc in range(32):
                P.wait(E, "sc", f"ve.{vi}")
                P.op(E, "dma_start",
                     vs[tc * 128:(tc + 1) * 128, vf * 512:(vf + 1) * 512],
                     vstage[vi % 2][:], inc=f"svs{vi % 2}", key=f"vsp.{vi}")
                vi += 1


    def k_iter_list(qt):
        its = []
        for j in range(4 * qt + 4):
            r = j - 4 * qt
            its.append((j * 128, j, r if r >= 0 else None, False))
        for j in range(4 * qt + 4):
            r = j - 4 * qt
            its.append((TOWN + j * 128, 16 + j, r if r >= 0 else None, True))
        return its

    def p2_sync(E):
        barrier("b2", "sync", E)
        P.op(E, "dma_start", c_tri[:], mtri, inc="scst")
        P.op(E, "dma_start", c_mrow[:], mrow, inc="scst", key="p2.consts")
        P.wait(E, "ssp", "ksp.31")
        P.wait(E, "svs0", "vsp.126")
        P.wait(E, "svs1", "vsp.127")
        vsr = vs.rearrange("(a p) f -> p a f", p=128)
        for h in range(NH):
            if h >= 2:
                P.wait(E, "st", f"az.{80*(h-2)+79}")
            P.op(E, "dma_start", kh[h % 2][:], ks[h * 128:(h + 1) * 128, :],
                 inc=f"shl{h % 2}")
            P.op(E, "dma_start", qh[h % 2][:], qs[h * 128:(h + 1) * 128, :],
                 inc=f"shl{h % 2}")
            P.op(E, "dma_start", vh[h % 2][:],
                 vsr[:, :, h * 128:(h + 1) * 128], inc=f"shl{h % 2}",
                 key=f"hl.{h}")

    def p2_tensor(E):
        barrier("b2", "tensor", E)
        P.wait(E, "scst", "p2.consts")
        for r in range(4):
            if r:
                P.wait(E, "sv", f"mbe.{r-1}")
            P.op(E, "matmul", pbc1[:], ones_r1b[:],
                 c_mrow[:, r * 512:(r + 1) * 512], start=True, stop=True,
                 inc="st", key=f"mbld.{r}")
        P.wait(E, "sv", "mbe.3")
        gi = 0
        for h in range(NH):
            P.wait(E, f"shl{h % 2}", f"hl.{h}")
            for qt in range(4):
                its = k_iter_list(qt)
                n = len(its)
                base = gi

                def s_mm(li):
                    g = base + li
                    col, vchunk, r, part = its[li]
                    c0 = 128 * (r or 0)
                    bank = pS3[g % 4]
                    if g >= 4:
                        P.wait(E, "sc", f"e.{g-4}")
                    P.op(E, "matmul", bank[:, c0:512],
                         kh[h % 2][:, col:col + 128],
                         qh[h % 2][:, qt * 512 + c0:(qt + 1) * 512],
                         start=True, stop=True, inc="st", key=f"s.{g}")

                def av_z(li):
                    g = base + li
                    col, vchunk, r, part = its[li]
                    c0 = 128 * (r or 0)
                    hq0 = h * 4 + qt
                    P.wait(E, "sc", f"e.{g}")
                    if li == 0 and hq0 >= 2:
                        P.wait(E, "sv", f"aoe.{hq0-2}")
                        P.wait(E, "sc", f"z.{hq0-2}")
                    P.op(E, "matmul", pavq[hq0 % 2][:, c0:512],
                         vh[h % 2][:, vchunk, :], p_sb[g % 4][:, c0:512],
                         start=(li == 0), stop=(li == n - 1))
                    P.op(E, "matmul", pzq[hq0 % 2][:, c0:512], ones128[:],
                         p_sb[g % 4][:, c0:512],
                         start=(li == 0), stop=(li == n - 1),
                         inc="st", key=f"az.{g}")

                s_mm(0)
                s_mm(1)
                for li in range(2, n):
                    s_mm(li)
                    av_z(li - 2)
                av_z(n - 2)
                av_z(n - 1)
                gi += n

    def p2_vector(E):
        barrier("b2", "vector", E)
        for r in range(4):
            P.wait(E, "st", f"mbld.{r}")
            P.op(E, "tensor_copy", mask_bc[:, r, :], pbc1[:],
                 inc="sv", key=f"mbe.{r}")
        gi = 0
        for h in range(NH):
            for qt in range(4):
                its = k_iter_list(qt)
                for li, (col, vchunk, r, part) in enumerate(its):
                    g = gi + li
                    if r is None:
                        continue
                    c0 = 128 * r
                    P.wait(E, "st", f"s.{g}")
                    bank = pS3[g % 4]
                    if not part:
                        P.op(E, "tensor_tensor", bank[:, c0:c0 + 128],
                             bank[:, c0:c0 + 128], c_tri[:], ALU.add,
                             inc="sv", key=f"m.{g}")
                    else:
                        P.op(E, "tensor_tensor", bank[:, c0:512],
                             bank[:, c0:512], mask_bc[:, r, c0:512],
                             ALU.add, inc="sv", key=f"m.{g}")
                gi += len(its)
                hq = h * 4 + qt
                P.wait(E, "sc", f"z.{hq}")
                P.op(E, "scalar_tensor_tensor", t_ao[:], rzb_sb[hq % 2][:],
                     1.0, pavq[hq % 2][:], ALU.mult, ALU.mult)
                P.op(E, "tensor_scalar", ao[:, h, qt * 512:(qt + 1) * 512],
                     t_ao[:], c_bv[:, h:h + 1], None, ALU.add,
                     inc="sv", key=f"aoe.{hq}")

    def p2_scalar(E):
        barrier("b2", "scalar", E)
        gi = 0
        for h in range(NH):
            for qt in range(4):
                its = k_iter_list(qt)
                n = len(its)
                for li, (col, vchunk, r, part) in enumerate(its):
                    g = gi + li
                    c0 = 128 * (r or 0)
                    if r is not None:
                        P.wait(E, "sv", f"m.{g}")
                    else:
                        P.wait(E, "st", f"s.{g}")
                    if g >= 4:
                        P.wait(E, "st", f"az.{g-4}")
                    P.op(E, "activation", p_sb[g % 4][:, c0:512],
                         pS3[g % 4][:, c0:512], AF.Exp, scale=EXP_SCALE,
                         inc="sc", key=f"e.{g}")
                gi += n
                hq = h * 4 + qt
                P.wait(E, "st", f"az.{gi-1}")
                if hq >= 2:
                    P.wait(E, "sv", f"aoe.{hq-2}")
                P.op(E, "activation", zl128[:], pzq[hq % 2][:], AF.Ln)
                P.op(E, "activation", rzb_sb[hq % 2][:], zl128[:], AF.Exp,
                     scale=-1.0, inc="sc", key=f"z.{hq}")

    pQ4 = [pA, pB, pav, pbc2]

    def p3_gpsimd(E):
        barrier("b3", "gpsimd", E)
        for ts in range(4):
            for oc in range(16):
                gg = ts * 16 + oc
                if gg >= 2:
                    P.wait(E, "st", f"y.{gg-2}")
                P.op(E, "dma_start", wo_sb[gg % 2][:],
                     wod[oc], inc=f"gwo{gg % 2}",
                     key=f"wol.{gg}")

    def p3_tensor(E):
        barrier("b3", "tensor", E)
        prev = [None]

        def p3_stats_l2(ts):
            sl = slice(ts * 512, (ts + 1) * 512)
            P.wait(E, "sv", f"sql2.{ts}")
            ln_stats_T(E, f"l2.{ts}", prev[0],
                       lambda j: yb[:, j, sl], lambda j: l3t[:, j, :])
            prev[0] = f"l2.{ts}"

        def p3_stats_l3(ts):
            P.wait(E, "sv", f"sql3.{ts}")
            ln_stats_T(E, f"l3.{ts}", prev[0],
                       lambda j: x2b[:, j, :], lambda j: l3t[:, j, :])
            prev[0] = f"l3.{ts}"

        for ts in range(4):
            sl = slice(ts * 512, (ts + 1) * 512)
            for oc in range(16):
                gg = ts * 16 + oc
                bank = pQ4[gg % 4]
                P.wait(E, f"gwo{gg % 2}", f"wol.{gg}")
                if gg >= 4:
                    P.wait(E, "sv", f"ye.{gg-4}")
                for j in range(16):
                    P.op(E, "matmul", bank[:], wo_sb[gg % 2][:, j, :],
                         ao[:, j, sl],
                         start=(j == 0), stop=(j == 15),
                         **({"inc": "st", "key": f"y.{gg}"} if j == 15 else {}))
                if oc == 7 and ts >= 1:
                    p3_stats_l2(ts - 1)
            if ts >= 1:
                p3_stats_l3(ts - 1)
        p3_stats_l2(3)
        p3_stats_l3(3)

    def p3_scalar(E):
        barrier("b3", "scalar", E)
        prev = None
        pdone = None
        for ts in range(4):
            ln_stats_S(E, f"l2.{ts}", prev, prev_done=pdone)
            prev = f"l2.{ts}"
            pdone = f"x2.{ts}.3"
            ln_stats_S(E, f"l3.{ts}", prev, prev_done=pdone)
            prev = f"l3.{ts}"
            pdone = f"l3n.{ts}"

    def p3_vector(E):
        barrier("b3", "vector", E)

        def ye1(ts, oc):
            sl = slice(ts * 512, (ts + 1) * 512)
            gg = ts * 16 + oc
            bank = pQ4[gg % 4]
            P.wait(E, "st", f"y.{gg}")
            P.op(E, "tensor_scalar", yb[:, oc, sl],
                 bank[:], c_bo[:, oc:oc + 1], None, ALU.add,
                 inc="sv", key=f"ye.{gg}")

        def tail_blocks(t):
            def b0():
                ln_stats_V(E, f"l3.{t}")
                P.wait(E, "st", f"st.l3.{t}")
                P.wait(E, "sx2w", f"x2w.{t}")
            blocks = [b0]
            for q in range(4):
                def bq(q=q):
                    for j in range(q * 4, q * 4 + 4):
                        P.op(E, "tensor_tensor", tnrm[:], x2b[:, j, :],
                             rb_bf[:], ALU.mult)
                        P.op(E, "tensor_tensor", l3t[:, j, :], tnrm[:],
                             mb_bf[:], ALU.subtract,
                             **({"inc": "sv", "key": f"l3n.{t}"}
                                if j == 15 else {}))
                blocks.append(bq)
            return blocks

        def mid_blocks(t):
            sl = slice(t * 512, (t + 1) * 512)

            def sq2(h):
                def f():
                    if h == 0 and t >= 1:
                        P.wait(E, "sl3w", f"l3sp.{t-1}")
                    for j in range(h * 8, h * 8 + 8):
                        P.op(E, "tensor_tensor", l3t[:, j, :], yb[:, j, sl],
                             yb[:, j, sl], ALU.mult,
                             **({"inc": "sv", "key": f"sql2.{t}"}
                                if j == 15 else {}))
                return f

            def vl2():
                ln_stats_V(E, f"l2.{t}")
                P.wait(E, "st", f"st.l2.{t}")
                if t >= 1:
                    P.wait(E, "sx2w", f"x2w.{t-1}")
            blocks = [sq2(0), sq2(1), vl2]
            for qtr in range(4):
                def xq(qtr=qtr):
                    P.wait(E, "sxo", f"xo.{t}.{qtr}")
                    for j4 in range(4):
                        j = qtr * 4 + j4
                        P.op(E, "tensor_tensor", tnrm[:], yb[:, j, sl],
                             rb_bf[:], ALU.mult)
                        P.op(E, "tensor_tensor", tnrm[:], tnrm[:], mb_bf[:],
                             ALU.subtract)
                        P.op(E, "tensor_tensor", x2b[:, j, :], xo_h[:, j4, :],
                             tnrm[:], ALU.add,
                             **({"inc": "sv", "key": f"x2.{t}.{qtr}"}
                                if j4 == 3 else {}))

                def s3q(qtr=qtr):
                    for j in range(qtr * 4, qtr * 4 + 4):
                        P.op(E, "tensor_tensor", l3t[:, j, :], x2b[:, j, :],
                             x2b[:, j, :], ALU.mult,
                             **({"inc": "sv", "key": f"sql3.{t}"}
                                if j == 15 else {}))
                blocks += [xq, s3q]
            return blocks

        for ts in range(4):
            blocks = []
            if ts >= 2:
                blocks += tail_blocks(ts - 2)
            if ts >= 1:
                blocks += mid_blocks(ts - 1)
            nb = len(blocks)
            bi = 0
            for oc in range(16):
                ye1(ts, oc)
                tgt = (oc + 1) * nb // 16
                while bi < tgt:
                    blocks[bi]()
                    bi += 1
            while bi < nb:
                blocks[bi]()
                bi += 1
        for blk in tail_blocks(2) + mid_blocks(3) + tail_blocks(3):
            blk()

    def p3_sync(E):
        barrier("b3", "sync", E)
        xr = xt.rearrange("(j p) t -> p j t", p=128)
        x2r = x2s.rearrange("(j p) t -> p j t", p=128)
        l3r = l3s.rearrange("(j p) t -> p j t", p=128)
        for ts in range(4):
            sl = slice(ts * 512, (ts + 1) * 512)
            for qtr in range(4):
                if ts >= 1 and qtr == 0:
                    P.wait(E, "sv", f"x2.{ts-1}.3")
                if qtr >= 1:
                    P.wait(E, "sv", f"x2.{ts}.{qtr-1}")
                P.op(E, "dma_start", xo_h[:],
                     xr[:, qtr * 4:(qtr + 1) * 4, sl],
                     inc="sxo", key=f"xo.{ts}.{qtr}")
            P.wait(E, "sv", f"x2.{ts}.3")
            P.op(E, "dma_start", x2r[:, :, sl], x2b[:], inc="sx2w",
                 key=f"x2w.{ts}")
            P.wait(E, "sv", f"l3n.{ts}")
            P.op(E, "dma_start", l3r[:, :, sl], l3t[:], inc="sl3w",
                 key=f"l3sp.{ts}")

    def wait_gate_done(E, grp):
        P.wait(E, "st", f"u.{grp*128+127}")
        P.wait(E, "sv", f"hh.{grp*128+127}")

    def wait_down_done(E, grp):
        P.wait(E, "sv", f"x3j.{grp}.1.1.7")
        P.wait(E, "st", f"d.{grp*32+31}")
        P.wait(E, "sod0", f"od.{grp}.1.1.6")
        P.wait(E, "sod1", f"od.{grp}.1.1.7")

    def odkey(seq):
        g, r = divmod(seq, 32)
        t, r2 = divmod(r, 16)
        hf, j8 = divmod(r2, 8)
        return f"od.{g}.{t}.{hf}.{j8}"

    def p4_gpsimd(E):
        barrier("b4", "gpsimd", E)
        pass
        for grp in range(2):
            for ic in range(64):
                gg = grp * 64 + ic
                if grp >= 1 and ic < 2:
                    wait_down_done(E, grp - 1)
                if gg >= 2:
                    P.wait(E, "st", f"u.{(gg-2)*2+1}")
                P.op(E, "dma_start", wg_sb[ic % 2][:],
                     wgt[ic], inc=f"ggu{gg % 2}")
                P.op(E, "dma_start", wu_sb[ic % 2][:],
                     wut[ic],
                     inc=f"ggu{gg % 2}", key=f"wgu.{gg}")
            hi = 0
            for ts in range(2):
                for oc in range(16):
                    for half in range(2):
                        gg2 = grp * 64 + hi
                        if hi < 2:
                            wait_gate_done(E, grp)
                        if gg2 >= 2:
                            pv = gg2 - 2
                            phi = pv % 64
                            pgrp = pv // 64
                            pts, prr = divmod(phi, 32)
                            poc, phalf = divmod(prr, 2)
                            pkey = (f"d.{pgrp*32+pts*16+poc}" if phalf == 1
                                    else f"dmmA.{pv}")
                            P.wait(E, "st", pkey)
                        P.op(E, "dma_start", wd_sb[hi % 2][:],
                             wdt[oc, :, half * 32:(half + 1) * 32, :],
                             inc=f"gwd{gg2 % 2}", key=f"wd.{gg2}")
                        hi += 1

    def p4_sync(E):
        barrier("b4", "sync", E)
        P.wait(E, "sx2w", "x2w.3")
        P.wait(E, "sl3w", "l3sp.3")
        l3r = l3s.rearrange("(j p) t -> p j t", p=128)
        x2r = x2s.rearrange("(j p) t -> p j t", p=128)
        orr = out.rearrange("(j p) t -> p j t", p=128)
        for grp in range(2):
            gsl = slice(grp * 1024, (grp + 1) * 1024)
            if grp >= 1:
                P.wait(E, "st", f"u.{(grp-1)*128+127}")
                wait_down_done(E, grp - 1)
            P.op(E, "dma_start", l3_sb[:], l3r[:, :, gsl], inc="sl3l",
                 key=f"l3l.{grp}")
            for ts in range(2):
                t0 = grp * 1024 + ts * 512
                for half in range(2):
                    seq = (grp * 2 + ts) * 2 + half
                    if ts == 0 and half == 0:
                        wait_gate_done(E, grp)
                    if seq >= 1:
                        g_, r_ = divmod(seq - 1, 4)
                        t_, hf_ = divmod(r_, 2)
                        P.wait(E, "sv", f"x3j.{g_}.{t_}.{hf_}.7")
                    P.op(E, "dma_start", x2h[:],
                         x2r[:, half * 8:(half + 1) * 8, t0:t0 + 512],
                         inc="sx2l", key=f"x2l.{grp}.{ts}.{half}")
                    for j8 in range(8):
                        P.wait(E, "sv", f"x3j.{grp}.{ts}.{half}.{j8}")
                        P.op(E, "dma_start",
                             orr[:, half * 8 + j8, t0:t0 + 512],
                             x3b[j8 % 2][:], inc=f"sod{j8 % 2}",
                             key=f"od.{grp}.{ts}.{half}.{j8}")
        if KDBG:
            P.op(E, "dma_start", dbg, x2s, inc="sx2w")
        if not P.emit:
            P.ms["end.sod0"] = P.cnt["sod0"]
            P.ms["end.sod1"] = P.cnt["sod1"]
        if P.emit:
            E.wait_ge(P.sems["sod0"], P.ms["end.sod0"])
            E.wait_ge(P.sems["sod1"], P.ms["end.sod1"])

    def p4_tensor(E):
        barrier("b4", "tensor", E)
        for grp in range(2):
            P.wait(E, "sl3l", f"l3l.{grp}")
            gi = 0
            for ic in range(64):
                gg = grp * 64 + ic
                P.wait(E, f"ggu{gg % 2}", f"wgu.{gg}")
                for ts in range(2):
                    g2 = grp * 128 + gi
                    if g2 >= 1:
                        P.wait(E, "sc", f"sl.{g2-1}")
                    for j in range(16):
                        P.op(E, "matmul", pA[:], wg_sb[ic % 2][:, j, :],
                             l3_sb[:, j, ts * 512:(ts + 1) * 512],
                             start=(j == 0), stop=(j == 15),
                             **({"inc": "st", "key": f"g.{g2}"}
                                if j == 15 else {}))
                    if g2 >= 1:
                        P.wait(E, "sv", f"he1.{g2-1}")
                    for j in range(16):
                        P.op(E, "matmul", pB[:], wu_sb[ic % 2][:, j, :],
                             l3_sb[:, j, ts * 512:(ts + 1) * 512],
                             start=(j == 0), stop=(j == 15),
                             **({"inc": "st", "key": f"u.{g2}"}
                                if j == 15 else {}))
                    gi += 1
            for ts in range(2):
                P.wait(E, "sv", f"hh.{grp*128+126+ts}")
                if grp * 2 + ts > 0:
                    pg, pt = divmod(grp * 2 + ts - 1, 2)
                    P.wait(E, "sc", f"rsbe.{pg}.{pt}")
                for m in range(64):
                    P.op(E, "matmul", pz[:], ones128[:],
                         hact[:, m, ts * 512:(ts + 1) * 512],
                         start=(m == 0), stop=(m == 63),
                         **({"inc": "st", "key": f"rs.{grp}.{ts}"}
                            if m == 63 else {}))
            hi = 0
            for ts in range(2):
                tag = f"ln4.{grp}.{ts}"
                prev = (f"ln4.{grp}.{ts-1}" if ts else
                        (f"ln4.{grp-1}.1" if grp else None))

                def lnmm(j):
                    if j == 0 and prev is not None:
                        P.wait(E, "sc", f"mq.{prev}")
                    P.wait(E, "sv", f"fsq.{grp}.{ts}.{j}")
                    P.op(E, "matmul", ps1[:], ones128[:], ffnb[:, j, :],
                         start=(j == 0), stop=(j == 15))
                    P.op(E, "matmul", ps2[:], ones128[:], sqtmp[j % 2][:],
                         start=(j == 0), stop=(j == 15),
                         **({"inc": "st", "key": f"st.{tag}"} if j == 15
                            else {"inc": "st", "key": f"lnmm.{grp}.{ts}.{j}"}))

                for oc in range(16):
                    dd = grp * 32 + ts * 16 + oc
                    for half in range(2):
                        gg2 = grp * 64 + hi
                        P.wait(E, f"gwd{gg2 % 2}", f"wd.{gg2}")
                        if half == 0 and dd >= 2:
                            P.wait(E, "sv", f"de.{dd-2}")
                        mk_ = ({"inc": "st", "key": f"d.{dd}"} if half == 1
                               else {"inc": "st", "key": f"dmmA.{gg2}"})
                        for m in range(32):
                            P.op(E, "matmul", pQ[dd % 2][:],
                                 wd_sb[hi % 2][:, m, :],
                                 hact[:, half * 32 + m,
                                      ts * 512:(ts + 1) * 512],
                                 start=(half == 0 and m == 0),
                                 stop=(half == 1 and m == 31),
                                 **(mk_ if m == 31 else {}))
                        hi += 1
                    if oc >= 2:
                        lnmm(oc - 2)
                lnmm(14)
                lnmm(15)


    def p4_scalar(E):
        barrier("b4", "scalar", E)
        for grp in range(2):
            gi = 0
            for ic in range(64):
                for ts in range(2):
                    g2 = grp * 128 + gi
                    P.wait(E, "st", f"g.{g2}")
                    if g2 >= 1:
                        P.wait(E, "sv", f"hh.{g2-1}")
                    if g2 >= 1:
                        P.wait(E, "sv", f"he1.{g2-1}")
                    P.op(E, "activation", silu_sb[:], pA[:], AF.Silu,
                         scale=c_gs[:, ic:ic + 1], inc="sc", key=f"sl.{g2}")
                    gi += 1
            for ts in range(2):
                P.wait(E, "st", f"rs.{grp}.{ts}")
                if grp * 2 + ts >= 2:
                    pg, pt = divmod(grp * 2 + ts - 2, 2)
                    P.wait(E, "sv", f"de.{pg*32+pt*16+15}")
                P.op(E, "activation", rsb_sb[ts][:], pz[:],
                     AF.Identity, scale=-128.0, inc="sc",
                     key=f"rsbe.{grp}.{ts}")
            for ts in range(2):
                tag = f"ln4.{grp}.{ts}"
                prev = (f"ln4.{grp}.{ts-1}" if ts else
                        (f"ln4.{grp-1}.1" if grp else None))
                pdone = (f"x3j.{grp}.0.1.7" if ts else
                         (f"x3j.{grp-1}.1.1.7" if grp else None))
                ln_stats_S(E, tag, prev, prev_done=pdone)

    def p4_vector(E):
        barrier("b4", "vector", E)
        for grp in range(2):
            gi = 0
            for ic in range(64):
                for ts in range(2):
                    g2 = grp * 128 + gi
                    P.wait(E, "sc", f"sl.{g2}")
                    P.wait(E, "st", f"u.{g2}")
                    P.op(E, "tensor_tensor", tnrm[:], silu_sb[:], pB[:],
                         ALU.mult, inc="sv", key=f"he1.{g2}")
                    P.op(E, "tensor_scalar",
                         hact[:, ic, ts * 512:(ts + 1) * 512],
                         tnrm[:], c_us[:, ic:ic + 1], None, ALU.mult,
                         inc="sv", key=f"hh.{g2}")
                    gi += 1
            for ts in range(2):
                tag = f"ln4.{grp}.{ts}"
                P.wait(E, "sc", f"rsbe.{grp}.{ts}")
                for oc in range(16):
                    dd = grp * 32 + ts * 16 + oc
                    P.wait(E, "st", f"d.{dd}")
                    P.op(E, "scalar_tensor_tensor", t_ao[:], rsb_sb[ts][:],
                         1.0, pQ[dd % 2][:], ALU.mult, ALU.add)
                    P.op(E, "tensor_scalar", ffnb[:, oc, :], t_ao[:],
                         c_ds[:, oc:oc + 1], None, ALU.mult,
                         inc="sv", key=f"de.{dd}")
                    if oc >= 2:
                        P.wait(E, "st", f"lnmm.{grp}.{ts}.{oc-2}")
                    P.op(E, "tensor_tensor", sqtmp[oc % 2][:], ffnb[:, oc, :],
                         ffnb[:, oc, :], ALU.mult, inc="sv",
                         key=f"fsq.{grp}.{ts}.{oc}")
                ln_stats_V(E, tag)
                for half in range(2):
                    P.wait(E, "sx2l", f"x2l.{grp}.{ts}.{half}")
                    for j8 in range(8):
                        j = half * 8 + j8
                        seq = ((grp * 2 + ts) * 2 + half) * 8 + j8
                        if seq >= 2:
                            P.wait(E, f"sod{j8 % 2}", odkey(seq - 2))
                        P.op(E, "tensor_tensor", tnrm[:], ffnb[:, j, :],
                             rb_bf[:], ALU.mult)
                        P.op(E, "tensor_tensor", tnrm[:], tnrm[:], mb_bf[:],
                             ALU.subtract)
                        P.op(E, "tensor_tensor", x3b[j8 % 2][:],
                             x2h[:, j8, :], tnrm[:], ALU.add,
                             inc="sv", key=f"x3j.{grp}.{ts}.{half}.{j8}")

    def p1d_tensor_wrap(E):
        p1ac_tensor(E)
        p1d_tensor(E)

    phase_tbl = {
        "sync": [p1ac_sync, p1d_sync, p2_sync, p3_sync, p4_sync],
        "gpsimd": [p1ac_gpsimd, p1d_gpsimd,
                   lambda E: barrier("b2", "gpsimd", E),
                   p3_gpsimd, p4_gpsimd],
        "tensor": [p1d_tensor_wrap, p2_tensor, p3_tensor, p4_tensor],
        "vector": [p1ac_vector, p1d_vector, p2_vector, p3_vector, p4_vector],
        "scalar": [p1ac_scalar, p1d_scalar, p2_scalar, p3_scalar, p4_scalar],
    }
    import os
    _L = int(os.environ.get("KPH", "4"))
    if _L == 0:
        phase_tbl = {"sync": [p1ac_sync], "gpsimd": [p1ac_gpsimd],
                     "tensor": [p1ac_tensor], "vector": [p1ac_vector],
                     "scalar": [p1ac_scalar]}
        if _os.environ.get("KB1", "0") == "1":
            for e in ENGS:
                phase_tbl[e] = phase_tbl[e] + [
                    (lambda ee: lambda E: barrier("b1", ee, E))(e)]
    elif _L < 4:
        cut = {"sync": 1 + _L, "gpsimd": 1 + _L, "tensor": _L,
               "vector": 1 + _L, "scalar": 1 + _L}
        phase_tbl = {k: v[:cut[k]] for k, v in phase_tbl.items()}

    sem_names = (["st", "sv", "sc", "sy", "sgb"] + SYNC_SEMS + GP_SEMS)
    sem_guards = []
    for sn in sem_names:
        g = nc.semaphore("sem_" + sn)
        P.sems[sn] = g.__enter__()
        sem_guards.append(g)
    with nc.Block() as block:
        P.start(False)
        for eng in ENGS:
            for fn in phase_tbl[eng]:
                fn(None)
        totals = dict(P.cnt)
        P.start(True)

        @block.sync
        def _(sync):
            for fn in phase_tbl["sync"]:
                fn(sync)

        @block.gpsimd
        def _(g):
            for fn in phase_tbl["gpsimd"]:
                fn(g)

        @block.tensor
        def _(t):
            for fn in phase_tbl["tensor"]:
                fn(t)

        @block.vector
        def _(v):
            for fn in phase_tbl["vector"]:
                fn(v)

        @block.scalar
        def _(s):
            for fn in phase_tbl["scalar"]:
                fn(s)

        for k in totals:
            assert P.cnt[k] == totals[k], (k, P.cnt[k], totals[k])
    return nc


_NC_CACHE = []


def kernel(**inputs):
    x = np.ascontiguousarray(np.asarray(inputs["L0_attn_x"], np.float32))
    Wqkv = np.ascontiguousarray(np.asarray(inputs["L0_attn_Wqkv"], np.float32))
    bqkv = np.asarray(inputs["L0_attn_bqkv"], np.float32)
    Wo = np.ascontiguousarray(
        np.asarray(inputs["L0_attn_Wo"], np.float32)
        .reshape(16, 128, 16, 128).transpose(2, 1, 0, 3))
    bo = np.asarray(inputs["L0_attn_bo"], np.float32)
    gw = np.asarray(inputs["L0_ffn_gate_w"], np.int32)
    gsc = np.asarray(inputs["L0_ffn_gate_scale"], np.float32)
    uw = np.asarray(inputs["L0_ffn_up_w"], np.int32)
    usc = np.asarray(inputs["L0_ffn_up_scale"], np.float32)
    dw = np.asarray(inputs["L0_ffn_down_w"], np.int32)
    dsc = np.asarray(inputs["L0_ffn_down_scale"], np.float32)

    wgt = np.ascontiguousarray(
        gw.T.reshape(16, 128, 64, 128).transpose(2, 1, 0, 3))
    wut = np.ascontiguousarray(
        uw.T.reshape(16, 128, 64, 128).transpose(2, 1, 0, 3))
    wdt = np.ascontiguousarray(
        dw.T.reshape(64, 128, 16, 128).transpose(2, 1, 0, 3))
    bqk = np.ascontiguousarray(bqkv[:2 * D].reshape(32, 128).T)
    bv_ = np.ascontiguousarray(bqkv[2 * D:].reshape(16, 128).T)
    bo_ = np.ascontiguousarray(bo.reshape(16, 128).T)
    gs_ = np.ascontiguousarray(gsc.reshape(64, 128).T)
    us_ = np.ascontiguousarray(usc.reshape(64, 128).T)
    ds_ = np.ascontiguousarray(dsc.reshape(16, 128).T)
    ii = np.arange(128)
    mtri = np.ascontiguousarray(
        np.where(ii[:, None] > ii[None, :], np.float32(MASK),
                 np.float32(0.0)).astype(np.float32))

    if not _NC_CACHE:
        _NC_CACHE.append(build_nc())
    nc = _NC_CACHE[0]
    in_maps = []
    perms = []
    for c in range(8):
        b, hg = divmod(c, 2)
        own = np.concatenate([np.arange(g * 128, (g + 1) * 128)
                              for g in range(hg, 32, 2)])
        par = np.concatenate([np.arange(g * 128, (g + 1) * 128)
                              for g in range(1 - hg, 32, 2)])
        perm = np.concatenate([own, par])
        perms.append(own)
        xt_c = np.ascontiguousarray(x[b].T[:, perm])
        mr = np.zeros((4, 512), np.float32)
        for r in range(4):
            mr[r, : (r + 1 - hg) * 128] = MASK
        mrow_c = np.ascontiguousarray(
            mr.reshape(1, 4 * 512)).astype(ml_dtypes.bfloat16)
        in_maps.append({
            "xt": xt_c, "wqkv": Wqkv, "bqk": bqk, "bv": bv_, "wo": Wo,
            "bo": bo_, "wgt": wgt, "wut": wut, "wdt": wdt, "gs": gs_,
            "us": us_, "ds": ds_, "mtri": mtri, "mrow": mrow_c,
        })

    res = run_bass_kernel_spmd(nc, in_maps, core_ids=list(range(8)))
    global _LAST_RES
    _LAST_RES = res.results
    outp = np.empty_like(x)
    for c in range(8):
        b = c // 2
        outp[b, perms[c], :] = res.results[c]["o"].T
    return outp
